# version 12
# speedup vs baseline: 1.9019x; 1.0322x over previous
import numpy as np
import concourse.bass as bass
import concourse.mybir as mybir
from concourse.bass_utils import run_bass_kernel_spmd

F32 = mybir.dt.float32
U32 = mybir.dt.uint32
BF16 = mybir.dt.bfloat16
I32 = mybir.dt.int32
AF = mybir.ActivationFunctionType
OP = mybir.AluOpType
AX = mybir.AxisListType

T = 2048
D = 1024
H = 16
DH = 64
E = 8
FF = 2048
NC = D // 128
KC = T // 128
NT16 = T // 128
PAIRS = H // 2
QB = T // 512
NFC = FF // 128
CAP = 768
CAPT = CAP // 128
EPS_LN = 1e-5

ENGINES = ("tensor", "vector", "scalar", "sync", "gpsimd")
SEMS = ("dW", "dX", "dI", "dS", "tS", "vS", "sS", "gS")
SEM_OWNER = {"tS": "tensor", "vS": "vector", "sS": "scalar", "gS": "gpsimd"}


class Prog:
    def __init__(self):
        self.cnt = {s: 0 for s in SEMS}
        self.streams = {e: [] for e in ENGINES}

    def emit(self, eng, fn, waits=(), inc=None, selfw=False):
        self.streams[eng].append((tuple(waits), fn, inc, selfw))
        if inc is not None:
            self.cnt[inc[0]] += inc[1]

    def run(self, block, sems):
        own_sem = {v: k for k, v in SEM_OWNER.items()}

        def make_runner2(name):
            def _run(eng):
                satisfied = {s: 0 for s in SEMS}
                own = own_sem.get(name)
                own_emitted = 0
                for waits, fn, inc, selfw in self.streams[name]:
                    if selfw and own is not None and own_emitted > 0:
                        eng.wait_ge(sems[own], own_emitted)
                        satisfied[own] = max(satisfied[own], own_emitted)
                    for s, v in waits:
                        if v > satisfied[s]:
                            eng.wait_ge(sems[s], v)
                            satisfied[s] = v
                    r = fn(eng)
                    if inc is not None:
                        s, n = inc
                        r.then_inc(sems[s], n)
                        if s == own:
                            own_emitted += n
            return _run

        for name in ENGINES:
            getattr(block, name)(make_runner2(name))


def build_nc(stop_after=None):
    nc = bass.Bass("TRN2", target_bir_lowering=False, detect_race_conditions=False)

    x_d = nc.dram_tensor("x", [T, D], F32, kind="ExternalInput").ap()
    wq_d = nc.dram_tensor("wq", [NC, 128, D], F32, kind="ExternalInput").ap()
    wk_d = nc.dram_tensor("wk", [NC, 128, D], F32, kind="ExternalInput").ap()
    wv_d = nc.dram_tensor("wv", [NC, 128, D], F32, kind="ExternalInput").ap()
    wo_d = nc.dram_tensor("wo", [NC, 128, D], F32, kind="ExternalInput").ap()
    bo_d = nc.dram_tensor("bo", [1, D], F32, kind="ExternalInput").ap()
    qsc_d = nc.dram_tensor("qsc", [H, DH], F32, kind="ExternalInput").ap()
    ksc_d = nc.dram_tensor("ksc", [H, DH], F32, kind="ExternalInput").ap()
    rw_d = nc.dram_tensor("rw", [NC, 128, E], F32, kind="ExternalInput").ap()
    w1_d = nc.dram_tensor("w1", [E, 2, NC, 128, 1024], F32, kind="ExternalInput").ap()
    w2_d = nc.dram_tensor("w2", [E, 2, NFC, 128, 512], F32, kind="ExternalInput").ap()
    y_d = nc.dram_tensor("y", [T, D], F32, kind="ExternalOutput").ap()
    acc_d = nc.dram_tensor("acc_scr", [T + 1, D], F32).ap()
    w1bf_d = nc.dram_tensor("w1bf_scr", [E, 128, 2 * NC * 1024], BF16).ap()
    w2bf_d = nc.dram_tensor("w2bf_scr", [E, 128, 2 * NFC * 512], BF16).ap()
    xg_d = nc.dram_tensor("xg_scr", [E * CAP, 1028], BF16).ap()

    base = (nc.sbuf_base + 31) & ~31

    def at(name, shape, dtype, off):
        return nc.alloc_sbuf_tensor_at(name, shape, dtype, offset=off)

    REG_A = base
    REG_B = base + 65536
    REG_C = REG_B + 32768
    REG_D = REG_C + 32768
    REG_E = REG_D + 32768
    SM = REG_E + 32768

    qfm = at("qfm", [128, PAIRS, T], BF16, REG_A)
    kfm = at("kfm", [128, PAIRS, T], BF16, REG_A + 32768)
    out_acc = at("out_acc", [128, NT16, D], F32, REG_A)

    xg_sb = at("xg_sb", [128, CAPT, 1028], BF16, REG_A)
    x1g_fm = at("x1g_fm", [128, NC, CAP], BF16, REG_A + 12352)
    y_sb = at("y_sb", [128, CAPT, D], F32, REG_A + 24640)
    x1st = at("x1st", [128, 2, 1024], BF16, REG_B)
    eConst = at("eConst", [128, NT16, E], F32, REG_B + 4096)
    eCm8 = at("eCm8", [128, NT16, E], F32, REG_B + 4608)
    padA = at("padA", [128, 32, E], F32, REG_B + 5120)
    padB = at("padB", [128, 32, E], F32, REG_B + 6144)
    oh0 = at("oh0", [128, NT16, E], F32, REG_B + 7168)
    oh1 = at("oh1", [128, NT16, E], F32, REG_B + 7680)
    exb = at("exb", [128, NT16, E], F32, REG_B + 8192)
    trstg = at("trstg", [128, 2, 512], F32, REG_A + 49216)
    x0_fm = at("x0_fm", [128, NC, T], BF16, REG_B)
    p_slab = at("p_slab", [128, 2, KC, 512], BF16, REG_B)
    w1ring = at("w1ring", [128, 2, NC, 1024], BF16, REG_B)

    vaug = at("vaug", [128, KC, D], BF16, REG_C)
    rowbuf = at("rowbuf", [128, NT16, 1028], BF16, REG_C)

    wring = at("wring", [128, 2, NC, 1024], BF16, REG_D)
    attn_fm = at("attn_fm", [128, PAIRS, T], BF16, REG_D)
    h_fm = at("h_fm", [128, NFC, CAP], BF16, REG_D + 2048)

    x0stA = at("x0stA", [128, 2, D], F32, REG_E)
    q2bf = at("q2bf", [128, 2, 512], BF16, REG_E + 8192)
    invst = at("invst", [128, 2, 512], F32, REG_E + 10240)
    invraw = at("invraw", [128, 2, 512], F32, REG_E + 14336)
    wot = at("wot", [128, NC, D], BF16, REG_E)
    x0stC = at("x0stC", [128, 2, D], F32, REG_E + 18432)
    BT = REG_E + 26624
    lnmsA = at("lnmsA", [128, NT16], F32, BT)
    lnssA = at("lnssA", [128, NT16], F32, BT + 64)
    muA = at("muA", [128, NT16], F32, BT + 128)
    invA = at("invA", [128, NT16], F32, BT + 192)
    t1A = at("t1A", [128, NT16], F32, BT + 256)
    t2A = at("t2A", [128, NT16], F32, BT + 320)
    lnvA = at("lnvA", [128, NT16], F32, BT + 384)
    m1A = at("m1A", [128, NT16], F32, BT + 448)
    c1A = at("c1A", [128, NT16], F32, BT + 512)
    m2mA = at("m2mA", [128, NT16], F32, BT + 576)
    indA = at("indA", [128, NT16], F32, BT + 640)
    d12A = at("d12A", [128, NT16], F32, BT + 704)
    m2A = at("m2A", [128, NT16], F32, BT + 768)
    e2xA = at("e2xA", [128, NT16], F32, BT + 832)
    rA = at("rA", [128, NT16], F32, BT + 896)
    r2A = at("r2A", [128, NT16], F32, BT + 960)
    lg_all = at("lg_all", [128, NT16, E], F32, BT + 1024)
    gbA = at("gbA", [128, NT16, E], F32, BT + 1536)
    gbB = at("gbB", [128, NT16, E], F32, BT + 2048)
    gbC = at("gbC", [128, NT16, E], F32, BT + 2560)
    bo_row = at("bo_row", [1, D], BF16, BT + 3072)
    w2res = at("w2res", [128, 2, NFC, 512], BF16, REG_E)
    outst = at("outst", [128, 2, D], F32, REG_E)
    eln_in = at("eln_in", [128, 2, D], F32, REG_E + 8192)

    sm = SM
    smalls = {}

    def sb(name, shape, dtype):
        nonlocal sm
        t = at(name, shape, dtype, sm)
        n = 1
        for s in shape[1:]:
            n *= s
        sm += (n * (2 if dtype == BF16 else 4) + 31) & ~31
        smalls[name] = t
        return t

    rec_f = sb("rec_f", [128, 512], F32)
    rec_bf = sb("rec_bf", [128, 512], BF16)
    rb_sb = sb("rb_sb", [128, 512], F32)
    gates = sb("gates", [128, NT16, E], F32)
    ident_f32 = sb("ident_f32", [128, 128], F32)
    ident_bf = sb("ident_bf", [128, 128], BF16)
    iot_a = sb("iot_a", [128, 128], I32)
    iot_b = sb("iot_b", [128, 128], I32)
    indB = sb("indB", [128, 128], BF16)
    ones_bf = sb("ones_bf", [128, 128], BF16)
    sq_scr = sb("sq_scr", [128, D], BF16)
    qsc_sb = sb("qsc_sb", [16, DH], F32)
    ksc_sb = sb("ksc_sb", [16, DH], F32)
    qs_col = sb("qs_col", [128, NC], F32)
    ks_col = sb("ks_col", [128, NC], F32)
    mrow = sb("mrow", [1, 16], F32)
    mb_sb = sb("mb_sb", [128, 16], F32)
    negMp = sb("negMp", [128, PAIRS], F32)
    rwt = sb("rwt", [128, NC, E], BF16)
    lg_sb = sb("lg_sb", [128, E], F32)
    g_eq = sb("g_eq", [128, E], F32)
    g_lm = sb("g_lm", [128, E], F32)
    g_et = sb("g_et", [128, E], F32)
    g_ge2 = sb("g_ge2", [128, E], F32)
    ln_ms = sb("ln_ms", [128, 1], F32)
    ln_ss = sb("ln_ss", [128, 1], F32)
    ln_mu = sb("ln_mu", [128, 1], F32)
    ln_t1 = sb("ln_t1", [128, 1], F32)
    ln_t2 = sb("ln_t2", [128, 1], F32)
    ln_inv = sb("ln_inv", [128, 1], F32)
    g_m1 = sb("g_m1", [128, 1], F32)
    g_c1 = sb("g_c1", [128, 1], F32)
    g_m2m = sb("g_m2m", [128, 1], F32)
    g_ind = sb("g_ind", [128, 1], F32)
    g_d12 = sb("g_d12", [128, 1], F32)
    g_m2 = sb("g_m2", [128, 1], F32)
    g_nm1 = sb("g_nm1", [128, 1], F32)
    g_e2 = sb("g_e2", [128, 1], F32)
    g_e2x = sb("g_e2x", [128, 1], F32)
    g_r2 = sb("g_r2", [128, 1], F32)
    ln_t3 = sb("ln_t3", [128, 1], F32)
    eps_l2 = sb("eps_l2", [128, 1], F32)
    zline = sb("zline", [128, 1028], BF16)
    tid_i = sb("tid_i", [128, NT16], I32)
    a_i = sb("a_i", [128, NT16], I32)
    a_bf = sb("a_bf", [128, NT16], BF16)
    b_bf = sb("b_bf", [128, NT16], BF16)
    dec_a = sb("dec_a", [128, 2, CAPT], F32)
    dec_b = sb("dec_b", [128, 2, CAPT], F32)
    triU = sb("triU", [128, 128], F32)
    ones_f32x = sb("ones_f32x", [128, 128], F32)
    tid1 = sb("tid1", [128, NT16], F32)
    e0f = sb("e0f", [128, NT16], F32)
    e1f = sb("e1f", [128, NT16], F32)
    g1A = sb("g1A", [128, NT16], F32)
    s0A = sb("s0A", [128, NT16], F32)
    slotf = sb("slotf", [128, NT16], F32)
    slotu = sb("slotu", [128, 2, NT16], U32)
    tot_sb = sb("tot_sb", [128, E], F32)
    pp_sb = sb("pp_sb", [128, E], F32)
    g0b_sb = sb("g0b_sb", [128, E], F32)
    base_sb = sb("base_sb", [128, E], F32)
    dec_t = sb("dec_t", [128, 2, CAPT], F32)
    dec_g = sb("dec_g", [128, 2, CAPT], F32)
    dec_u = sb("dec_u", [128, 2, CAPT], U32)
    g_r = sb("g_r", [128, 1], F32)
    qmax = sb("qmax", [16, 1], F32)
    kmax = sb("kmax", [16, 1], F32)
    m16 = sb("m16", [16, 1], F32)
    assert sm <= nc.sbuf_top, (sm, nc.sbuf_top)

    prog = Prog()
    c = prog.cnt

    milestones = {}

    def build_program(ps, stop_after=stop_after):
        def PSb(bank, lo=0, hi=512):
            return ps[:, bank * 512 + lo: bank * 512 + hi]

        def dump_stop(src_handle, nelem, waits):
            flat = src_handle.rearrange("p ... -> p (...)") if False else None
            out_ap = bass.AP(y_d.tensor, 0, [[16384, 128], [1, nelem]])
            in_ap = bass.AP(src_handle, 0, [[nelem, 128], [1, nelem]])
            prog.emit("gpsimd", lambda g: g.dma_start(out=out_ap, in_=in_ap,
                      max_dma_last_dim=32768), waits=waits, inc=("dW", 16))
            prog.emit("sync", lambda s: s.nop(), waits=[("dW", c["dW"])])

        prog.emit("gpsimd", lambda g: g.dma_start(out=qsc_sb[:, :], in_=qsc_d[:, :]), inc=("dW", 16))
        dw_qsc = c["dW"]
        prog.emit("gpsimd", lambda g: g.dma_start(out=ksc_sb[:, :], in_=ksc_d[:, :]), inc=("dW", 16))
        dw_ksc = c["dW"]
        prog.emit("gpsimd", lambda g: g.dma_start(
            out=qs_col[:, :], in_=bass.AP(qsc_d.tensor, 0, [[1, 128], [128, NC]]),
            allow_slow_non_contiguous=True), inc=("dW", 16))
        dw_qcol = c["dW"]
        prog.emit("gpsimd", lambda g: g.dma_start(
            out=ks_col[:, :], in_=bass.AP(ksc_d.tensor, 0, [[1, 128], [128, NC]]),
            allow_slow_non_contiguous=True), inc=("dW", 16))
        dw_kcol = c["dW"]
        prog.emit("gpsimd", lambda g: g.dma_start(out=rwt[:, :, :], in_=rw_d.transpose([1, 0, 2])), inc=("dW", 16))
        dw_rwt = c["dW"]
        prog.emit("gpsimd", lambda g: g.dma_start(out=bo_row[:, :], in_=bo_d[:, :]), inc=("dW", 16))
        dw_bo = c["dW"]
        prog.emit("gpsimd", lambda g: g.dma_start(out=wring[:, 0, :, :], in_=wq_d.transpose([1, 0, 2])), inc=("dW", 16))
        dw_wq = c["dW"]
        prog.emit("gpsimd", lambda g: g.dma_start(out=wring[:, 1, :, :], in_=wk_d.transpose([1, 0, 2])), inc=("dW", 16))
        dw_wk = c["dW"]

        prog.emit("gpsimd", lambda g: g.iota(iot_a[:, :], [[-1, 128]], base=0, channel_multiplier=1),
                  inc=("gS", 1))
        g_i1 = c["gS"]
        prog.emit("vector", lambda v: v.tensor_scalar(out=ident_f32[:, :], in0=iot_a[:, :],
                                                      scalar1=0, scalar2=None, op0=OP.is_equal),
                  waits=[("gS", g_i1)], inc=("vS", 1))
        prog.emit("vector", lambda v: v.tensor_scalar(out=ident_bf[:, :], in0=iot_a[:, :],
                                                      scalar1=0, scalar2=None, op0=OP.is_equal),
                  inc=("vS", 1))
        prog.emit("vector", lambda v: v.tensor_scalar(out=triU[:, :], in0=iot_a[:, :],
                                                      scalar1=0, scalar2=None, op0=OP.is_lt),
                  inc=("vS", 1))
        v_id = c["vS"]
        prog.emit("gpsimd", lambda g: g.iota(iot_a[:, :], [[0, 128]], base=0, channel_multiplier=1),
                  waits=[("vS", v_id)], inc=("gS", 1))
        prog.emit("gpsimd", lambda g: g.iota(iot_b[:, :], [[1, 128]], base=0, channel_multiplier=0),
                  inc=("gS", 1))
        g_i2 = c["gS"]
        prog.emit("vector", lambda v: v.tensor_scalar(out=iot_a[:, :], in0=iot_a[:, :], scalar1=6,
                                                      scalar2=None, op0=OP.arith_shift_right),
                  waits=[("gS", g_i2)], inc=("vS", 1))
        prog.emit("vector", lambda v: v.tensor_scalar(out=iot_b[:, :], in0=iot_b[:, :], scalar1=6,
                                                      scalar2=None, op0=OP.arith_shift_right), inc=("vS", 1))
        prog.emit("vector", lambda v: v.tensor_tensor(out=indB[:, :], in0=iot_a[:, :], in1=iot_b[:, :],
                                                      op=OP.is_equal), selfw=True, inc=("vS", 1))
        prog.emit("vector", lambda v: v.memset(ones_bf[:, :], 1.0), inc=("vS", 1))
        prog.emit("vector", lambda v: v.memset(eps_l2[:, :], 1e-18), inc=("vS", 1))
        prog.emit("vector", lambda v: v.memset(ones_f32x[:, :], 1.0), inc=("vS", 1))
        prog.emit("vector", lambda v: v.memset(zline[:, :], 0.0), inc=("vS", 1))
        v_zline = c["vS"]
        prog.emit("gpsimd", lambda g: g.iota(tid1[:, :], [[128, NT16]], base=1, channel_multiplier=1,
                                             allow_small_or_imprecise_dtypes=True), inc=("gS", 1))
        prog.emit("gpsimd", lambda g: g.iota(tid_i[:, :], [[128, NT16]], base=1, channel_multiplier=1),
                  inc=("gS", 1))
        g_tid = c["gS"]
        prog.emit("vector", lambda v: v.tensor_scalar(out=a_i[:, :], in0=tid_i[:, :], scalar1=6,
                                                      scalar2=None, op0=OP.arith_shift_right),
                  waits=[("gS", g_tid)], inc=("vS", 1))
        prog.emit("vector", lambda v: v.tensor_copy(a_bf[:, :], a_i[:, :]), selfw=True, inc=("vS", 1))
        prog.emit("vector", lambda v: v.tensor_scalar(out=a_i[:, :], in0=a_i[:, :], scalar1=-64,
                                                      scalar2=None, op0=OP.mult), selfw=True, inc=("vS", 1))
        prog.emit("vector", lambda v: v.tensor_tensor(out=a_i[:, :], in0=tid_i[:, :], in1=a_i[:, :],
                                                      op=OP.add), selfw=True, inc=("vS", 1))
        prog.emit("vector", lambda v: v.tensor_copy(b_bf[:, :], a_i[:, :]), selfw=True, inc=("vS", 1))
        v_idsplit = c["vS"]
        for blk in range(E * CAPT):
            prog.emit("sync", lambda s, blk=blk: s.dma_start(
                out=xg_d[blk * 128:(blk + 1) * 128, :], in_=zline[:, :]),
                waits=[("vS", v_zline)] if blk == 0 else [], inc=("dX", 16))
        prog.emit("gpsimd", lambda g: g.dma_start(
            out=acc_d[T:T + 1, :], in_=bass.AP(zline, 0, [[1026, 1], [1, D]])),
            waits=[("vS", v_zline)], inc=("dW", 16))
        dx_zero_done = c["dX"]
        dw_ztrash = c["dW"]
        prog.emit("vector", lambda v: v.reduce_max(qmax[:, :], qsc_sb[:, :], axis=AX.X,
                                                   apply_absolute_value=True),
                  waits=[("dW", dw_qsc)], inc=("vS", 1))
        prog.emit("vector", lambda v: v.reduce_max(kmax[:, :], ksc_sb[:, :], axis=AX.X,
                                                   apply_absolute_value=True),
                  waits=[("dW", dw_ksc)], inc=("vS", 1))
        prog.emit("vector", lambda v: v.tensor_tensor(out=m16[:, :], in0=qmax[:, :], in1=kmax[:, :],
                                                      op=OP.mult), selfw=True, inc=("vS", 1))
        v_m16 = c["vS"]
        prog.emit("gpsimd", lambda g: g.dma_start(
            out=bass.AP(mrow, 0, [[16, 1], [1, 16]]),
            in_=bass.AP(m16, 0, [[1, 16], [1, 1]]),
            allow_slow_non_contiguous=True),
            waits=[("vS", v_m16)], inc=("dW", 16))
        dw_mrow = c["dW"]
        prog.emit("vector", lambda v: v.tensor_copy(rec_bf[0:1, 0:16], mrow[0:1, :]),
                  waits=[("dW", dw_mrow)], inc=("vS", 1))
        v_mbf = c["vS"]
        prog.emit("tensor", lambda t: t.matmul(ps[:, 3616:3632][0:64, :16], ones_bf[0:1, 0:64],
                                               rec_bf[0:1, 0:16], start=True, stop=True),
                  waits=[("vS", v_mbf)], inc=("tS", 1))
        prog.emit("tensor", lambda t: t.matmul(ps[:, 3616:3632][64:128, :16], ones_bf[0:1, 0:64],
                                               rec_bf[0:1, 0:16], start=True, stop=True,
                                               tile_position=(0, 64)),
                  inc=("tS", 1))
        t_mb = c["tS"]
        prog.emit("vector", lambda v: v.tensor_copy(mb_sb[:, :], ps[:, 3616:3632]),
                  waits=[("tS", t_mb)], inc=("vS", 1))
        prog.emit("vector", lambda v: v.tensor_tensor(out=negMp[:, :], in0=mb_sb[:, 0:16:2],
                                                      in1=mb_sb[:, 1:16:2], op=OP.max), selfw=True, inc=("vS", 1))
        prog.emit("vector", lambda v: v.tensor_scalar(out=negMp[:, :], in0=negMp[:, :],
                                                      scalar1=-0.125, scalar2=None, op0=OP.mult),
                  selfw=True, inc=("vS", 1))
        v_setup = c["vS"]

        x0fm_done = [0] * NT16
        tr_evac = [0, 0]
        dx_tile = [0] * NT16

        def emit_x0_load(t16):
            par = t16 % 2
            waits = []
            if t16 >= 2:
                waits.append(("vS", x0fm_done[t16 - 2]))
            prog.emit("sync", lambda s, t16=t16, par=par: s.dma_start(
                out=x0stA[:, par, :], in_=x_d[t16 * 128:(t16 + 1) * 128, :]),
                waits=waits, inc=("dX", 16))
            dx_tile[t16] = c["dX"]

        emit_x0_load(0)
        emit_x0_load(1)
        for t16 in range(NT16):
            par = t16 % 2
            t_tr4 = [0, 0]
            for half in range(2):
                bank = 4 + half
                waits = [("dX", dx_tile[t16]), ("vS", v_id)]
                if t16 >= 1:
                    waits.append(("vS", tr_evac[half]))
                for j in range(4):
                    cc = half * 4 + j
                    prog.emit("tensor", lambda t, par=par, cc=cc, bank=bank, j=j: t.transpose(
                        PSb(bank, j * 128, (j + 1) * 128),
                        x0stA[:, par, cc * 128:(cc + 1) * 128], ident_f32[:, :]),
                        waits=waits if j == 0 else [], inc=("tS", 1) if j == 3 else None)
                t_tr4[half] = c["tS"]
            for half in range(2):
                bank = 4 + half
                prog.emit("vector", lambda v, t16=t16, half=half, bank=bank: v.tensor_copy(
                    x0_fm[:, half * 4:(half + 1) * 4, t16 * 128:(t16 + 1) * 128], PSb(bank)),
                    waits=[("tS", t_tr4[half])], inc=("vS", 1))
                tr_evac[half] = c["vS"]
            x0fm_done[t16] = c["vS"]
            if t16 + 2 < NT16:
                emit_x0_load(t16 + 2)
        v_x0fm = c["vS"]
        if stop_after == "x0fm":
            return dump_stop(x0_fm, NC * T, [("vS", v_x0fm)])

        NU = 2 * NC * 2
        grp_done = [0] * NU
        sq_done = [[0, 0] for _ in range(NU)]
        ssmm_done = [[0, 0] for _ in range(NU)]
        exp_done = [[0, 0] for _ in range(NU)]
        stt_done = [[0, 0] for _ in range(NU)]

        def unit_info(u):
            proj = u // 16
            co = (u % 16) // 2
            half = u % 2
            wslot = 0 if proj == 0 else 1
            dst = qfm if proj == 0 else kfm
            scol = qs_col if proj == 0 else ks_col
            dw_w = dw_wq if proj == 0 else dw_wk
            dw_s = dw_qcol if proj == 0 else dw_kcol
            return proj, co, half, wslot, dst, scol, dw_w, dw_s

        def emit_ssmm(u):
            for loc in range(2):
                waits = [("sS", sq_done[u][loc])]
                if u == 0:
                    waits.append(("vS", v_setup))
                if u >= 1:
                    waits.append(("sS", exp_done[u - 1][loc]))
                prog.emit("tensor", lambda t, u=u, loc=loc: t.matmul(
                    PSb(6 + loc), indB[:, :], q2bf[:, loc, :], start=True, stop=True),
                    waits=waits, inc=("tS", 1))
                ssmm_done[u][loc] = c["tS"]

        def emit_grp(u):
            _, co, half, wslot, dst, scol, dw_w, dw_s = unit_info(u)
            b0 = (u % 3) * 2
            for dc in range(NC):
                waits = []
                if dc == 0:
                    waits = [("dW", dw_w), ("vS", v_x0fm)]
                    if u >= 3:
                        waits.append(("vS", stt_done[u - 3][1]))
                for loc in range(2):
                    t5 = half * 2 + loc
                    prog.emit("tensor", lambda t, co=co, dc=dc, t5=t5, wslot=wslot, b=b0 + loc: t.matmul(
                        PSb(b), wring[:, wslot, dc, co * 128:(co + 1) * 128],
                        x0_fm[:, dc, t5 * 512:(t5 + 1) * 512],
                        start=(dc == 0), stop=(dc == NC - 1)),
                        waits=waits if loc == 0 else [],
                        inc=("tS", 1) if (dc == NC - 1 and loc == 1) else None)
            grp_done[u] = c["tS"]

        def emit_sq(u):
            b0 = (u % 3) * 2
            for loc in range(2):
                waits = [("tS", grp_done[u])]
                if u >= 1:
                    waits.append(("tS", ssmm_done[u - 1][loc]))
                prog.emit("scalar", lambda s, loc=loc, b=b0 + loc: s.activation(
                    q2bf[:, loc, :], PSb(b), AF.Square),
                    waits=waits, inc=("sS", 1))
                sq_done[u][loc] = c["sS"]

        def emit_lnexp(u):
            for loc in range(2):
                waits = [("tS", ssmm_done[u][loc])]
                prog.emit("scalar", lambda s, loc=loc: s.activation(
                    invraw[:, loc, :], PSb(6 + loc), AF.Ln, bias=eps_l2[:, 0:1]),
                    waits=waits, inc=("sS", 1))
                w2_ = [("vS", stt_done[u - 1][loc])] if u >= 1 else []
                prog.emit("scalar", lambda s, loc=loc: s.activation(
                    invst[:, loc, :], invraw[:, loc, :], AF.Exp, scale=-0.5),
                    waits=w2_, selfw=True, inc=("sS", 1))
                exp_done[u][loc] = c["sS"]

        def emit_stt(u):
            _, co, half, wslot, dst, scol, dw_w, dw_s = unit_info(u)
            b0 = (u % 3) * 2
            for loc in range(2):
                t5 = half * 2 + loc
                prog.emit("vector", lambda v, co=co, t5=t5, loc=loc, b=b0 + loc, dst=dst, scol=scol: v.scalar_tensor_tensor(
                    out=dst[:, co, t5 * 512:(t5 + 1) * 512], in0=PSb(b),
                    scalar=scol[:, co:co + 1], in1=invst[:, loc, :],
                    op0=OP.mult, op1=OP.mult),
                    waits=[("dW", dw_s), ("sS", exp_done[u][loc])], inc=("vS", 1))
                stt_done[u][loc] = c["vS"]

        for u in range(NU + 2):
            if u >= 2:
                emit_ssmm(u - 2)
            if u < NU:
                emit_grp(u)
            if 1 <= u <= NU:
                emit_sq(u - 1)
            if u >= 2:
                emit_lnexp(u - 2)
                emit_stt(u - 2)
        v_q = stt_done[15][1]
        v_k = stt_done[NU - 1][1]
        if stop_after == "qfm":
            return dump_stop(qfm, PAIRS * T, [("vS", v_q)])
        prog.emit("gpsimd", lambda g: g.dma_start(out=wring[:, 0, :, :], in_=wv_d.transpose([1, 0, 2])),
                  waits=[("vS", v_q)], inc=("dW", 16))
        dw_wv = c["dW"]
        if stop_after == "kfm":
            return dump_stop(kfm, PAIRS * T, [("vS", v_k)])

        psv_evac = [0, 0]
        for t16 in range(NT16):
            pp = t16 % 2
            banks = (2 * pp, 2 * pp + 1)
            for dc in range(NC):
                waits = []
                if dc == 0:
                    waits = [("dW", dw_wv), ("vS", v_k)]
                    if t16 >= 2:
                        waits.append(("vS", psv_evac[pp]))
                for eh in range(2):
                    prog.emit("tensor", lambda t, t16=t16, dc=dc, eh=eh, banks=banks: t.matmul(
                        PSb(banks[eh]), x0_fm[:, dc, t16 * 128:(t16 + 1) * 128],
                        wring[:, 0, dc, eh * 512:(eh + 1) * 512],
                        start=(dc == 0), stop=(dc == NC - 1)),
                        waits=waits if eh == 0 else [],
                        inc=("tS", 1) if (dc == NC - 1 and eh == 1) else None)
            t_v = c["tS"]
            for eh in range(2):
                prog.emit("vector", lambda v, t16=t16, eh=eh, banks=banks: v.tensor_copy(
                    vaug[:, t16, eh * 512:(eh + 1) * 512], PSb(banks[eh])),
                    waits=[("tS", t_v)], inc=("vS", 1))
            psv_evac[pp] = c["vS"]
        v_vaug = c["vS"]
        if stop_after == "vaug":
            return dump_stop(vaug, KC * D, [("vS", v_vaug)])

        prog.emit("gpsimd", lambda g: g.dma_start(out=wot[:, :, :], in_=wo_d.transpose([1, 0, 2])),
                  waits=[("vS", v_vaug)], inc=("dW", 16))
        dw_wot = c["dW"]

        dw_cast_w1 = [[0, 0] for _ in range(E)]
        dw_cast_w2 = [0] * E
        for e in range(E):
            for fh in range(2):
                prog.emit("gpsimd", lambda g, e=e, fh=fh: g.dma_start(
                    out=bass.AP(w1bf_d.tensor, e * 128 * 2 * NC * 1024 + fh * NC * 1024,
                                [[2 * NC * 1024, 128], [1, NC * 1024]]),
                    in_=w1_d[e, fh].transpose([1, 0, 2])), inc=("dW", 16))
                dw_cast_w1[e][fh] = c["dW"]
            prog.emit("gpsimd", lambda g, e=e: g.dma_start(
                out=w2bf_d[e],
                in_=bass.AP(w2_d.tensor, e * 2 * NFC * 128 * 512,
                            [[512, 128], [NFC * 128 * 512, 2], [128 * 512, NFC], [1, 512]])),
                inc=("dW", 16))
            dw_cast_w2[e] = c["dW"]

        units = [(pr, qb) for pr in range(PAIRS) for qb in range(QB)]
        NU = len(units)
        prev_mult = [0, 0]
        prev_rec_par = [0, 0]
        av_last_final = 0
        ppb = [(0, 1), (2, 3)]
        qk_done = {}
        exp_done = {}

        def emit_qk(u, kc, extra):
            pr, qb = units[u]
            b0, b1 = ppb[kc % 2]
            prog.emit("tensor", lambda t, pr=pr, qb=qb, kc=kc, b0=b0: t.matmul(
                PSb(b0), kfm[0:64, pr, kc * 128:(kc + 1) * 128],
                qfm[0:64, pr, qb * 512:(qb + 1) * 512], start=True, stop=True),
                waits=extra)
            prog.emit("tensor", lambda t, pr=pr, qb=qb, kc=kc, b1=b1: t.matmul(
                PSb(b1), kfm[64:128, pr, kc * 128:(kc + 1) * 128],
                qfm[64:128, pr, qb * 512:(qb + 1) * 512], start=True, stop=True),
                inc=("tS", 1))
            qk_done[(u, kc)] = c["tS"]

        emit_qk(0, 0, [("vS", v_vaug)])
        emit_qk(0, 1, [])
        for it in range(NU):
            pr, qb = units[it]
            avb = 4 + (it % 2)
            DENB = 6 + (it % 2)
            for kc in range(KC):
                b0 = ppb[kc % 2][0]
                prog.emit("scalar", lambda s, pr=pr, kc=kc, b0=b0: s.activation(
                    p_slab[:, :, kc, :], ps[:, b0 * 512:(b0 + 2) * 512], AF.Exp,
                    bias=negMp[:, pr:pr + 1], scale=0.125),
                    waits=[("tS", qk_done[(it, kc)])], inc=("sS", 1))
                exp_done[(it, kc)] = c["sS"]
                waits = [("sS", exp_done[(it, kc)])]
                if kc == 0 and it >= 2:
                    waits.append(("vS", prev_mult[it % 2]))
                    waits.append(("vS", prev_rec_par[it % 2]))
                prog.emit("tensor", lambda t, pr=pr, kc=kc, avb=avb: t.matmul(
                    PSb(avb)[0:64, :], vaug[:, kc, (2 * pr) * 64:(2 * pr + 1) * 64],
                    p_slab[:, 0, kc, :], start=(kc == 0), stop=(kc == KC - 1)),
                    waits=waits)
                prog.emit("tensor", lambda t, pr=pr, kc=kc, avb=avb: t.matmul(
                    PSb(avb)[64:128, :], vaug[:, kc, (2 * pr + 1) * 64:(2 * pr + 2) * 64],
                    p_slab[:, 1, kc, :], start=(kc == 0), stop=(kc == KC - 1),
                    tile_position=(0, 64)))
                prog.emit("tensor", lambda t, kc=kc, DENB=DENB: t.matmul(
                    PSb(DENB)[0:64, :], ones_bf[:, 0:64], p_slab[:, 0, kc, :],
                    start=(kc == 0), stop=(kc == KC - 1), tile_position=(0, 0)))
                prog.emit("tensor", lambda t, kc=kc, DENB=DENB: t.matmul(
                    PSb(DENB)[64:128, :], ones_bf[:, 0:64], p_slab[:, 1, kc, :],
                    start=(kc == 0), stop=(kc == KC - 1), tile_position=(0, 64)),
                    inc=("tS", 1))
                if kc == KC - 1:
                    av_last = c["tS"]
                if kc + 2 < KC:
                    emit_qk(it, kc + 2, [])
                elif it + 1 < NU:
                    emit_qk(it + 1, kc + 2 - KC, [("sS", exp_done[(it, kc)])])

            prog.emit("vector", lambda v, DENB=DENB: v.reciprocal(rb_sb[:, :], PSb(DENB)),
                      waits=[("tS", av_last)], inc=("vS", 1))
            v_rec = c["vS"]
            prog.emit("vector", lambda v, pr=pr, qb=qb, avb=avb: v.tensor_tensor(
                out=attn_fm[:, pr, qb * 512:(qb + 1) * 512], in0=PSb(avb),
                in1=rb_sb[:, :], op=OP.mult), selfw=True, inc=("vS", 1))
            prev_mult[it % 2] = c["vS"]
            prev_rec_par[it % 2] = v_rec
            av_last_final = av_last
        v_attn = c["vS"]
        t_pslab_free = av_last_final
        if stop_after == "attn":
            return dump_stop(attn_fm, PAIRS * T, [("vS", v_attn)])

        dxc_tile = [0] * NT16
        pre_done = [0, 0]

        def emit_xc_load(t16):
            par = t16 % 2
            waits = [("tS", t_pslab_free)]
            if t16 >= 2:
                waits.append(("vS", pre_done[par]))
            prog.emit("sync", lambda s, t16=t16, par=par: s.dma_start(
                out=x0stC[:, par, :], in_=x_d[t16 * 128:(t16 + 1) * 128, :]),
                waits=waits, inc=("dX", 16))
            dxc_tile[t16] = c["dX"]

        emit_xc_load(0)
        emit_xc_load(1)
        s_stats = [0] * NT16
        for t16 in range(NT16):
            par = t16 % 2
            wob = (0, 1) if par == 0 else (2, 3)
            for dh2 in range(2):
                for cc in range(NC):
                    waits = []
                    if dh2 == 0 and cc == 0:
                        waits = [("dW", dw_wot), ("vS", v_attn)]
                        if t16 >= 2:
                            waits.append(("vS", pre_done[par]))
                    prog.emit("tensor", lambda t, t16=t16, dh2=dh2, cc=cc, wob=wob: t.matmul(
                        PSb(wob[dh2]), attn_fm[:, cc, t16 * 128:(t16 + 1) * 128],
                        wot[:, cc, dh2 * 512:(dh2 + 1) * 512],
                        start=(cc == 0), stop=False),
                        waits=waits)
                prog.emit("tensor", lambda t, dh2=dh2, wob=wob: t.matmul(
                    PSb(wob[dh2]), ones_bf[0:1, 0:128],
                    bo_row[0:1, dh2 * 512:(dh2 + 1) * 512],
                    start=False, stop=True),
                    waits=[("dW", dw_bo)] if dh2 == 0 else [],
                    inc=("tS", 1) if dh2 == 1 else None)
            t_wo = c["tS"]
            prog.emit("vector", lambda v, t16=t16, par=par, wob=wob: v.tensor_tensor(
                out=out_acc[:, t16, :], in0=ps[:, wob[0] * 512:(wob[0] + 2) * 512],
                in1=x0stC[:, par, :], op=OP.add),
                waits=[("tS", t_wo), ("dX", dxc_tile[t16])], inc=("vS", 1))
            v_pre = c["vS"]
            pre_done[par] = v_pre
            prog.emit("scalar", lambda s, t16=t16: s.activation(
                sq_scr[:, :], out_acc[:, t16, :], AF.Copy, accum_out=lnmsA[:, t16:t16 + 1]),
                waits=[("vS", v_pre)], inc=("sS", 1))
            prog.emit("scalar", lambda s, t16=t16: s.activation(
                sq_scr[:, :], out_acc[:, t16, :], AF.Square, accum_out=lnssA[:, t16:t16 + 1]),
                inc=("sS", 1))
            s_stats[t16] = c["sS"]
            if t16 + 2 < NT16:
                emit_xc_load(t16 + 2)
        t_wo15 = c["tS"]

        prog.emit("vector", lambda v: v.tensor_scalar(out=muA[:, :], in0=lnmsA[:, :],
                                                      scalar1=1.0 / D, scalar2=None, op0=OP.mult),
                  waits=[("sS", s_stats[NT16 - 1])], inc=("vS", 1))
        v_mu = c["vS"]
        prog.emit("scalar", lambda s: s.activation(t1A[:, :], muA[:, :], AF.Square),
                  waits=[("vS", v_mu)], selfw=True, inc=("sS", 1))
        s_musq = c["sS"]
        prog.emit("vector", lambda v: v.tensor_scalar(out=t2A[:, :], in0=lnssA[:, :],
                                                      scalar1=1.0 / D, scalar2=EPS_LN,
                                                      op0=OP.mult, op1=OP.add), inc=("vS", 1))
        prog.emit("vector", lambda v: v.tensor_tensor(out=t2A[:, :], in0=t2A[:, :],
                                                      in1=t1A[:, :], op=OP.subtract),
                  waits=[("sS", s_musq)], selfw=True, inc=("vS", 1))
        v_ln = c["vS"]
        prog.emit("scalar", lambda s: s.activation(lnvA[:, :], t2A[:, :], AF.Ln),
                  waits=[("vS", v_ln)], selfw=True, inc=("sS", 1))
        prog.emit("scalar", lambda s: s.activation(invA[:, :], lnvA[:, :], AF.Exp, scale=-0.5),
                  selfw=True, inc=("sS", 1))
        s_inv = c["sS"]

        x1_done = [0] * NT16
        tr_evac = [0, 0]
        t_lg_all = [0] * NT16
        s_row = [0] * NT16
        dx_accst = [0] * NT16
        for t16 in range(NT16):
            prog.emit("vector", lambda v, t16=t16: v.tensor_scalar(
                out=out_acc[:, t16, :], in0=out_acc[:, t16, :], scalar1=muA[:, t16:t16 + 1],
                scalar2=invA[:, t16:t16 + 1], op0=OP.subtract, op1=OP.mult),
                waits=[("sS", s_inv)], inc=("vS", 1))
            x1_done[t16] = c["vS"]
            prog.emit("scalar", lambda s, t16=t16: s.activation(
                rowbuf[:, t16, 0:1024], out_acc[:, t16, :], AF.Copy),
                waits=[("vS", x1_done[t16])], inc=("sS", 1))
            s_row[t16] = c["sS"]
            if t16 == 0:
                prog.emit("vector", lambda v: v.tensor_copy(
                    bass.AP(rowbuf, 1026, [[NT16 * 1028, 128], [1028, NT16], [1, 1]]),
                    a_bf[:, :]), waits=[("vS", v_idsplit)], inc=("vS", 1))
                prog.emit("vector", lambda v: v.tensor_copy(
                    bass.AP(rowbuf, 1027, [[NT16 * 1028, 128], [1028, NT16], [1, 1]]),
                    b_bf[:, :]), inc=("vS", 1))
                prog.emit("vector", lambda v: v.memset(
                    bass.AP(rowbuf, 1025, [[NT16 * 1028, 128], [1028, NT16], [1, 1]]), 0.0),
                    inc=("vS", 1))
            prog.emit("sync", lambda s, t16=t16: s.dma_start(
                out=acc_d[t16 * 128:(t16 + 1) * 128, :], in_=out_acc[:, t16, :]),
                waits=[("vS", x1_done[t16])], inc=("dX", 16))
            dx_accst[t16] = c["dX"]
            par = t16 % 2
            t_trh = [0, 0]
            for half in range(2):
                bank = 4 + half
                waits = [("vS", x1_done[t16])]
                if t16 >= 1:
                    waits.append(("vS", tr_evac[half]))
                for j in range(4):
                    cc = half * 4 + j
                    prog.emit("tensor", lambda t, t16=t16, cc=cc, bank=bank, j=j: t.transpose(
                        PSb(bank, j * 128, (j + 1) * 128),
                        out_acc[:, t16, cc * 128:(cc + 1) * 128], ident_f32[:, :]),
                        waits=waits if j == 0 else [], inc=("tS", 1) if j == 3 else None)
                t_trh[half] = c["tS"]
            for half in range(2):
                bank = 4 + half
                waits = [("tS", t_trh[half])]
                if t16 >= 2:
                    waits.append(("tS", t_lg_all[t16 - 2]))
                prog.emit("vector", lambda v, par=par, half=half, bank=bank: v.tensor_copy(
                    x1st[:, par, half * 512:(half + 1) * 512], PSb(bank)),
                    waits=waits, inc=("vS", 1))
                tr_evac[half] = c["vS"]
            v_stage = c["vS"]
            for cc in range(NC):
                prog.emit("tensor", lambda t, par=par, cc=cc: t.matmul(
                    PSb(6, 0, E), x1st[:, par, cc * 128:(cc + 1) * 128],
                    rwt[:, cc, :], start=(cc == 0), stop=(cc == NC - 1)),
                    waits=[("vS", v_stage), ("dW", dw_rwt)] if cc == 0 else [],
                    inc=("tS", 1) if cc == NC - 1 else None)
            t_lg_all[t16] = c["tS"]
            prog.emit("vector", lambda v, t16=t16: v.tensor_copy(lg_all[:, t16, :], PSb(6, 0, E)),
                      waits=[("tS", t_lg_all[t16])], inc=("vS", 1))
        v_lg = c["vS"]
        dx_acc_init = c["dX"]

        def bc16(th_):
            return bass.AP(th_, 0, [[NT16, 128], [1, NT16], [0, E]])

        prog.emit("vector", lambda v: v.reduce_max(m1A[:, :], lg_all[:, :, :], axis=AX.X),
                  selfw=True, inc=("vS", 1))
        prog.emit("vector", lambda v: v.tensor_tensor(out=gbA[:, :, :], in0=lg_all[:, :, :],
                                                      in1=bc16(m1A), op=OP.is_equal),
                  selfw=True, inc=("vS", 1))
        prog.emit("vector", lambda v: v.reduce_sum(c1A[:, :], gbA[:, :, :], axis=AX.X),
                  selfw=True, inc=("vS", 1))
        prog.emit("vector", lambda v: v.scalar_tensor_tensor(
            out=gbB[:, :, :], in0=gbA[:, :, :], scalar=-1e30, in1=lg_all[:, :, :],
            op0=OP.mult, op1=OP.add), selfw=True, inc=("vS", 1))
        prog.emit("vector", lambda v: v.reduce_max(m2mA[:, :], gbB[:, :, :], axis=AX.X),
                  selfw=True, inc=("vS", 1))
        prog.emit("vector", lambda v: v.tensor_scalar(out=indA[:, :], in0=c1A[:, :],
                                                      scalar1=2.0, scalar2=None, op0=OP.is_ge),
                  selfw=True, inc=("vS", 1))
        prog.emit("vector", lambda v: v.tensor_tensor(out=d12A[:, :], in0=m1A[:, :],
                                                      in1=m2mA[:, :], op=OP.subtract),
                  selfw=True, inc=("vS", 1))
        prog.emit("vector", lambda v: v.tensor_tensor(out=d12A[:, :], in0=indA[:, :],
                                                      in1=d12A[:, :], op=OP.mult),
                  selfw=True, inc=("vS", 1))
        prog.emit("vector", lambda v: v.tensor_tensor(out=m2A[:, :], in0=d12A[:, :],
                                                      in1=m2mA[:, :], op=OP.add),
                  selfw=True, inc=("vS", 1))
        prog.emit("vector", lambda v: v.tensor_tensor(out=gbA[:, :, :], in0=lg_all[:, :, :],
                                                      in1=bc16(m1A), op=OP.subtract),
                  selfw=True, inc=("vS", 1))
        prog.emit("vector", lambda v: v.tensor_tensor(out=e2xA[:, :], in0=m2A[:, :],
                                                      in1=m1A[:, :], op=OP.subtract),
                  selfw=True, inc=("vS", 1))
        v_g = c["vS"]
        prog.emit("scalar", lambda s: s.activation(gbC[:, :, :], gbA[:, :, :], AF.Exp),
                  waits=[("vS", v_g)], selfw=True, inc=("sS", 1))
        prog.emit("scalar", lambda s: s.activation(e2xA[:, :], e2xA[:, :], AF.Exp),
                  inc=("sS", 1))
        s_e2 = c["sS"]
        prog.emit("vector", lambda v: v.tensor_scalar(out=rA[:, :], in0=e2xA[:, :],
                                                      scalar1=1.0, scalar2=None, op0=OP.add),
                  waits=[("sS", s_e2)], inc=("vS", 1))
        prog.emit("vector", lambda v: v.reciprocal(r2A[:, :], rA[:, :]), selfw=True, inc=("vS", 1))
        prog.emit("vector", lambda v: v.tensor_tensor(out=gbB[:, :, :], in0=lg_all[:, :, :],
                                                      in1=bc16(m2A), op=OP.is_ge),
                  selfw=True, inc=("vS", 1))
        prog.emit("vector", lambda v: v.tensor_tensor(out=gbB[:, :, :], in0=gbC[:, :, :],
                                                      in1=gbB[:, :, :], op=OP.mult),
                  selfw=True, inc=("vS", 1))
        prog.emit("vector", lambda v: v.tensor_tensor(out=gates[:, :, :], in0=gbB[:, :, :],
                                                      in1=bc16(r2A), op=OP.mult),
                  selfw=True, inc=("vS", 1))
        v_x1_all = c["vS"]
        v_gates_all = c["vS"]
        if stop_after == "x1":
            return dump_stop(out_acc, NT16 * D, [("vS", v_x1_all)])
        if stop_after == "gates":
            return dump_stop(gates, NT16 * E, [("vS", v_x1_all)])

        def vsw(fn, waits=()):
            prog.emit("vector", fn, waits=waits, selfw=True, inc=("vS", 1))
            return c["vS"]

        prog.emit("gpsimd", lambda g: g.iota(eConst.rearrange("p a b -> p (a b)"),
                                             [[0, NT16], [1, E]], base=0, channel_multiplier=0,
                                             allow_small_or_imprecise_dtypes=True),
                  waits=[("tS", t_pslab_free)], inc=("gS", 1))
        prog.emit("gpsimd", lambda g: g.iota(eCm8.rearrange("p a b -> p (a b)"),
                                             [[0, NT16], [1, E]], base=-8, channel_multiplier=0,
                                             allow_small_or_imprecise_dtypes=True), inc=("gS", 1))
        g_eiota = c["gS"]
        vsw(lambda v: v.tensor_tensor(out=gbA[:, :, :], in0=lg_all[:, :, :],
                                      in1=bc16(m1A), op=OP.is_equal))
        vsw(lambda v: v.tensor_tensor(out=gbB[:, :, :], in0=gbA[:, :, :],
                                      in1=eCm8[:, :, :], op=OP.mult), waits=[("gS", g_eiota)])
        vsw(lambda v: v.tensor_reduce(out=e0f[:, :], in_=gbB[:, :, :], axis=AX.X, op=OP.min))
        vsw(lambda v: v.tensor_scalar(out=e0f[:, :], in0=e0f[:, :], scalar1=8.0,
                                      scalar2=None, op0=OP.add))
        vsw(lambda v: v.tensor_tensor(out=oh0[:, :, :], in0=eConst[:, :, :],
                                      in1=bc16(e0f), op=OP.is_equal))
        vsw(lambda v: v.tensor_tensor(out=gbB[:, :, :], in0=lg_all[:, :, :],
                                      in1=bc16(m2A), op=OP.is_equal))
        vsw(lambda v: v.tensor_scalar(out=gbC[:, :, :], in0=oh0[:, :, :],
                                      scalar1=-1.0, scalar2=1.0, op0=OP.mult, op1=OP.add))
        vsw(lambda v: v.tensor_tensor(out=gbB[:, :, :], in0=gbB[:, :, :],
                                      in1=gbC[:, :, :], op=OP.mult))
        vsw(lambda v: v.tensor_tensor(out=gbC[:, :, :], in0=gbB[:, :, :],
                                      in1=eCm8[:, :, :], op=OP.mult))
        vsw(lambda v: v.tensor_reduce(out=e1f[:, :], in_=gbC[:, :, :], axis=AX.X, op=OP.min))
        vsw(lambda v: v.tensor_scalar(out=e1f[:, :], in0=e1f[:, :], scalar1=8.0,
                                      scalar2=None, op0=OP.add))
        vsw(lambda v: v.tensor_tensor(out=oh1[:, :, :], in0=eConst[:, :, :],
                                      in1=bc16(e1f), op=OP.is_equal))
        vsw(lambda v: v.tensor_tensor(out=g1A[:, :], in0=e2xA[:, :], in1=r2A[:, :], op=OP.mult))
        v_ranks = c["vS"]

        def prefix_and_slots(oh, basev, e_f, g_f, slot_col, gate_waits):
            vsw(lambda v: v.memset(padA[:, :, :], 0.0))
            vsw(lambda v: v.tensor_copy(padA[:, 16:32, :], oh[:, :, :]))
            bufs = [padA, padB]
            for i, sh in enumerate((1, 2, 4, 8)):
                a, b = bufs[i % 2], bufs[(i + 1) % 2]
                vsw(lambda v, b=b: v.memset(b[:, 0:16, :], 0.0))
                vsw(lambda v, a=a, b=b, sh=sh: v.tensor_tensor(
                    out=b[:, 16:32, :], in0=a[:, 16:32, :],
                    in1=a[:, 16 - sh:32 - sh, :], op=OP.add))
            fin = bufs[0]
            vsw(lambda v, fin=fin: v.tensor_tensor(out=exb[:, :, :], in0=fin[:, 16:32, :],
                                                   in1=oh[:, :, :], op=OP.subtract))
            vsw(lambda v, fin=fin: v.tensor_copy(tot_sb[:, :], fin[:, 31, :]))
            v_tot = c["vS"]
            prog.emit("tensor", lambda t: t.matmul(PSb(0, 0, E), triU[:, :], tot_sb[:, :],
                                                   start=True, stop=True),
                      waits=[("vS", v_tot)], inc=("tS", 1))
            prog.emit("tensor", lambda t: t.matmul(PSb(1, 0, E), ones_f32x[:, :], tot_sb[:, :],
                                                   start=True, stop=True), inc=("tS", 1))
            t_pp = c["tS"]
            vsw(lambda v: v.tensor_copy(pp_sb[:, :], PSb(0, 0, E)), waits=[("tS", t_pp)])
            vsw(lambda v: v.tensor_copy(g0b_sb[:, :], PSb(1, 0, E)))
            if basev is not None:
                vsw(lambda v, basev=basev: v.tensor_tensor(out=pp_sb[:, :], in0=pp_sb[:, :],
                                                           in1=basev[:, :], op=OP.add))
            ppb = bass.AP(pp_sb, 0, [[E, 128], [0, NT16], [1, E]])
            vsw(lambda v, ppb=ppb: v.tensor_tensor(out=gbC[:, :, :], in0=exb[:, :, :],
                                                   in1=ppb, op=OP.add))
            vsw(lambda v, oh=oh: v.tensor_tensor(out=gbC[:, :, :], in0=gbC[:, :, :],
                                                 in1=oh[:, :, :], op=OP.mult))
            vsw(lambda v: v.reduce_sum(s0A[:, :], gbC[:, :, :], axis=AX.X))
            vsw(lambda v, e_f=e_f: v.scalar_tensor_tensor(
                out=slotf[:, :], in0=e_f[:, :], scalar=float(CAP), in1=s0A[:, :],
                op0=OP.mult, op1=OP.add))
            vsw(lambda v, slot_col=slot_col: v.tensor_copy(slotu[:, slot_col, :], slotf[:, :]))
            vsw(lambda v, g_f=g_f: v.tensor_copy(
                bass.AP(rowbuf, 1024, [[NT16 * 1028, 128], [1028, NT16], [1, 1]]),
                g_f[:, :]), waits=gate_waits)
            return c["vS"]

        v_slots0 = prefix_and_slots(oh0, None, e0f, r2A, 0, [])
        vsw(lambda v: v.tensor_copy(base_sb[:, :], g0b_sb[:, :]))

        disp0 = [0] * NT16
        for t16 in range(NT16):
            waits = [("vS", v_slots0), ("sS", s_row[t16]), ("dX", dx_zero_done)]
            prog.emit("gpsimd", lambda g, t16=t16: g.indirect_dma_start(
                out=xg_d, out_offset=bass.IndirectOffsetOnAxis(
                    ap=slotu[:, 0, t16:t16 + 1], axis=0),
                in_=rowbuf[:, t16, :], in_offset=None),
                waits=waits, inc=("dI", 16))
            disp0[t16] = c["dI"]

        v_slots1 = prefix_and_slots(oh1, base_sb, e1f, g1A, 1,
                                    [("dI", disp0[NT16 - 1])])
        disp1 = [0] * NT16
        for t16 in range(NT16):
            waits = [("vS", v_slots1)]
            prog.emit("gpsimd", lambda g, t16=t16: g.indirect_dma_start(
                out=xg_d, out_offset=bass.IndirectOffsetOnAxis(
                    ap=slotu[:, 1, t16:t16 + 1], axis=0),
                in_=rowbuf[:, t16, :], in_offset=None),
                waits=waits, inc=("dI", 16))
            disp1[t16] = c["dI"]
        di_dispatch = c["dI"]

        s_gelu_last = 0
        di_prev_scatter = di_dispatch
        di_scat_hist = []
        dw_w1 = [0, 0]
        dw_w1_cons = [0, 0]
        t_y_prev = 0
        t_tr_prev = 0
        v_dec_prev = 0
        for e in range(E):
            if e == 0:
                for fh in range(2):
                    prog.emit("scalar", lambda s, fh=fh: s.dma_start(
                        out=w1ring[:, fh, :, :],
                        in_=bass.AP(w1bf_d.tensor, fh * NC * 1024,
                                    [[2 * NC * 1024, 128], [1, NC * 1024]])),
                        waits=[("vS", v_slots1), ("dW", dw_cast_w1[0][fh])], inc=("dS", 16))
                    dw_w1[fh] = c["dS"]

            waits = [("dI", di_dispatch)]
            if e > 0:
                waits += [("tS", t_tr_prev), ("vS", v_dec_prev)]
            prog.emit("sync", lambda s, e=e: s.dma_start(
                out=xg_sb[:, :, :],
                in_=bass.AP(xg_d.tensor, e * CAP * 1028,
                            [[1028, 128], [128 * 1028, CAPT], [1, 1028]])),
                waits=waits, inc=("dX", 16))
            dx_xg = c["dX"]

            waits = ([("vS", v_slots1), ("dW", dw_cast_w2[0])] if e == 0
                     else [("tS", t_y_prev), ("dW", dw_cast_w2[e])])
            prog.emit("sync", lambda s, e=e: s.dma_start(
                out=w2res.rearrange("p a b c -> p (a b c)"),
                in_=w2bf_d[e]),
                waits=waits, inc=("dX", 16))
            dw_w2 = c["dX"]

            par = e % 2
            aAP = bass.AP(xg_sb, 1026, [[CAPT * 1028, 128], [1028, CAPT], [1, 1]])
            bAP = bass.AP(xg_sb, 1027, [[CAPT * 1028, 128], [1028, CAPT], [1, 1]])
            gAP = bass.AP(xg_sb, 1024, [[CAPT * 1028, 128], [1028, CAPT], [1, 1]])
            dwaits = [("dX", dx_xg)]
            if e >= 2:
                dwaits.append(("dI", di_scat_hist[e - 2]))
            v0 = vsw(lambda v, aAP=aAP, par=par: v.tensor_copy(dec_a[:, par, :], aAP),
                     waits=dwaits)
            vsw(lambda v, bAP=bAP, par=par: v.tensor_copy(dec_b[:, par, :], bAP))
            vsw(lambda v, gAP=gAP, par=par: v.tensor_copy(dec_g[:, par, :], gAP))
            vsw(lambda v, par=par: v.scalar_tensor_tensor(
                out=dec_t[:, par, :], in0=dec_a[:, par, :], scalar=64.0, in1=dec_b[:, par, :],
                op0=OP.mult, op1=OP.add))
            vsw(lambda v, par=par: v.tensor_scalar(out=dec_t[:, par, :], in0=dec_t[:, par, :],
                                                   scalar1=1.0, scalar2=None, op0=OP.subtract))
            vsw(lambda v, par=par: v.tensor_scalar(out=s0A[:, 0:CAPT], in0=dec_t[:, par, :],
                                                   scalar1=0.0, scalar2=float(T + 1), op0=OP.is_lt,
                                                   op1=OP.mult))
            vsw(lambda v, par=par: v.tensor_tensor(out=dec_t[:, par, :], in0=dec_t[:, par, :],
                                                   in1=s0A[:, 0:CAPT], op=OP.add))
            v_dec = vsw(lambda v, par=par: v.tensor_copy(dec_u[:, par, :], dec_t[:, par, :]))
            v_dec_prev = v_dec

            trev = {}
            for j in range(CAPT):
                for half in range(2):
                    unit = j * 2 + half
                    bank = unit % 4
                    waits = []
                    if unit == 0:
                        waits.append(("dX", dx_xg))
                        if e > 0:
                            waits.append(("sS", s_gelu_last))
                    if unit >= 4:
                        waits.append(("vS", trev[unit - 4]))
                    for q in range(4):
                        prog.emit("tensor", lambda t, j=j, half=half, bank=bank, q=q: t.matmul(
                            PSb(bank, q * 128, (q + 1) * 128),
                            xg_sb[:, j, half * 512 + q * 128:half * 512 + (q + 1) * 128],
                            ident_bf[:, :], start=True, stop=True),
                            waits=waits if q == 0 else [], inc=("tS", 1) if q == 3 else None)
                    t_trh = c["tS"]
                    prog.emit("vector", lambda v, j=j, half=half, bank=bank: v.tensor_copy(
                        x1g_fm[:, half * 4:(half + 1) * 4, j * 128:(j + 1) * 128], PSb(bank)),
                        waits=[("tS", t_trh)], inc=("vS", 1))
                    trev[unit] = c["vS"]
            v_tr_all = c["vS"]
            t_tr_prev = c["tS"]

            gelu_cnt = {}
            for fh in range(2):
                for fl in range(8):
                    fc = fh * 8 + fl
                    b0 = 2 * (fc % 2)
                    waits0 = [("dS", dw_w1[fh]), ("vS", v_tr_all)]
                    if fc >= 2:
                        waits0.append(("sS", gelu_cnt[fc - 2]))
                    for dc in range(NC):
                        for tt, (lo, hi) in enumerate(((0, 512), (512, CAP))):
                            prog.emit("tensor",
                                      lambda t, fh=fh, fl=fl, dc=dc, lo=lo, hi=hi, b0=b0, tt=tt: t.matmul(
                                          ps[:, (b0 + tt) * 512:(b0 + tt) * 512 + (hi - lo)],
                                          w1ring[:, fh, dc, fl * 128:(fl + 1) * 128],
                                          x1g_fm[:, dc, lo:hi],
                                          start=(dc == 0), stop=(dc == NC - 1)),
                                      waits=waits0 if (dc == 0 and tt == 0) else [],
                                      inc=("tS", 1) if (dc == NC - 1 and tt == 1) else None)
                    t_h = c["tS"]
                    prog.emit("scalar", lambda s, fc=fc, b0=b0: s.activation(
                        h_fm[:, fc, :], ps[:, b0 * 512:b0 * 512 + CAP], AF.Gelu),
                        waits=[("tS", t_h)], inc=("sS", 1))
                    gelu_cnt[fc] = c["sS"]
                dw_w1_cons[fh] = c["tS"]
                if e + 1 < E:
                    prog.emit("scalar", lambda s, e=e, fh=fh: s.dma_start(
                        out=w1ring[:, fh, :, :],
                        in_=bass.AP(w1bf_d.tensor,
                                    (e + 1) * 128 * 2 * NC * 1024 + fh * NC * 1024,
                                    [[2 * NC * 1024, 128], [1, NC * 1024]])),
                        waits=[("tS", dw_w1_cons[fh]), ("dW", dw_cast_w1[e + 1][fh])],
                        inc=("dS", 16))
                    dw_w1[fh] = c["dS"]
            s_gelu_last = c["sS"]

            yev = [0] * CAPT
            for j in range(CAPT):
                b0 = 4 + 2 * (j % 2)
                waits0 = [("dX", dw_w2), ("sS", s_gelu_last)]
                if j >= 2:
                    waits0.append(("sS", yev[j - 2]))
                for f in range(NFC):
                    for dh in range(2):
                        prog.emit("tensor", lambda t, j=j, f=f, dh=dh, b0=b0: t.matmul(
                            ps[:, (b0 + dh) * 512:(b0 + dh + 1) * 512],
                            h_fm[:, f, j * 128:(j + 1) * 128],
                            w2res[:, dh, f, :],
                            start=(f == 0), stop=(f == NFC - 1)),
                            waits=waits0 if (f == 0 and dh == 0) else [],
                            inc=("tS", 1) if (f == NFC - 1 and dh == 1) else None)
                t_y = c["tS"]
                prog.emit("scalar", lambda s, j=j, b0=b0, par=par: s.activation(
                    y_sb[:, j, :], ps[:, b0 * 512:(b0 + 2) * 512], AF.Copy,
                    scale=dec_g[:, par, j:j + 1]),
                    waits=[("tS", t_y), ("dI", di_prev_scatter)], inc=("sS", 1))
                yev[j] = c["sS"]
            t_y_prev = c["tS"]
            for j in range(CAPT):
                waits = [("sS", yev[j]), ("dI", di_prev_scatter), ("dW", dw_ztrash)]
                if e == 0 and j == 0:
                    waits.append(("dX", dx_acc_init))
                prog.emit("gpsimd", lambda g, j=j, par=par: g.indirect_dma_start(
                    out=acc_d, out_offset=bass.IndirectOffsetOnAxis(
                        ap=dec_u[:, par, j:j + 1], axis=0),
                    in_=y_sb[:, j, :], in_offset=None, compute_op=OP.add),
                    waits=waits, inc=("dI", 16))
            di_prev_scatter = c["dI"]
            di_scat_hist.append(di_prev_scatter)
        di_all_scatters = c["dI"]

        eload = [None] * NT16
        for t16 in range(NT16):
            eng, sem = ("sync", "dX") if t16 % 2 == 0 else ("scalar", "dS")
            prog.emit(eng, lambda s, t16=t16: s.dma_start(
                out=out_acc[:, t16, :], in_=acc_d[t16 * 128:(t16 + 1) * 128, :]),
                waits=[("dI", di_all_scatters)], inc=(sem, 16))
            eload[t16] = (sem, c[sem])
        s_estats = [0] * NT16
        for t16 in range(NT16):
            prog.emit("scalar", lambda s, t16=t16: s.activation(
                sq_scr[:, :], out_acc[:, t16, :], AF.Copy, accum_out=lnmsA[:, t16:t16 + 1]),
                waits=[eload[t16]], inc=("sS", 1))
            prog.emit("scalar", lambda s, t16=t16: s.activation(
                sq_scr[:, :], out_acc[:, t16, :], AF.Square, accum_out=lnssA[:, t16:t16 + 1]),
                inc=("sS", 1))
            s_estats[t16] = c["sS"]
        prog.emit("vector", lambda v: v.tensor_scalar(out=muA[:, :], in0=lnmsA[:, :],
                                                      scalar1=1.0 / D, scalar2=None, op0=OP.mult),
                  waits=[("sS", s_estats[NT16 - 1])], inc=("vS", 1))
        v_muE = c["vS"]
        prog.emit("scalar", lambda s: s.activation(t1A[:, :], muA[:, :], AF.Square),
                  waits=[("vS", v_muE)], selfw=True, inc=("sS", 1))
        s_musqE = c["sS"]
        prog.emit("vector", lambda v: v.tensor_scalar(out=t2A[:, :], in0=lnssA[:, :],
                                                      scalar1=1.0 / D, scalar2=EPS_LN,
                                                      op0=OP.mult, op1=OP.add), inc=("vS", 1))
        prog.emit("vector", lambda v: v.tensor_tensor(out=t2A[:, :], in0=t2A[:, :],
                                                      in1=t1A[:, :], op=OP.subtract),
                  waits=[("sS", s_musqE)], selfw=True, inc=("vS", 1))
        v_lnE = c["vS"]
        prog.emit("scalar", lambda s: s.activation(lnvA[:, :], t2A[:, :], AF.Ln),
                  waits=[("vS", v_lnE)], selfw=True, inc=("sS", 1))
        prog.emit("scalar", lambda s: s.activation(invA[:, :], lnvA[:, :], AF.Exp, scale=-0.5),
                  selfw=True, inc=("sS", 1))
        s_invE = c["sS"]
        for t16 in range(NT16):
            prog.emit("vector", lambda v, t16=t16: v.tensor_scalar(
                out=out_acc[:, t16, :], in0=out_acc[:, t16, :], scalar1=muA[:, t16:t16 + 1],
                scalar2=invA[:, t16:t16 + 1], op0=OP.subtract, op1=OP.mult),
                waits=[("sS", s_invE)], inc=("vS", 1))
            v_fin = c["vS"]
            eng, sem = ("gpsimd", "dW") if t16 % 2 == 0 else ("sync", "dX")
            prog.emit(eng, lambda s, t16=t16: s.dma_start(
                out=y_d[t16 * 128:(t16 + 1) * 128, :], in_=out_acc[:, t16, :]),
                waits=[("vS", v_fin)], inc=(sem, 16))

        prog.emit("sync", lambda s: s.nop(),
                  waits=[("dX", c["dX"]), ("dW", c["dW"]), ("dS", c["dS"])])

    with (
        nc.psum_tensor("PS", [128, 4096], F32) as ps,
        nc.semaphore("dW") as s_dW,
        nc.semaphore("dX") as s_dX,
        nc.semaphore("tS") as s_tS,
        nc.semaphore("vS") as s_vS,
        nc.semaphore("sS") as s_sS,
        nc.semaphore("gS") as s_gS,
        nc.semaphore("dI") as s_dI,
        nc.semaphore("dS") as s_dS,
        nc.Block() as block,
    ):
        sems = {"dW": s_dW, "dX": s_dX, "dI": s_dI, "dS": s_dS, "tS": s_tS, "vS": s_vS, "sS": s_sS, "gS": s_gS}
        build_program(ps)
        prog.run(block, sems)
    return nc


_NC_CACHE = None


def kernel(L0_attn_x, L0_attn_Wq, L0_attn_Wk, L0_attn_Wv, L0_attn_Wo, L0_attn_bo,
           L0_attn_q_scale, L0_attn_k_scale, L0_ffn_router_w, L0_ffn_w1, L0_ffn_w2):
    x = np.ascontiguousarray(np.asarray(L0_attn_x, dtype=np.float32))
    Wq = np.asarray(L0_attn_Wq, dtype=np.float32)
    Wk = np.asarray(L0_attn_Wk, dtype=np.float32)
    Wv = np.asarray(L0_attn_Wv, dtype=np.float32)
    Wo = np.asarray(L0_attn_Wo, dtype=np.float32)
    bo = np.ascontiguousarray(np.asarray(L0_attn_bo, dtype=np.float32).reshape(1, D))
    qsc = np.ascontiguousarray(np.asarray(L0_attn_q_scale, dtype=np.float32).reshape(H, DH))
    ksc = np.ascontiguousarray(np.asarray(L0_attn_k_scale, dtype=np.float32).reshape(H, DH))
    rw = np.asarray(L0_ffn_router_w, dtype=np.float32)
    w1 = np.asarray(L0_ffn_w1, dtype=np.float32)
    w2 = np.asarray(L0_ffn_w2, dtype=np.float32)

    wq_t = np.ascontiguousarray(Wq.T.reshape(NC, 128, D))
    wk_t = np.ascontiguousarray(Wk.T.reshape(NC, 128, D))
    wv_t = np.ascontiguousarray(Wv.T.reshape(NC, 128, D))
    wo_t = np.ascontiguousarray(Wo.T.reshape(NC, 128, D))
    rw_t = np.ascontiguousarray(rw.T.reshape(NC, 128, E))
    w1_t = np.ascontiguousarray(
        w1.transpose(0, 2, 1).reshape(E, NC, 128, 2, 1024).transpose(0, 3, 1, 2, 4))
    w2_t = np.ascontiguousarray(
        w2.transpose(0, 2, 1).reshape(E, NFC, 128, 2, 512).transpose(0, 3, 1, 2, 4))

    global _NC_CACHE
    if _NC_CACHE is None:
        _NC_CACHE = build_nc()
    nc = _NC_CACHE

    shared = {"wq": wq_t, "wk": wk_t, "wv": wv_t, "wo": wo_t, "bo": bo,
              "qsc": qsc, "ksc": ksc, "rw": rw_t, "w1": w1_t, "w2": w2_t}
    in_maps = []
    for core in range(8):
        m = dict(shared)
        m["x"] = np.ascontiguousarray(x[core])
        in_maps.append(m)
    res = run_bass_kernel_spmd(nc, in_maps, core_ids=list(range(8)))
    return np.stack([res.results[core]["y"] for core in range(8)], axis=0)


# version 13
# speedup vs baseline: 1.9695x; 1.0356x over previous
import numpy as np
import concourse.bass as bass
import concourse.mybir as mybir
from concourse.bass_utils import run_bass_kernel_spmd

F32 = mybir.dt.float32
U32 = mybir.dt.uint32
BF16 = mybir.dt.bfloat16
I32 = mybir.dt.int32
AF = mybir.ActivationFunctionType
OP = mybir.AluOpType
AX = mybir.AxisListType

T = 2048
D = 1024
H = 16
DH = 64
E = 8
FF = 2048
NC = D // 128
KC = T // 128
NT16 = T // 128
PAIRS = H // 2
QB = T // 512
NFC = FF // 128
CAP = 768
CAPT = CAP // 128
EPS_LN = 1e-5

ENGINES = ("tensor", "vector", "scalar", "sync", "gpsimd")
SEMS = ("dW", "dX", "dI", "dS", "tS", "vS", "sS", "gS")
SEM_OWNER = {"tS": "tensor", "vS": "vector", "sS": "scalar", "gS": "gpsimd"}


class Prog:
    def __init__(self):
        self.cnt = {s: 0 for s in SEMS}
        self.streams = {e: [] for e in ENGINES}

    def emit(self, eng, fn, waits=(), inc=None, selfw=False):
        self.streams[eng].append((tuple(waits), fn, inc, selfw))
        if inc is not None:
            self.cnt[inc[0]] += inc[1]

    def run(self, block, sems):
        own_sem = {v: k for k, v in SEM_OWNER.items()}

        def make_runner2(name):
            def _run(eng):
                satisfied = {s: 0 for s in SEMS}
                own = own_sem.get(name)
                own_emitted = 0
                for waits, fn, inc, selfw in self.streams[name]:
                    if selfw and own is not None and own_emitted > 0:
                        eng.wait_ge(sems[own], own_emitted)
                        satisfied[own] = max(satisfied[own], own_emitted)
                    for s, v in waits:
                        if v > satisfied[s]:
                            eng.wait_ge(sems[s], v)
                            satisfied[s] = v
                    r = fn(eng)
                    if inc is not None:
                        s, n = inc
                        r.then_inc(sems[s], n)
                        if s == own:
                            own_emitted += n
            return _run

        for name in ENGINES:
            getattr(block, name)(make_runner2(name))


def build_nc(stop_after=None):
    nc = bass.Bass("TRN2", target_bir_lowering=False, detect_race_conditions=False)

    x_d = nc.dram_tensor("x", [T, D], F32, kind="ExternalInput").ap()
    wq_d = nc.dram_tensor("wq", [NC, 128, D], F32, kind="ExternalInput").ap()
    wk_d = nc.dram_tensor("wk", [NC, 128, D], F32, kind="ExternalInput").ap()
    wv_d = nc.dram_tensor("wv", [NC, 128, D], F32, kind="ExternalInput").ap()
    wo_d = nc.dram_tensor("wo", [NC, 128, D], F32, kind="ExternalInput").ap()
    bo_d = nc.dram_tensor("bo", [1, D], F32, kind="ExternalInput").ap()
    qsc_d = nc.dram_tensor("qsc", [H, DH], F32, kind="ExternalInput").ap()
    ksc_d = nc.dram_tensor("ksc", [H, DH], F32, kind="ExternalInput").ap()
    rw_d = nc.dram_tensor("rw", [NC, 128, E], F32, kind="ExternalInput").ap()
    w1_d = nc.dram_tensor("w1", [E, 2, NC, 128, 1024], F32, kind="ExternalInput").ap()
    w2_d = nc.dram_tensor("w2", [E, 2, NFC, 128, 512], F32, kind="ExternalInput").ap()
    y_d = nc.dram_tensor("y", [T, D], F32, kind="ExternalOutput").ap()
    acc_d = nc.dram_tensor("acc_scr", [T + 1, D], F32).ap()
    w1bf_d = nc.dram_tensor("w1bf_scr", [E, 128, 2 * NC * 1024], BF16).ap()
    w2bf_d = nc.dram_tensor("w2bf_scr", [E, 128, 2 * NFC * 512], BF16).ap()
    xg_d = nc.dram_tensor("xg_scr", [E * CAP, 1028], BF16).ap()

    base = (nc.sbuf_base + 31) & ~31

    def at(name, shape, dtype, off):
        return nc.alloc_sbuf_tensor_at(name, shape, dtype, offset=off)

    REG_A = base
    REG_B = base + 65536
    REG_C = REG_B + 32768
    REG_D = REG_C + 32768
    REG_E = REG_D + 32768
    SM = REG_E + 32768

    qfm = at("qfm", [128, PAIRS, T], BF16, REG_A)
    kfm = at("kfm", [128, PAIRS, T], BF16, REG_A + 32768)
    out_acc = at("out_acc", [128, NT16, D], F32, REG_A)

    xg_sb = at("xg_sb", [128, CAPT, 1028], BF16, REG_A)
    x1g_fm = at("x1g_fm", [128, NC, CAP], BF16, REG_A + 12352)
    y_sb = at("y_sb", [128, CAPT, D], F32, REG_A + 24640)
    x1st = at("x1st", [128, 2, 1024], BF16, REG_B)
    eConst = at("eConst", [128, NT16, E], F32, REG_B + 4096)
    eCm8 = at("eCm8", [128, NT16, E], F32, REG_B + 4608)
    padA = at("padA", [128, 32, E], F32, REG_B + 5120)
    padB = at("padB", [128, 32, E], F32, REG_B + 6144)
    oh0 = at("oh0", [128, NT16, E], F32, REG_B + 7168)
    oh1 = at("oh1", [128, NT16, E], F32, REG_B + 7680)
    exb = at("exb", [128, NT16, E], F32, REG_B + 8192)
    trstg = at("trstg", [128, 2, 512], F32, REG_A + 49216)
    x0_fm = at("x0_fm", [128, NC, T], BF16, REG_B)
    p_slab = at("p_slab", [128, 2, KC, 512], BF16, REG_B)
    w1ring = at("w1ring", [128, 2, NC, 1024], BF16, REG_B)

    vaug = at("vaug", [128, KC, D], BF16, REG_C)
    rowbuf = at("rowbuf", [128, NT16, 1028], BF16, REG_C)

    wring = at("wring", [128, 2, NC, 1024], BF16, REG_D)
    attn_fm = at("attn_fm", [128, PAIRS, T], BF16, REG_D)
    h_fm = at("h_fm", [128, NFC, CAP], BF16, REG_D + 2048)

    x0stA = at("x0stA", [128, 2, D], F32, REG_E)
    q2bf = at("q2bf", [128, 2, 512], BF16, REG_E + 8192)
    invst = at("invst", [128, 2, 512], F32, REG_E + 10240)
    invraw = at("invraw", [128, 2, 512], F32, REG_E + 14336)
    wot = at("wot", [128, NC, D], BF16, REG_E)
    x0stC = at("x0stC", [128, 2, D], F32, REG_E + 18432)
    BT = REG_E + 26624
    lnmsA = at("lnmsA", [128, NT16], F32, BT)
    lnssA = at("lnssA", [128, NT16], F32, BT + 64)
    muA = at("muA", [128, NT16], F32, BT + 128)
    invA = at("invA", [128, NT16], F32, BT + 192)
    t1A = at("t1A", [128, NT16], F32, BT + 256)
    t2A = at("t2A", [128, NT16], F32, BT + 320)
    lnvA = at("lnvA", [128, NT16], F32, BT + 384)
    m1A = at("m1A", [128, NT16], F32, BT + 448)
    c1A = at("c1A", [128, NT16], F32, BT + 512)
    m2mA = at("m2mA", [128, NT16], F32, BT + 576)
    indA = at("indA", [128, NT16], F32, BT + 640)
    d12A = at("d12A", [128, NT16], F32, BT + 704)
    m2A = at("m2A", [128, NT16], F32, BT + 768)
    e2xA = at("e2xA", [128, NT16], F32, BT + 832)
    rA = at("rA", [128, NT16], F32, BT + 896)
    r2A = at("r2A", [128, NT16], F32, BT + 960)
    lg_all = at("lg_all", [128, NT16, E], F32, BT + 1024)
    gbA = at("gbA", [128, NT16, E], F32, BT + 1536)
    gbB = at("gbB", [128, NT16, E], F32, BT + 2048)
    gbC = at("gbC", [128, NT16, E], F32, BT + 2560)
    bo_row = at("bo_row", [1, D], BF16, BT + 3072)
    w2res = at("w2res", [128, 2, NFC, 512], BF16, REG_E)
    outst = at("outst", [128, 2, D], F32, REG_E)
    eln_in = at("eln_in", [128, 2, D], F32, REG_E + 8192)

    sm = SM
    smalls = {}

    def sb(name, shape, dtype):
        nonlocal sm
        t = at(name, shape, dtype, sm)
        n = 1
        for s in shape[1:]:
            n *= s
        sm += (n * (2 if dtype == BF16 else 4) + 31) & ~31
        smalls[name] = t
        return t

    rec_f = sb("rec_f", [128, 512], F32)
    rec_bf = sb("rec_bf", [128, 512], BF16)
    rb_sb = sb("rb_sb", [128, 512], F32)
    gates = sb("gates", [128, NT16, E], F32)
    ident_f32 = sb("ident_f32", [128, 128], F32)
    ident_bf = sb("ident_bf", [128, 128], BF16)
    iot_a = sb("iot_a", [128, 128], I32)
    iot_b = sb("iot_b", [128, 128], I32)
    indB = sb("indB", [128, 128], BF16)
    ones_bf = sb("ones_bf", [128, 128], BF16)
    sq_scr = sb("sq_scr", [128, D], BF16)
    qsc_sb = sb("qsc_sb", [16, DH], F32)
    ksc_sb = sb("ksc_sb", [16, DH], F32)
    qs_col = sb("qs_col", [128, NC], F32)
    ks_col = sb("ks_col", [128, NC], F32)
    mrow = sb("mrow", [1, 16], F32)
    mb_sb = sb("mb_sb", [128, 16], F32)
    negMp = sb("negMp", [128, PAIRS], F32)
    rwt = sb("rwt", [128, NC, E], BF16)
    lg_sb = sb("lg_sb", [128, E], F32)
    g_eq = sb("g_eq", [128, E], F32)
    g_lm = sb("g_lm", [128, E], F32)
    g_et = sb("g_et", [128, E], F32)
    g_ge2 = sb("g_ge2", [128, E], F32)
    ln_ms = sb("ln_ms", [128, 1], F32)
    ln_ss = sb("ln_ss", [128, 1], F32)
    ln_mu = sb("ln_mu", [128, 1], F32)
    ln_t1 = sb("ln_t1", [128, 1], F32)
    ln_t2 = sb("ln_t2", [128, 1], F32)
    ln_inv = sb("ln_inv", [128, 1], F32)
    g_m1 = sb("g_m1", [128, 1], F32)
    g_c1 = sb("g_c1", [128, 1], F32)
    g_m2m = sb("g_m2m", [128, 1], F32)
    g_ind = sb("g_ind", [128, 1], F32)
    g_d12 = sb("g_d12", [128, 1], F32)
    g_m2 = sb("g_m2", [128, 1], F32)
    g_nm1 = sb("g_nm1", [128, 1], F32)
    g_e2 = sb("g_e2", [128, 1], F32)
    g_e2x = sb("g_e2x", [128, 1], F32)
    g_r2 = sb("g_r2", [128, 1], F32)
    ln_t3 = sb("ln_t3", [128, 1], F32)
    eps_l2 = sb("eps_l2", [128, 1], F32)
    zline = sb("zline", [128, 1028], BF16)
    tid_i = sb("tid_i", [128, NT16], I32)
    a_i = sb("a_i", [128, NT16], I32)
    a_bf = sb("a_bf", [128, NT16], BF16)
    b_bf = sb("b_bf", [128, NT16], BF16)
    dec_a = sb("dec_a", [128, 2, CAPT], F32)
    dec_b = sb("dec_b", [128, 2, CAPT], F32)
    triU = sb("triU", [128, 128], F32)
    ones_f32x = sb("ones_f32x", [128, 128], F32)
    tid1 = sb("tid1", [128, NT16], F32)
    e0f = sb("e0f", [128, NT16], F32)
    e1f = sb("e1f", [128, NT16], F32)
    g1A = sb("g1A", [128, NT16], F32)
    s0A = sb("s0A", [128, NT16], F32)
    slotf = sb("slotf", [128, NT16], F32)
    slotu = sb("slotu", [128, 2, NT16], U32)
    tot_sb = sb("tot_sb", [128, E], F32)
    pp_sb = sb("pp_sb", [128, E], F32)
    g0b_sb = sb("g0b_sb", [128, E], F32)
    base_sb = sb("base_sb", [128, E], F32)
    dec_t = sb("dec_t", [128, 2, CAPT], F32)
    dec_g = sb("dec_g", [128, 2, CAPT], F32)
    dec_u = sb("dec_u", [128, 2, CAPT], U32)
    g_r = sb("g_r", [128, 1], F32)
    qmax = sb("qmax", [16, 1], F32)
    kmax = sb("kmax", [16, 1], F32)
    m16 = sb("m16", [16, 1], F32)
    assert sm <= nc.sbuf_top, (sm, nc.sbuf_top)

    prog = Prog()
    c = prog.cnt

    milestones = {}

    def build_program(ps, stop_after=stop_after):
        def PSb(bank, lo=0, hi=512):
            return ps[:, bank * 512 + lo: bank * 512 + hi]

        def dump_stop(src_handle, nelem, waits):
            flat = src_handle.rearrange("p ... -> p (...)") if False else None
            out_ap = bass.AP(y_d.tensor, 0, [[16384, 128], [1, nelem]])
            in_ap = bass.AP(src_handle, 0, [[nelem, 128], [1, nelem]])
            prog.emit("gpsimd", lambda g: g.dma_start(out=out_ap, in_=in_ap,
                      max_dma_last_dim=32768), waits=waits, inc=("dW", 16))
            prog.emit("sync", lambda s: s.nop(), waits=[("dW", c["dW"])])

        prog.emit("gpsimd", lambda g: g.dma_start(out=qsc_sb[:, :], in_=qsc_d[:, :]), inc=("dW", 16))
        dw_qsc = c["dW"]
        prog.emit("gpsimd", lambda g: g.dma_start(out=ksc_sb[:, :], in_=ksc_d[:, :]), inc=("dW", 16))
        dw_ksc = c["dW"]
        prog.emit("gpsimd", lambda g: g.dma_start(
            out=qs_col[:, :], in_=bass.AP(qsc_d.tensor, 0, [[1, 128], [128, NC]]),
            allow_slow_non_contiguous=True), inc=("dW", 16))
        dw_qcol = c["dW"]
        prog.emit("gpsimd", lambda g: g.dma_start(
            out=ks_col[:, :], in_=bass.AP(ksc_d.tensor, 0, [[1, 128], [128, NC]]),
            allow_slow_non_contiguous=True), inc=("dW", 16))
        dw_kcol = c["dW"]
        prog.emit("gpsimd", lambda g: g.dma_start(out=rwt[:, :, :], in_=rw_d.transpose([1, 0, 2])), inc=("dW", 16))
        dw_rwt = c["dW"]
        prog.emit("gpsimd", lambda g: g.dma_start(out=bo_row[:, :], in_=bo_d[:, :]), inc=("dW", 16))
        dw_bo = c["dW"]
        prog.emit("gpsimd", lambda g: g.dma_start(out=wring[:, 0, :, :], in_=wq_d.transpose([1, 0, 2])), inc=("dW", 16))
        dw_wq = c["dW"]
        prog.emit("gpsimd", lambda g: g.dma_start(out=wring[:, 1, :, :], in_=wk_d.transpose([1, 0, 2])), inc=("dW", 16))
        dw_wk = c["dW"]

        prog.emit("gpsimd", lambda g: g.iota(iot_a[:, :], [[-1, 128]], base=0, channel_multiplier=1),
                  inc=("gS", 1))
        g_i1 = c["gS"]
        prog.emit("vector", lambda v: v.tensor_scalar(out=ident_f32[:, :], in0=iot_a[:, :],
                                                      scalar1=0, scalar2=None, op0=OP.is_equal),
                  waits=[("gS", g_i1)], inc=("vS", 1))
        prog.emit("vector", lambda v: v.tensor_scalar(out=ident_bf[:, :], in0=iot_a[:, :],
                                                      scalar1=0, scalar2=None, op0=OP.is_equal),
                  inc=("vS", 1))
        prog.emit("vector", lambda v: v.tensor_scalar(out=triU[:, :], in0=iot_a[:, :],
                                                      scalar1=0, scalar2=None, op0=OP.is_lt),
                  inc=("vS", 1))
        v_id = c["vS"]
        prog.emit("gpsimd", lambda g: g.iota(iot_a[:, :], [[0, 128]], base=0, channel_multiplier=1),
                  waits=[("vS", v_id)], inc=("gS", 1))
        prog.emit("gpsimd", lambda g: g.iota(iot_b[:, :], [[1, 128]], base=0, channel_multiplier=0),
                  inc=("gS", 1))
        g_i2 = c["gS"]
        prog.emit("vector", lambda v: v.tensor_scalar(out=iot_a[:, :], in0=iot_a[:, :], scalar1=6,
                                                      scalar2=None, op0=OP.arith_shift_right),
                  waits=[("gS", g_i2)], inc=("vS", 1))
        prog.emit("vector", lambda v: v.tensor_scalar(out=iot_b[:, :], in0=iot_b[:, :], scalar1=6,
                                                      scalar2=None, op0=OP.arith_shift_right), inc=("vS", 1))
        prog.emit("vector", lambda v: v.tensor_tensor(out=indB[:, :], in0=iot_a[:, :], in1=iot_b[:, :],
                                                      op=OP.is_equal), selfw=True, inc=("vS", 1))
        prog.emit("vector", lambda v: v.memset(ones_bf[:, :], 1.0), inc=("vS", 1))
        prog.emit("vector", lambda v: v.memset(eps_l2[:, :], 1e-18), inc=("vS", 1))
        prog.emit("vector", lambda v: v.memset(ones_f32x[:, :], 1.0), inc=("vS", 1))
        prog.emit("vector", lambda v: v.memset(zline[:, :], 0.0), inc=("vS", 1))
        v_zline = c["vS"]
        prog.emit("gpsimd", lambda g: g.iota(tid1[:, :], [[128, NT16]], base=1, channel_multiplier=1,
                                             allow_small_or_imprecise_dtypes=True), inc=("gS", 1))
        prog.emit("gpsimd", lambda g: g.iota(tid_i[:, :], [[128, NT16]], base=1, channel_multiplier=1),
                  inc=("gS", 1))
        g_tid = c["gS"]
        prog.emit("vector", lambda v: v.tensor_scalar(out=a_i[:, :], in0=tid_i[:, :], scalar1=6,
                                                      scalar2=None, op0=OP.arith_shift_right),
                  waits=[("gS", g_tid)], inc=("vS", 1))
        prog.emit("vector", lambda v: v.tensor_copy(a_bf[:, :], a_i[:, :]), selfw=True, inc=("vS", 1))
        prog.emit("vector", lambda v: v.tensor_scalar(out=a_i[:, :], in0=a_i[:, :], scalar1=-64,
                                                      scalar2=None, op0=OP.mult), selfw=True, inc=("vS", 1))
        prog.emit("vector", lambda v: v.tensor_tensor(out=a_i[:, :], in0=tid_i[:, :], in1=a_i[:, :],
                                                      op=OP.add), selfw=True, inc=("vS", 1))
        prog.emit("vector", lambda v: v.tensor_copy(b_bf[:, :], a_i[:, :]), selfw=True, inc=("vS", 1))
        v_idsplit = c["vS"]
        prog.emit("gpsimd", lambda g: g.dma_start(
            out=acc_d[T:T + 1, :], in_=bass.AP(zline, 0, [[1026, 1], [1, D]])),
            waits=[("vS", v_zline)], inc=("dW", 16))
        dw_ztrash = c["dW"]
        prog.emit("vector", lambda v: v.reduce_max(qmax[:, :], qsc_sb[:, :], axis=AX.X,
                                                   apply_absolute_value=True),
                  waits=[("dW", dw_qsc)], inc=("vS", 1))
        prog.emit("vector", lambda v: v.reduce_max(kmax[:, :], ksc_sb[:, :], axis=AX.X,
                                                   apply_absolute_value=True),
                  waits=[("dW", dw_ksc)], inc=("vS", 1))
        prog.emit("vector", lambda v: v.tensor_tensor(out=m16[:, :], in0=qmax[:, :], in1=kmax[:, :],
                                                      op=OP.mult), selfw=True, inc=("vS", 1))
        v_m16 = c["vS"]
        prog.emit("gpsimd", lambda g: g.dma_start(
            out=bass.AP(mrow, 0, [[16, 1], [1, 16]]),
            in_=bass.AP(m16, 0, [[1, 16], [1, 1]]),
            allow_slow_non_contiguous=True),
            waits=[("vS", v_m16)], inc=("dW", 16))
        dw_mrow = c["dW"]
        prog.emit("vector", lambda v: v.tensor_copy(rec_bf[0:1, 0:16], mrow[0:1, :]),
                  waits=[("dW", dw_mrow)], inc=("vS", 1))
        v_mbf = c["vS"]
        prog.emit("tensor", lambda t: t.matmul(ps[:, 3616:3632][0:64, :16], ones_bf[0:1, 0:64],
                                               rec_bf[0:1, 0:16], start=True, stop=True),
                  waits=[("vS", v_mbf)], inc=("tS", 1))
        prog.emit("tensor", lambda t: t.matmul(ps[:, 3616:3632][64:128, :16], ones_bf[0:1, 0:64],
                                               rec_bf[0:1, 0:16], start=True, stop=True,
                                               tile_position=(0, 64)),
                  inc=("tS", 1))
        t_mb = c["tS"]
        prog.emit("vector", lambda v: v.tensor_copy(mb_sb[:, :], ps[:, 3616:3632]),
                  waits=[("tS", t_mb)], inc=("vS", 1))
        prog.emit("vector", lambda v: v.tensor_tensor(out=negMp[:, :], in0=mb_sb[:, 0:16:2],
                                                      in1=mb_sb[:, 1:16:2], op=OP.max), selfw=True, inc=("vS", 1))
        prog.emit("vector", lambda v: v.tensor_scalar(out=negMp[:, :], in0=negMp[:, :],
                                                      scalar1=-0.125, scalar2=None, op0=OP.mult),
                  selfw=True, inc=("vS", 1))
        v_setup = c["vS"]

        x0fm_done = [0] * NT16
        tr_evac = [0, 0]
        dx_tile = [0] * NT16

        def emit_x0_load(t16):
            par = t16 % 2
            waits = []
            if t16 >= 2:
                waits.append(("vS", x0fm_done[t16 - 2]))
            prog.emit("sync", lambda s, t16=t16, par=par: s.dma_start(
                out=x0stA[:, par, :], in_=x_d[t16 * 128:(t16 + 1) * 128, :]),
                waits=waits, inc=("dX", 16))
            dx_tile[t16] = c["dX"]

        emit_x0_load(0)
        emit_x0_load(1)
        for t16 in range(NT16):
            par = t16 % 2
            t_tr4 = [0, 0]
            for half in range(2):
                bank = 4 + half
                waits = [("dX", dx_tile[t16]), ("vS", v_id)]
                if t16 >= 1:
                    waits.append(("vS", tr_evac[half]))
                for j in range(4):
                    cc = half * 4 + j
                    prog.emit("tensor", lambda t, par=par, cc=cc, bank=bank, j=j: t.transpose(
                        PSb(bank, j * 128, (j + 1) * 128),
                        x0stA[:, par, cc * 128:(cc + 1) * 128], ident_f32[:, :]),
                        waits=waits if j == 0 else [], inc=("tS", 1) if j == 3 else None)
                t_tr4[half] = c["tS"]
            for half in range(2):
                bank = 4 + half
                prog.emit("vector", lambda v, t16=t16, half=half, bank=bank: v.tensor_copy(
                    x0_fm[:, half * 4:(half + 1) * 4, t16 * 128:(t16 + 1) * 128], PSb(bank)),
                    waits=[("tS", t_tr4[half])], inc=("vS", 1))
                tr_evac[half] = c["vS"]
            x0fm_done[t16] = c["vS"]
            if t16 + 2 < NT16:
                emit_x0_load(t16 + 2)
        v_x0fm = c["vS"]
        for blk in range(E * CAPT):
            if blk % 2 == 0:
                prog.emit("sync", lambda s, blk=blk: s.dma_start(
                    out=xg_d[blk * 128:(blk + 1) * 128, :], in_=zline[:, :]),
                    waits=[("vS", v_zline)] if blk == 0 else [], inc=("dX", 16))
            else:
                prog.emit("scalar", lambda s, blk=blk: s.dma_start(
                    out=xg_d[blk * 128:(blk + 1) * 128, :], in_=zline[:, :]),
                    waits=[("vS", v_zline)] if blk == 1 else [], inc=("dS", 16))
        dx_zero_done = c["dX"]
        ds_zero_done = c["dS"]
        if stop_after == "x0fm":
            return dump_stop(x0_fm, NC * T, [("vS", v_x0fm)])

        NU = 2 * NC * 2
        grp_done = [0] * NU
        sq_done = [[0, 0] for _ in range(NU)]
        ssmm_done = [[0, 0] for _ in range(NU)]
        exp_done = [[0, 0] for _ in range(NU)]
        stt_done = [[0, 0] for _ in range(NU)]

        def unit_info(u):
            proj = u // 16
            co = (u % 16) // 2
            half = u % 2
            wslot = 0 if proj == 0 else 1
            dst = qfm if proj == 0 else kfm
            scol = qs_col if proj == 0 else ks_col
            dw_w = dw_wq if proj == 0 else dw_wk
            dw_s = dw_qcol if proj == 0 else dw_kcol
            return proj, co, half, wslot, dst, scol, dw_w, dw_s

        def emit_ssmm(u):
            for loc in range(2):
                waits = [("sS", sq_done[u][loc])]
                if u == 0:
                    waits.append(("vS", v_setup))
                if u >= 1:
                    waits.append(("sS", exp_done[u - 1][loc]))
                prog.emit("tensor", lambda t, u=u, loc=loc: t.matmul(
                    PSb(6 + loc), indB[:, :], q2bf[:, loc, :], start=True, stop=True),
                    waits=waits, inc=("tS", 1))
                ssmm_done[u][loc] = c["tS"]

        def emit_grp(u):
            _, co, half, wslot, dst, scol, dw_w, dw_s = unit_info(u)
            b0 = (u % 3) * 2
            for dc in range(NC):
                waits = []
                if dc == 0:
                    waits = [("dW", dw_w), ("vS", v_x0fm)]
                    if u >= 3:
                        waits.append(("vS", stt_done[u - 3][1]))
                for loc in range(2):
                    t5 = half * 2 + loc
                    prog.emit("tensor", lambda t, co=co, dc=dc, t5=t5, wslot=wslot, b=b0 + loc: t.matmul(
                        PSb(b), wring[:, wslot, dc, co * 128:(co + 1) * 128],
                        x0_fm[:, dc, t5 * 512:(t5 + 1) * 512],
                        start=(dc == 0), stop=(dc == NC - 1)),
                        waits=waits if loc == 0 else [],
                        inc=("tS", 1) if (dc == NC - 1 and loc == 1) else None)
            grp_done[u] = c["tS"]

        def emit_sq(u):
            b0 = (u % 3) * 2
            for loc in range(2):
                waits = [("tS", grp_done[u])]
                if u >= 1:
                    waits.append(("tS", ssmm_done[u - 1][loc]))
                prog.emit("scalar", lambda s, loc=loc, b=b0 + loc: s.activation(
                    q2bf[:, loc, :], PSb(b), AF.Square),
                    waits=waits, inc=("sS", 1))
                sq_done[u][loc] = c["sS"]

        def emit_lnexp(u):
            for loc in range(2):
                waits = [("tS", ssmm_done[u][loc])]
                prog.emit("scalar", lambda s, loc=loc: s.activation(
                    invraw[:, loc, :], PSb(6 + loc), AF.Ln, bias=eps_l2[:, 0:1]),
                    waits=waits, inc=("sS", 1))
                w2_ = [("vS", stt_done[u - 1][loc])] if u >= 1 else []
                prog.emit("scalar", lambda s, loc=loc: s.activation(
                    invst[:, loc, :], invraw[:, loc, :], AF.Exp, scale=-0.5),
                    waits=w2_, selfw=True, inc=("sS", 1))
                exp_done[u][loc] = c["sS"]

        def emit_stt(u):
            _, co, half, wslot, dst, scol, dw_w, dw_s = unit_info(u)
            b0 = (u % 3) * 2
            for loc in range(2):
                t5 = half * 2 + loc
                prog.emit("vector", lambda v, co=co, t5=t5, loc=loc, b=b0 + loc, dst=dst, scol=scol: v.scalar_tensor_tensor(
                    out=dst[:, co, t5 * 512:(t5 + 1) * 512], in0=PSb(b),
                    scalar=scol[:, co:co + 1], in1=invst[:, loc, :],
                    op0=OP.mult, op1=OP.mult),
                    waits=[("dW", dw_s), ("sS", exp_done[u][loc])], inc=("vS", 1))
                stt_done[u][loc] = c["vS"]

        for u in range(NU + 2):
            if u >= 2:
                emit_ssmm(u - 2)
            if u < NU:
                emit_grp(u)
            if 1 <= u <= NU:
                emit_sq(u - 1)
            if u >= 2:
                emit_lnexp(u - 2)
                emit_stt(u - 2)
        v_q = stt_done[15][1]
        v_k = stt_done[NU - 1][1]
        if stop_after == "qfm":
            return dump_stop(qfm, PAIRS * T, [("vS", v_q)])
        prog.emit("gpsimd", lambda g: g.dma_start(out=wring[:, 0, :, :], in_=wv_d.transpose([1, 0, 2])),
                  waits=[("vS", v_q)], inc=("dW", 16))
        dw_wv = c["dW"]
        if stop_after == "kfm":
            return dump_stop(kfm, PAIRS * T, [("vS", v_k)])

        psv_evac = [0, 0]
        for t16 in range(NT16):
            pp = t16 % 2
            banks = (2 * pp, 2 * pp + 1)
            for dc in range(NC):
                waits = []
                if dc == 0:
                    waits = [("dW", dw_wv), ("vS", v_k)]
                    if t16 >= 2:
                        waits.append(("vS", psv_evac[pp]))
                for eh in range(2):
                    prog.emit("tensor", lambda t, t16=t16, dc=dc, eh=eh, banks=banks: t.matmul(
                        PSb(banks[eh]), x0_fm[:, dc, t16 * 128:(t16 + 1) * 128],
                        wring[:, 0, dc, eh * 512:(eh + 1) * 512],
                        start=(dc == 0), stop=(dc == NC - 1)),
                        waits=waits if eh == 0 else [],
                        inc=("tS", 1) if (dc == NC - 1 and eh == 1) else None)
            t_v = c["tS"]
            for eh in range(2):
                prog.emit("vector", lambda v, t16=t16, eh=eh, banks=banks: v.tensor_copy(
                    vaug[:, t16, eh * 512:(eh + 1) * 512], PSb(banks[eh])),
                    waits=[("tS", t_v)], inc=("vS", 1))
            psv_evac[pp] = c["vS"]
        v_vaug = c["vS"]
        if stop_after == "vaug":
            return dump_stop(vaug, KC * D, [("vS", v_vaug)])

        prog.emit("gpsimd", lambda g: g.dma_start(out=wot[:, :, :], in_=wo_d.transpose([1, 0, 2])),
                  waits=[("vS", v_vaug)], inc=("dW", 16))
        dw_wot = c["dW"]

        dw_cast_w1 = [[0, 0] for _ in range(E)]
        dw_cast_w2 = [0] * E
        for e in range(E):
            for fh in range(2):
                prog.emit("gpsimd", lambda g, e=e, fh=fh: g.dma_start(
                    out=bass.AP(w1bf_d.tensor, e * 128 * 2 * NC * 1024 + fh * NC * 1024,
                                [[2 * NC * 1024, 128], [1, NC * 1024]]),
                    in_=w1_d[e, fh].transpose([1, 0, 2])), inc=("dW", 16))
                dw_cast_w1[e][fh] = c["dW"]
            prog.emit("gpsimd", lambda g, e=e: g.dma_start(
                out=w2bf_d[e],
                in_=bass.AP(w2_d.tensor, e * 2 * NFC * 128 * 512,
                            [[512, 128], [NFC * 128 * 512, 2], [128 * 512, NFC], [1, 512]])),
                inc=("dW", 16))
            dw_cast_w2[e] = c["dW"]

        units = [(pr, qb) for pr in range(PAIRS) for qb in range(QB)]
        NU = len(units)
        prev_mult = [0, 0]
        prev_rec_par = [0, 0]
        av_last_final = 0
        ppb = [(0, 1), (2, 3)]
        qk_done = {}
        exp_done = {}

        def emit_qk(u, kc, extra):
            pr, qb = units[u]
            b0, b1 = ppb[kc % 2]
            prog.emit("tensor", lambda t, pr=pr, qb=qb, kc=kc, b0=b0: t.matmul(
                PSb(b0), kfm[0:64, pr, kc * 128:(kc + 1) * 128],
                qfm[0:64, pr, qb * 512:(qb + 1) * 512], start=True, stop=True),
                waits=extra)
            prog.emit("tensor", lambda t, pr=pr, qb=qb, kc=kc, b1=b1: t.matmul(
                PSb(b1), kfm[64:128, pr, kc * 128:(kc + 1) * 128],
                qfm[64:128, pr, qb * 512:(qb + 1) * 512], start=True, stop=True),
                inc=("tS", 1))
            qk_done[(u, kc)] = c["tS"]

        emit_qk(0, 0, [("vS", v_vaug)])
        emit_qk(0, 1, [])
        for it in range(NU):
            pr, qb = units[it]
            avb = 4 + (it % 2)
            DENB = 6 + (it % 2)
            for kc in range(KC):
                b0 = ppb[kc % 2][0]
                prog.emit("scalar", lambda s, pr=pr, kc=kc, b0=b0: s.activation(
                    p_slab[:, :, kc, :], ps[:, b0 * 512:(b0 + 2) * 512], AF.Exp,
                    bias=negMp[:, pr:pr + 1], scale=0.125),
                    waits=[("tS", qk_done[(it, kc)])], inc=("sS", 1))
                exp_done[(it, kc)] = c["sS"]
                waits = [("sS", exp_done[(it, kc)])]
                if kc == 0 and it >= 2:
                    waits.append(("vS", prev_mult[it % 2]))
                    waits.append(("vS", prev_rec_par[it % 2]))
                prog.emit("tensor", lambda t, pr=pr, kc=kc, avb=avb: t.matmul(
                    PSb(avb)[0:64, :], vaug[:, kc, (2 * pr) * 64:(2 * pr + 1) * 64],
                    p_slab[:, 0, kc, :], start=(kc == 0), stop=(kc == KC - 1)),
                    waits=waits)
                prog.emit("tensor", lambda t, pr=pr, kc=kc, avb=avb: t.matmul(
                    PSb(avb)[64:128, :], vaug[:, kc, (2 * pr + 1) * 64:(2 * pr + 2) * 64],
                    p_slab[:, 1, kc, :], start=(kc == 0), stop=(kc == KC - 1),
                    tile_position=(0, 64)))
                prog.emit("tensor", lambda t, kc=kc, DENB=DENB: t.matmul(
                    PSb(DENB)[0:64, :], ones_bf[:, 0:64], p_slab[:, 0, kc, :],
                    start=(kc == 0), stop=(kc == KC - 1), tile_position=(0, 0)))
                prog.emit("tensor", lambda t, kc=kc, DENB=DENB: t.matmul(
                    PSb(DENB)[64:128, :], ones_bf[:, 0:64], p_slab[:, 1, kc, :],
                    start=(kc == 0), stop=(kc == KC - 1), tile_position=(0, 64)),
                    inc=("tS", 1))
                if kc == KC - 1:
                    av_last = c["tS"]
                if kc + 2 < KC:
                    emit_qk(it, kc + 2, [])
                elif it + 1 < NU:
                    emit_qk(it + 1, kc + 2 - KC, [("sS", exp_done[(it, kc)])])

            prog.emit("vector", lambda v, DENB=DENB: v.reciprocal(rb_sb[:, :], PSb(DENB)),
                      waits=[("tS", av_last)], inc=("vS", 1))
            v_rec = c["vS"]
            prog.emit("vector", lambda v, pr=pr, qb=qb, avb=avb: v.tensor_tensor(
                out=attn_fm[:, pr, qb * 512:(qb + 1) * 512], in0=PSb(avb),
                in1=rb_sb[:, :], op=OP.mult), selfw=True, inc=("vS", 1))
            prev_mult[it % 2] = c["vS"]
            prev_rec_par[it % 2] = v_rec
            av_last_final = av_last
        v_attn = c["vS"]
        t_pslab_free = av_last_final
        if stop_after == "attn":
            return dump_stop(attn_fm, PAIRS * T, [("vS", v_attn)])

        dxc_tile = [0] * NT16
        pre_done = [0, 0]

        def emit_xc_load(t16):
            par = t16 % 2
            waits = [("tS", t_pslab_free)]
            if t16 >= 2:
                waits.append(("vS", pre_done[par]))
            prog.emit("sync", lambda s, t16=t16, par=par: s.dma_start(
                out=x0stC[:, par, :], in_=x_d[t16 * 128:(t16 + 1) * 128, :]),
                waits=waits, inc=("dX", 16))
            dxc_tile[t16] = c["dX"]

        emit_xc_load(0)
        emit_xc_load(1)
        s_stats = [0] * NT16
        for t16 in range(NT16):
            par = t16 % 2
            wob = (0, 1) if par == 0 else (2, 3)
            for dh2 in range(2):
                for cc in range(NC):
                    waits = []
                    if dh2 == 0 and cc == 0:
                        waits = [("dW", dw_wot), ("vS", v_attn)]
                        if t16 >= 2:
                            waits.append(("vS", pre_done[par]))
                    prog.emit("tensor", lambda t, t16=t16, dh2=dh2, cc=cc, wob=wob: t.matmul(
                        PSb(wob[dh2]), attn_fm[:, cc, t16 * 128:(t16 + 1) * 128],
                        wot[:, cc, dh2 * 512:(dh2 + 1) * 512],
                        start=(cc == 0), stop=False),
                        waits=waits)
                prog.emit("tensor", lambda t, dh2=dh2, wob=wob: t.matmul(
                    PSb(wob[dh2]), ones_bf[0:1, 0:128],
                    bo_row[0:1, dh2 * 512:(dh2 + 1) * 512],
                    start=False, stop=True),
                    waits=[("dW", dw_bo)] if dh2 == 0 else [],
                    inc=("tS", 1) if dh2 == 1 else None)
            t_wo = c["tS"]
            prog.emit("vector", lambda v, t16=t16, par=par, wob=wob: v.tensor_tensor(
                out=out_acc[:, t16, :], in0=ps[:, wob[0] * 512:(wob[0] + 2) * 512],
                in1=x0stC[:, par, :], op=OP.add),
                waits=[("tS", t_wo), ("dX", dxc_tile[t16])], inc=("vS", 1))
            v_pre = c["vS"]
            pre_done[par] = v_pre
            prog.emit("scalar", lambda s, t16=t16: s.activation(
                sq_scr[:, :], out_acc[:, t16, :], AF.Copy, accum_out=lnmsA[:, t16:t16 + 1]),
                waits=[("vS", v_pre)], inc=("sS", 1))
            prog.emit("scalar", lambda s, t16=t16: s.activation(
                sq_scr[:, :], out_acc[:, t16, :], AF.Square, accum_out=lnssA[:, t16:t16 + 1]),
                inc=("sS", 1))
            s_stats[t16] = c["sS"]
            if t16 + 2 < NT16:
                emit_xc_load(t16 + 2)
        t_wo15 = c["tS"]

        prog.emit("vector", lambda v: v.tensor_scalar(out=muA[:, :], in0=lnmsA[:, :],
                                                      scalar1=1.0 / D, scalar2=None, op0=OP.mult),
                  waits=[("sS", s_stats[NT16 - 1])], inc=("vS", 1))
        v_mu = c["vS"]
        prog.emit("scalar", lambda s: s.activation(t1A[:, :], muA[:, :], AF.Square),
                  waits=[("vS", v_mu)], selfw=True, inc=("sS", 1))
        s_musq = c["sS"]
        prog.emit("vector", lambda v: v.tensor_scalar(out=t2A[:, :], in0=lnssA[:, :],
                                                      scalar1=1.0 / D, scalar2=EPS_LN,
                                                      op0=OP.mult, op1=OP.add), inc=("vS", 1))
        prog.emit("vector", lambda v: v.tensor_tensor(out=t2A[:, :], in0=t2A[:, :],
                                                      in1=t1A[:, :], op=OP.subtract),
                  waits=[("sS", s_musq)], selfw=True, inc=("vS", 1))
        v_ln = c["vS"]
        prog.emit("scalar", lambda s: s.activation(lnvA[:, :], t2A[:, :], AF.Ln),
                  waits=[("vS", v_ln)], selfw=True, inc=("sS", 1))
        prog.emit("scalar", lambda s: s.activation(invA[:, :], lnvA[:, :], AF.Exp, scale=-0.5),
                  selfw=True, inc=("sS", 1))
        s_inv = c["sS"]

        x1_done = [0] * NT16
        tr_evac = [0, 0]
        t_lg_all = [0] * NT16
        s_row = [0] * NT16
        dx_accst = [0] * NT16
        for t16 in range(NT16):
            prog.emit("vector", lambda v, t16=t16: v.tensor_scalar(
                out=out_acc[:, t16, :], in0=out_acc[:, t16, :], scalar1=muA[:, t16:t16 + 1],
                scalar2=invA[:, t16:t16 + 1], op0=OP.subtract, op1=OP.mult),
                waits=[("sS", s_inv)], inc=("vS", 1))
            x1_done[t16] = c["vS"]
            prog.emit("scalar", lambda s, t16=t16: s.activation(
                rowbuf[:, t16, 0:1024], out_acc[:, t16, :], AF.Copy),
                waits=[("vS", x1_done[t16])], inc=("sS", 1))
            s_row[t16] = c["sS"]
            if t16 == 0:
                prog.emit("vector", lambda v: v.tensor_copy(
                    bass.AP(rowbuf, 1026, [[NT16 * 1028, 128], [1028, NT16], [1, 1]]),
                    a_bf[:, :]), waits=[("vS", v_idsplit)], inc=("vS", 1))
                prog.emit("vector", lambda v: v.tensor_copy(
                    bass.AP(rowbuf, 1027, [[NT16 * 1028, 128], [1028, NT16], [1, 1]]),
                    b_bf[:, :]), inc=("vS", 1))
                prog.emit("vector", lambda v: v.memset(
                    bass.AP(rowbuf, 1025, [[NT16 * 1028, 128], [1028, NT16], [1, 1]]), 0.0),
                    inc=("vS", 1))
            prog.emit("sync", lambda s, t16=t16: s.dma_start(
                out=acc_d[t16 * 128:(t16 + 1) * 128, :], in_=out_acc[:, t16, :]),
                waits=[("vS", x1_done[t16])], inc=("dX", 16))
            dx_accst[t16] = c["dX"]
            par = t16 % 2
            t_trh = [0, 0]
            for half in range(2):
                bank = 4 + half
                waits = [("vS", x1_done[t16])]
                if t16 >= 1:
                    waits.append(("vS", tr_evac[half]))
                for j in range(4):
                    cc = half * 4 + j
                    prog.emit("tensor", lambda t, t16=t16, cc=cc, bank=bank, j=j: t.transpose(
                        PSb(bank, j * 128, (j + 1) * 128),
                        out_acc[:, t16, cc * 128:(cc + 1) * 128], ident_f32[:, :]),
                        waits=waits if j == 0 else [], inc=("tS", 1) if j == 3 else None)
                t_trh[half] = c["tS"]
            for half in range(2):
                bank = 4 + half
                waits = [("tS", t_trh[half])]
                if t16 >= 2:
                    waits.append(("tS", t_lg_all[t16 - 2]))
                prog.emit("vector", lambda v, par=par, half=half, bank=bank: v.tensor_copy(
                    x1st[:, par, half * 512:(half + 1) * 512], PSb(bank)),
                    waits=waits, inc=("vS", 1))
                tr_evac[half] = c["vS"]
            v_stage = c["vS"]
            for cc in range(NC):
                prog.emit("tensor", lambda t, par=par, cc=cc: t.matmul(
                    PSb(6, 0, E), x1st[:, par, cc * 128:(cc + 1) * 128],
                    rwt[:, cc, :], start=(cc == 0), stop=(cc == NC - 1)),
                    waits=[("vS", v_stage), ("dW", dw_rwt)] if cc == 0 else [],
                    inc=("tS", 1) if cc == NC - 1 else None)
            t_lg_all[t16] = c["tS"]
            prog.emit("vector", lambda v, t16=t16: v.tensor_copy(lg_all[:, t16, :], PSb(6, 0, E)),
                      waits=[("tS", t_lg_all[t16])], inc=("vS", 1))
        v_lg = c["vS"]
        dx_acc_init = c["dX"]

        def bc16(th_):
            return bass.AP(th_, 0, [[NT16, 128], [1, NT16], [0, E]])

        prog.emit("vector", lambda v: v.reduce_max(m1A[:, :], lg_all[:, :, :], axis=AX.X),
                  selfw=True, inc=("vS", 1))
        prog.emit("vector", lambda v: v.tensor_tensor(out=gbA[:, :, :], in0=lg_all[:, :, :],
                                                      in1=bc16(m1A), op=OP.is_equal),
                  selfw=True, inc=("vS", 1))
        prog.emit("vector", lambda v: v.reduce_sum(c1A[:, :], gbA[:, :, :], axis=AX.X),
                  selfw=True, inc=("vS", 1))
        prog.emit("vector", lambda v: v.scalar_tensor_tensor(
            out=gbB[:, :, :], in0=gbA[:, :, :], scalar=-1e30, in1=lg_all[:, :, :],
            op0=OP.mult, op1=OP.add), selfw=True, inc=("vS", 1))
        prog.emit("vector", lambda v: v.reduce_max(m2mA[:, :], gbB[:, :, :], axis=AX.X),
                  selfw=True, inc=("vS", 1))
        prog.emit("vector", lambda v: v.tensor_scalar(out=indA[:, :], in0=c1A[:, :],
                                                      scalar1=2.0, scalar2=None, op0=OP.is_ge),
                  selfw=True, inc=("vS", 1))
        prog.emit("vector", lambda v: v.tensor_tensor(out=d12A[:, :], in0=m1A[:, :],
                                                      in1=m2mA[:, :], op=OP.subtract),
                  selfw=True, inc=("vS", 1))
        prog.emit("vector", lambda v: v.tensor_tensor(out=d12A[:, :], in0=indA[:, :],
                                                      in1=d12A[:, :], op=OP.mult),
                  selfw=True, inc=("vS", 1))
        prog.emit("vector", lambda v: v.tensor_tensor(out=m2A[:, :], in0=d12A[:, :],
                                                      in1=m2mA[:, :], op=OP.add),
                  selfw=True, inc=("vS", 1))
        prog.emit("vector", lambda v: v.tensor_tensor(out=gbA[:, :, :], in0=lg_all[:, :, :],
                                                      in1=bc16(m1A), op=OP.subtract),
                  selfw=True, inc=("vS", 1))
        prog.emit("vector", lambda v: v.tensor_tensor(out=e2xA[:, :], in0=m2A[:, :],
                                                      in1=m1A[:, :], op=OP.subtract),
                  selfw=True, inc=("vS", 1))
        v_g = c["vS"]
        prog.emit("scalar", lambda s: s.activation(gbC[:, :, :], gbA[:, :, :], AF.Exp),
                  waits=[("vS", v_g)], selfw=True, inc=("sS", 1))
        prog.emit("scalar", lambda s: s.activation(e2xA[:, :], e2xA[:, :], AF.Exp),
                  inc=("sS", 1))
        s_e2 = c["sS"]
        prog.emit("vector", lambda v: v.tensor_scalar(out=rA[:, :], in0=e2xA[:, :],
                                                      scalar1=1.0, scalar2=None, op0=OP.add),
                  waits=[("sS", s_e2)], inc=("vS", 1))
        prog.emit("vector", lambda v: v.reciprocal(r2A[:, :], rA[:, :]), selfw=True, inc=("vS", 1))
        prog.emit("vector", lambda v: v.tensor_tensor(out=gbB[:, :, :], in0=lg_all[:, :, :],
                                                      in1=bc16(m2A), op=OP.is_ge),
                  selfw=True, inc=("vS", 1))
        prog.emit("vector", lambda v: v.tensor_tensor(out=gbB[:, :, :], in0=gbC[:, :, :],
                                                      in1=gbB[:, :, :], op=OP.mult),
                  selfw=True, inc=("vS", 1))
        prog.emit("vector", lambda v: v.tensor_tensor(out=gates[:, :, :], in0=gbB[:, :, :],
                                                      in1=bc16(r2A), op=OP.mult),
                  selfw=True, inc=("vS", 1))
        v_x1_all = c["vS"]
        v_gates_all = c["vS"]
        if stop_after == "x1":
            return dump_stop(out_acc, NT16 * D, [("vS", v_x1_all)])
        if stop_after == "gates":
            return dump_stop(gates, NT16 * E, [("vS", v_x1_all)])

        def vsw(fn, waits=()):
            prog.emit("vector", fn, waits=waits, selfw=True, inc=("vS", 1))
            return c["vS"]

        prog.emit("gpsimd", lambda g: g.iota(eConst.rearrange("p a b -> p (a b)"),
                                             [[0, NT16], [1, E]], base=0, channel_multiplier=0,
                                             allow_small_or_imprecise_dtypes=True),
                  waits=[("tS", t_pslab_free)], inc=("gS", 1))
        prog.emit("gpsimd", lambda g: g.iota(eCm8.rearrange("p a b -> p (a b)"),
                                             [[0, NT16], [1, E]], base=-8, channel_multiplier=0,
                                             allow_small_or_imprecise_dtypes=True), inc=("gS", 1))
        g_eiota = c["gS"]
        vsw(lambda v: v.tensor_tensor(out=gbA[:, :, :], in0=lg_all[:, :, :],
                                      in1=bc16(m1A), op=OP.is_equal))
        vsw(lambda v: v.tensor_tensor(out=gbB[:, :, :], in0=gbA[:, :, :],
                                      in1=eCm8[:, :, :], op=OP.mult), waits=[("gS", g_eiota)])
        vsw(lambda v: v.tensor_reduce(out=e0f[:, :], in_=gbB[:, :, :], axis=AX.X, op=OP.min))
        vsw(lambda v: v.tensor_scalar(out=e0f[:, :], in0=e0f[:, :], scalar1=8.0,
                                      scalar2=None, op0=OP.add))
        vsw(lambda v: v.tensor_tensor(out=oh0[:, :, :], in0=eConst[:, :, :],
                                      in1=bc16(e0f), op=OP.is_equal))
        vsw(lambda v: v.tensor_tensor(out=gbB[:, :, :], in0=lg_all[:, :, :],
                                      in1=bc16(m2A), op=OP.is_equal))
        vsw(lambda v: v.tensor_scalar(out=gbC[:, :, :], in0=oh0[:, :, :],
                                      scalar1=-1.0, scalar2=1.0, op0=OP.mult, op1=OP.add))
        vsw(lambda v: v.tensor_tensor(out=gbB[:, :, :], in0=gbB[:, :, :],
                                      in1=gbC[:, :, :], op=OP.mult))
        vsw(lambda v: v.tensor_tensor(out=gbC[:, :, :], in0=gbB[:, :, :],
                                      in1=eCm8[:, :, :], op=OP.mult))
        vsw(lambda v: v.tensor_reduce(out=e1f[:, :], in_=gbC[:, :, :], axis=AX.X, op=OP.min))
        vsw(lambda v: v.tensor_scalar(out=e1f[:, :], in0=e1f[:, :], scalar1=8.0,
                                      scalar2=None, op0=OP.add))
        vsw(lambda v: v.tensor_tensor(out=oh1[:, :, :], in0=eConst[:, :, :],
                                      in1=bc16(e1f), op=OP.is_equal))
        vsw(lambda v: v.tensor_tensor(out=g1A[:, :], in0=e2xA[:, :], in1=r2A[:, :], op=OP.mult))
        v_ranks = c["vS"]

        def prefix_and_slots(oh, basev, e_f, g_f, slot_col, gate_waits):
            vsw(lambda v: v.memset(padA[:, :, :], 0.0))
            vsw(lambda v: v.tensor_copy(padA[:, 16:32, :], oh[:, :, :]))
            bufs = [padA, padB]
            for i, sh in enumerate((1, 2, 4, 8)):
                a, b = bufs[i % 2], bufs[(i + 1) % 2]
                vsw(lambda v, b=b: v.memset(b[:, 0:16, :], 0.0))
                vsw(lambda v, a=a, b=b, sh=sh: v.tensor_tensor(
                    out=b[:, 16:32, :], in0=a[:, 16:32, :],
                    in1=a[:, 16 - sh:32 - sh, :], op=OP.add))
            fin = bufs[0]
            vsw(lambda v, fin=fin: v.tensor_tensor(out=exb[:, :, :], in0=fin[:, 16:32, :],
                                                   in1=oh[:, :, :], op=OP.subtract))
            vsw(lambda v, fin=fin: v.tensor_copy(tot_sb[:, :], fin[:, 31, :]))
            v_tot = c["vS"]
            prog.emit("tensor", lambda t: t.matmul(PSb(0, 0, E), triU[:, :], tot_sb[:, :],
                                                   start=True, stop=True),
                      waits=[("vS", v_tot)], inc=("tS", 1))
            prog.emit("tensor", lambda t: t.matmul(PSb(1, 0, E), ones_f32x[:, :], tot_sb[:, :],
                                                   start=True, stop=True), inc=("tS", 1))
            t_pp = c["tS"]
            vsw(lambda v: v.tensor_copy(pp_sb[:, :], PSb(0, 0, E)), waits=[("tS", t_pp)])
            vsw(lambda v: v.tensor_copy(g0b_sb[:, :], PSb(1, 0, E)))
            if basev is not None:
                vsw(lambda v, basev=basev: v.tensor_tensor(out=pp_sb[:, :], in0=pp_sb[:, :],
                                                           in1=basev[:, :], op=OP.add))
            ppb = bass.AP(pp_sb, 0, [[E, 128], [0, NT16], [1, E]])
            vsw(lambda v, ppb=ppb: v.tensor_tensor(out=gbC[:, :, :], in0=exb[:, :, :],
                                                   in1=ppb, op=OP.add))
            vsw(lambda v, oh=oh: v.tensor_tensor(out=gbC[:, :, :], in0=gbC[:, :, :],
                                                 in1=oh[:, :, :], op=OP.mult))
            vsw(lambda v: v.reduce_sum(s0A[:, :], gbC[:, :, :], axis=AX.X))
            vsw(lambda v, e_f=e_f: v.scalar_tensor_tensor(
                out=slotf[:, :], in0=e_f[:, :], scalar=float(CAP), in1=s0A[:, :],
                op0=OP.mult, op1=OP.add))
            vsw(lambda v, slot_col=slot_col: v.tensor_copy(slotu[:, slot_col, :], slotf[:, :]))
            vsw(lambda v, g_f=g_f: v.tensor_copy(
                bass.AP(rowbuf, 1024, [[NT16 * 1028, 128], [1028, NT16], [1, 1]]),
                g_f[:, :]), waits=gate_waits)
            return c["vS"]

        v_slots0 = prefix_and_slots(oh0, None, e0f, r2A, 0, [])
        vsw(lambda v: v.tensor_copy(base_sb[:, :], g0b_sb[:, :]))

        disp0 = [0] * NT16
        for t16 in range(NT16):
            waits = [("vS", v_slots0), ("sS", s_row[t16]), ("dX", dx_zero_done),
                     ("dS", ds_zero_done)]
            prog.emit("gpsimd", lambda g, t16=t16: g.indirect_dma_start(
                out=xg_d, out_offset=bass.IndirectOffsetOnAxis(
                    ap=slotu[:, 0, t16:t16 + 1], axis=0),
                in_=rowbuf[:, t16, :], in_offset=None),
                waits=waits, inc=("dI", 16))
            disp0[t16] = c["dI"]

        v_slots1 = prefix_and_slots(oh1, base_sb, e1f, g1A, 1,
                                    [("dI", disp0[NT16 - 1])])
        disp1 = [0] * NT16
        for t16 in range(NT16):
            waits = [("vS", v_slots1)]
            prog.emit("gpsimd", lambda g, t16=t16: g.indirect_dma_start(
                out=xg_d, out_offset=bass.IndirectOffsetOnAxis(
                    ap=slotu[:, 1, t16:t16 + 1], axis=0),
                in_=rowbuf[:, t16, :], in_offset=None),
                waits=waits, inc=("dI", 16))
            disp1[t16] = c["dI"]
        di_dispatch = c["dI"]

        s_gelu_last = 0
        di_prev_scatter = di_dispatch
        di_scat_hist = []
        dw_w1 = [0, 0]
        dw_w1_cons = [0, 0]
        t_y_prev = 0
        t_tr_prev = 0
        v_dec_prev = 0
        for e in range(E):
            if e == 0:
                for fh in range(2):
                    prog.emit("scalar", lambda s, fh=fh: s.dma_start(
                        out=w1ring[:, fh, :, :],
                        in_=bass.AP(w1bf_d.tensor, fh * NC * 1024,
                                    [[2 * NC * 1024, 128], [1, NC * 1024]])),
                        waits=[("vS", v_slots1), ("dW", dw_cast_w1[0][fh])], inc=("dS", 16))
                    dw_w1[fh] = c["dS"]

            waits = [("dI", di_dispatch)]
            if e > 0:
                waits += [("tS", t_tr_prev), ("vS", v_dec_prev)]
            prog.emit("sync", lambda s, e=e: s.dma_start(
                out=xg_sb[:, :, :],
                in_=bass.AP(xg_d.tensor, e * CAP * 1028,
                            [[1028, 128], [128 * 1028, CAPT], [1, 1028]])),
                waits=waits, inc=("dX", 16))
            dx_xg = c["dX"]

            waits = ([("vS", v_slots1), ("dW", dw_cast_w2[0])] if e == 0
                     else [("tS", t_y_prev), ("dW", dw_cast_w2[e])])
            prog.emit("sync", lambda s, e=e: s.dma_start(
                out=w2res.rearrange("p a b c -> p (a b c)"),
                in_=w2bf_d[e]),
                waits=waits, inc=("dX", 16))
            dw_w2 = c["dX"]

            par = e % 2
            aAP = bass.AP(xg_sb, 1026, [[CAPT * 1028, 128], [1028, CAPT], [1, 1]])
            bAP = bass.AP(xg_sb, 1027, [[CAPT * 1028, 128], [1028, CAPT], [1, 1]])
            gAP = bass.AP(xg_sb, 1024, [[CAPT * 1028, 128], [1028, CAPT], [1, 1]])
            dwaits = [("dX", dx_xg)]
            if e >= 2:
                dwaits.append(("dI", di_scat_hist[e - 2]))
            v0 = vsw(lambda v, aAP=aAP, par=par: v.tensor_copy(dec_a[:, par, :], aAP),
                     waits=dwaits)
            vsw(lambda v, bAP=bAP, par=par: v.tensor_copy(dec_b[:, par, :], bAP))
            vsw(lambda v, gAP=gAP, par=par: v.tensor_copy(dec_g[:, par, :], gAP))
            vsw(lambda v, par=par: v.scalar_tensor_tensor(
                out=dec_t[:, par, :], in0=dec_a[:, par, :], scalar=64.0, in1=dec_b[:, par, :],
                op0=OP.mult, op1=OP.add))
            vsw(lambda v, par=par: v.tensor_scalar(out=dec_t[:, par, :], in0=dec_t[:, par, :],
                                                   scalar1=1.0, scalar2=None, op0=OP.subtract))
            vsw(lambda v, par=par: v.tensor_scalar(out=s0A[:, 0:CAPT], in0=dec_t[:, par, :],
                                                   scalar1=0.0, scalar2=float(T + 1), op0=OP.is_lt,
                                                   op1=OP.mult))
            vsw(lambda v, par=par: v.tensor_tensor(out=dec_t[:, par, :], in0=dec_t[:, par, :],
                                                   in1=s0A[:, 0:CAPT], op=OP.add))
            v_dec = vsw(lambda v, par=par: v.tensor_copy(dec_u[:, par, :], dec_t[:, par, :]))
            v_dec_prev = v_dec

            trev = {}
            for j in range(CAPT):
                for half in range(2):
                    unit = j * 2 + half
                    bank = unit % 4
                    waits = []
                    if unit == 0:
                        waits.append(("dX", dx_xg))
                        if e > 0:
                            waits.append(("sS", s_gelu_last))
                    if unit >= 4:
                        waits.append(("vS", trev[unit - 4]))
                    for q in range(4):
                        prog.emit("tensor", lambda t, j=j, half=half, bank=bank, q=q: t.matmul(
                            PSb(bank, q * 128, (q + 1) * 128),
                            xg_sb[:, j, half * 512 + q * 128:half * 512 + (q + 1) * 128],
                            ident_bf[:, :], start=True, stop=True),
                            waits=waits if q == 0 else [], inc=("tS", 1) if q == 3 else None)
                    t_trh = c["tS"]
                    prog.emit("vector", lambda v, j=j, half=half, bank=bank: v.tensor_copy(
                        x1g_fm[:, half * 4:(half + 1) * 4, j * 128:(j + 1) * 128], PSb(bank)),
                        waits=[("tS", t_trh)], inc=("vS", 1))
                    trev[unit] = c["vS"]
            v_tr_all = c["vS"]
            t_tr_prev = c["tS"]

            gelu_cnt = {}
            for fh in range(2):
                for fl in range(8):
                    fc = fh * 8 + fl
                    b0 = 2 * (fc % 2)
                    waits0 = [("dS", dw_w1[fh]), ("vS", v_tr_all)]
                    if fc >= 2:
                        waits0.append(("sS", gelu_cnt[fc - 2]))
                    for dc in range(NC):
                        for tt, (lo, hi) in enumerate(((0, 512), (512, CAP))):
                            prog.emit("tensor",
                                      lambda t, fh=fh, fl=fl, dc=dc, lo=lo, hi=hi, b0=b0, tt=tt: t.matmul(
                                          ps[:, (b0 + tt) * 512:(b0 + tt) * 512 + (hi - lo)],
                                          w1ring[:, fh, dc, fl * 128:(fl + 1) * 128],
                                          x1g_fm[:, dc, lo:hi],
                                          start=(dc == 0), stop=(dc == NC - 1)),
                                      waits=waits0 if (dc == 0 and tt == 0) else [],
                                      inc=("tS", 1) if (dc == NC - 1 and tt == 1) else None)
                    t_h = c["tS"]
                    prog.emit("scalar", lambda s, fc=fc, b0=b0: s.activation(
                        h_fm[:, fc, :], ps[:, b0 * 512:b0 * 512 + CAP], AF.Gelu),
                        waits=[("tS", t_h)], inc=("sS", 1))
                    gelu_cnt[fc] = c["sS"]
                dw_w1_cons[fh] = c["tS"]
                if e + 1 < E:
                    prog.emit("scalar", lambda s, e=e, fh=fh: s.dma_start(
                        out=w1ring[:, fh, :, :],
                        in_=bass.AP(w1bf_d.tensor,
                                    (e + 1) * 128 * 2 * NC * 1024 + fh * NC * 1024,
                                    [[2 * NC * 1024, 128], [1, NC * 1024]])),
                        waits=[("tS", dw_w1_cons[fh]), ("dW", dw_cast_w1[e + 1][fh])],
                        inc=("dS", 16))
                    dw_w1[fh] = c["dS"]
            s_gelu_last = c["sS"]

            yev = [0] * CAPT
            for j in range(CAPT):
                b0 = 4 + 2 * (j % 2)
                waits0 = [("dX", dw_w2), ("sS", s_gelu_last)]
                if j >= 2:
                    waits0.append(("sS", yev[j - 2]))
                for f in range(NFC):
                    for dh in range(2):
                        prog.emit("tensor", lambda t, j=j, f=f, dh=dh, b0=b0: t.matmul(
                            ps[:, (b0 + dh) * 512:(b0 + dh + 1) * 512],
                            h_fm[:, f, j * 128:(j + 1) * 128],
                            w2res[:, dh, f, :],
                            start=(f == 0), stop=(f == NFC - 1)),
                            waits=waits0 if (f == 0 and dh == 0) else [],
                            inc=("tS", 1) if (f == NFC - 1 and dh == 1) else None)
                t_y = c["tS"]
                prog.emit("scalar", lambda s, j=j, b0=b0, par=par: s.activation(
                    y_sb[:, j, :], ps[:, b0 * 512:(b0 + 2) * 512], AF.Copy,
                    scale=dec_g[:, par, j:j + 1]),
                    waits=[("tS", t_y), ("dI", di_prev_scatter)], inc=("sS", 1))
                yev[j] = c["sS"]
            t_y_prev = c["tS"]
            for j in range(CAPT):
                waits = [("sS", yev[j]), ("dI", di_prev_scatter), ("dW", dw_ztrash)]
                if e == 0 and j == 0:
                    waits.append(("dX", dx_acc_init))
                prog.emit("gpsimd", lambda g, j=j, par=par: g.indirect_dma_start(
                    out=acc_d, out_offset=bass.IndirectOffsetOnAxis(
                        ap=dec_u[:, par, j:j + 1], axis=0),
                    in_=y_sb[:, j, :], in_offset=None, compute_op=OP.add),
                    waits=waits, inc=("dI", 16))
            di_prev_scatter = c["dI"]
            di_scat_hist.append(di_prev_scatter)
        di_all_scatters = c["dI"]

        eload = [None] * NT16
        for t16 in range(NT16):
            eng, sem = ("sync", "dX") if t16 % 2 == 0 else ("scalar", "dS")
            prog.emit(eng, lambda s, t16=t16: s.dma_start(
                out=out_acc[:, t16, :], in_=acc_d[t16 * 128:(t16 + 1) * 128, :]),
                waits=[("dI", di_all_scatters)], inc=(sem, 16))
            eload[t16] = (sem, c[sem])
        s_estats = [0] * NT16
        for t16 in range(NT16):
            prog.emit("scalar", lambda s, t16=t16: s.activation(
                sq_scr[:, :], out_acc[:, t16, :], AF.Copy, accum_out=lnmsA[:, t16:t16 + 1]),
                waits=[eload[t16]], inc=("sS", 1))
            prog.emit("scalar", lambda s, t16=t16: s.activation(
                sq_scr[:, :], out_acc[:, t16, :], AF.Square, accum_out=lnssA[:, t16:t16 + 1]),
                inc=("sS", 1))
            s_estats[t16] = c["sS"]
        prog.emit("vector", lambda v: v.tensor_scalar(out=muA[:, :], in0=lnmsA[:, :],
                                                      scalar1=1.0 / D, scalar2=None, op0=OP.mult),
                  waits=[("sS", s_estats[NT16 - 1])], inc=("vS", 1))
        v_muE = c["vS"]
        prog.emit("scalar", lambda s: s.activation(t1A[:, :], muA[:, :], AF.Square),
                  waits=[("vS", v_muE)], selfw=True, inc=("sS", 1))
        s_musqE = c["sS"]
        prog.emit("vector", lambda v: v.tensor_scalar(out=t2A[:, :], in0=lnssA[:, :],
                                                      scalar1=1.0 / D, scalar2=EPS_LN,
                                                      op0=OP.mult, op1=OP.add), inc=("vS", 1))
        prog.emit("vector", lambda v: v.tensor_tensor(out=t2A[:, :], in0=t2A[:, :],
                                                      in1=t1A[:, :], op=OP.subtract),
                  waits=[("sS", s_musqE)], selfw=True, inc=("vS", 1))
        v_lnE = c["vS"]
        prog.emit("scalar", lambda s: s.activation(lnvA[:, :], t2A[:, :], AF.Ln),
                  waits=[("vS", v_lnE)], selfw=True, inc=("sS", 1))
        prog.emit("scalar", lambda s: s.activation(invA[:, :], lnvA[:, :], AF.Exp, scale=-0.5),
                  selfw=True, inc=("sS", 1))
        s_invE = c["sS"]
        for t16 in range(NT16):
            prog.emit("vector", lambda v, t16=t16: v.tensor_scalar(
                out=out_acc[:, t16, :], in0=out_acc[:, t16, :], scalar1=muA[:, t16:t16 + 1],
                scalar2=invA[:, t16:t16 + 1], op0=OP.subtract, op1=OP.mult),
                waits=[("sS", s_invE)], inc=("vS", 1))
            v_fin = c["vS"]
            eng, sem = ("gpsimd", "dW") if t16 % 2 == 0 else ("sync", "dX")
            prog.emit(eng, lambda s, t16=t16: s.dma_start(
                out=y_d[t16 * 128:(t16 + 1) * 128, :], in_=out_acc[:, t16, :]),
                waits=[("vS", v_fin)], inc=(sem, 16))

        prog.emit("sync", lambda s: s.nop(),
                  waits=[("dX", c["dX"]), ("dW", c["dW"]), ("dS", c["dS"])])

    with (
        nc.psum_tensor("PS", [128, 4096], F32) as ps,
        nc.semaphore("dW") as s_dW,
        nc.semaphore("dX") as s_dX,
        nc.semaphore("tS") as s_tS,
        nc.semaphore("vS") as s_vS,
        nc.semaphore("sS") as s_sS,
        nc.semaphore("gS") as s_gS,
        nc.semaphore("dI") as s_dI,
        nc.semaphore("dS") as s_dS,
        nc.Block() as block,
    ):
        sems = {"dW": s_dW, "dX": s_dX, "dI": s_dI, "dS": s_dS, "tS": s_tS, "vS": s_vS, "sS": s_sS, "gS": s_gS}
        build_program(ps)
        prog.run(block, sems)
    return nc


_NC_CACHE = None


def kernel(L0_attn_x, L0_attn_Wq, L0_attn_Wk, L0_attn_Wv, L0_attn_Wo, L0_attn_bo,
           L0_attn_q_scale, L0_attn_k_scale, L0_ffn_router_w, L0_ffn_w1, L0_ffn_w2):
    x = np.ascontiguousarray(np.asarray(L0_attn_x, dtype=np.float32))
    Wq = np.asarray(L0_attn_Wq, dtype=np.float32)
    Wk = np.asarray(L0_attn_Wk, dtype=np.float32)
    Wv = np.asarray(L0_attn_Wv, dtype=np.float32)
    Wo = np.asarray(L0_attn_Wo, dtype=np.float32)
    bo = np.ascontiguousarray(np.asarray(L0_attn_bo, dtype=np.float32).reshape(1, D))
    qsc = np.ascontiguousarray(np.asarray(L0_attn_q_scale, dtype=np.float32).reshape(H, DH))
    ksc = np.ascontiguousarray(np.asarray(L0_attn_k_scale, dtype=np.float32).reshape(H, DH))
    rw = np.asarray(L0_ffn_router_w, dtype=np.float32)
    w1 = np.asarray(L0_ffn_w1, dtype=np.float32)
    w2 = np.asarray(L0_ffn_w2, dtype=np.float32)

    wq_t = np.ascontiguousarray(Wq.T.reshape(NC, 128, D))
    wk_t = np.ascontiguousarray(Wk.T.reshape(NC, 128, D))
    wv_t = np.ascontiguousarray(Wv.T.reshape(NC, 128, D))
    wo_t = np.ascontiguousarray(Wo.T.reshape(NC, 128, D))
    rw_t = np.ascontiguousarray(rw.T.reshape(NC, 128, E))
    w1_t = np.ascontiguousarray(
        w1.transpose(0, 2, 1).reshape(E, NC, 128, 2, 1024).transpose(0, 3, 1, 2, 4))
    w2_t = np.ascontiguousarray(
        w2.transpose(0, 2, 1).reshape(E, NFC, 128, 2, 512).transpose(0, 3, 1, 2, 4))

    global _NC_CACHE
    if _NC_CACHE is None:
        _NC_CACHE = build_nc()
    nc = _NC_CACHE

    shared = {"wq": wq_t, "wk": wk_t, "wv": wv_t, "wo": wo_t, "bo": bo,
              "qsc": qsc, "ksc": ksc, "rw": rw_t, "w1": w1_t, "w2": w2_t}
    in_maps = []
    for core in range(8):
        m = dict(shared)
        m["x"] = np.ascontiguousarray(x[core])
        in_maps.append(m)
    res = run_bass_kernel_spmd(nc, in_maps, core_ids=list(range(8)))
    return np.stack([res.results[core]["y"] for core in range(8)], axis=0)


# version 14
# speedup vs baseline: 2.0507x; 1.0412x over previous
import numpy as np
import concourse.bass as bass
import concourse.mybir as mybir
from concourse.bass_utils import run_bass_kernel_spmd

F32 = mybir.dt.float32
U32 = mybir.dt.uint32
BF16 = mybir.dt.bfloat16
I32 = mybir.dt.int32
AF = mybir.ActivationFunctionType
OP = mybir.AluOpType
AX = mybir.AxisListType

T = 2048
D = 1024
H = 16
DH = 64
E = 8
FF = 2048
NC = D // 128
KC = T // 128
NT16 = T // 128
PAIRS = H // 2
QB = T // 512
NFC = FF // 128
CAP = 640
CAPT = CAP // 128
EPS_LN = 1e-5

ENGINES = ("tensor", "vector", "scalar", "sync", "gpsimd")
SEMS = ("dW", "dX", "dI", "dS", "tS", "vS", "sS", "gS")
SEM_OWNER = {"tS": "tensor", "vS": "vector", "sS": "scalar", "gS": "gpsimd"}


class Prog:
    def __init__(self):
        self.cnt = {s: 0 for s in SEMS}
        self.streams = {e: [] for e in ENGINES}

    def emit(self, eng, fn, waits=(), inc=None, selfw=False):
        self.streams[eng].append((tuple(waits), fn, inc, selfw))
        if inc is not None:
            self.cnt[inc[0]] += inc[1]

    def run(self, block, sems):
        own_sem = {v: k for k, v in SEM_OWNER.items()}

        def make_runner2(name):
            def _run(eng):
                satisfied = {s: 0 for s in SEMS}
                own = own_sem.get(name)
                own_emitted = 0
                for waits, fn, inc, selfw in self.streams[name]:
                    if selfw and own is not None and own_emitted > 0:
                        eng.wait_ge(sems[own], own_emitted)
                        satisfied[own] = max(satisfied[own], own_emitted)
                    for s, v in waits:
                        if v > satisfied[s]:
                            eng.wait_ge(sems[s], v)
                            satisfied[s] = v
                    r = fn(eng)
                    if inc is not None:
                        s, n = inc
                        r.then_inc(sems[s], n)
                        if s == own:
                            own_emitted += n
            return _run

        for name in ENGINES:
            getattr(block, name)(make_runner2(name))


def build_nc(stop_after=None):
    nc = bass.Bass("TRN2", target_bir_lowering=False, detect_race_conditions=False)

    x_d = nc.dram_tensor("x", [T, D], F32, kind="ExternalInput").ap()
    wq_d = nc.dram_tensor("wq", [NC, 128, D], F32, kind="ExternalInput").ap()
    wk_d = nc.dram_tensor("wk", [NC, 128, D], F32, kind="ExternalInput").ap()
    wv_d = nc.dram_tensor("wv", [NC, 128, D], F32, kind="ExternalInput").ap()
    wo_d = nc.dram_tensor("wo", [NC, 128, D], F32, kind="ExternalInput").ap()
    bo_d = nc.dram_tensor("bo", [1, D], F32, kind="ExternalInput").ap()
    qsc_d = nc.dram_tensor("qsc", [H, DH], F32, kind="ExternalInput").ap()
    ksc_d = nc.dram_tensor("ksc", [H, DH], F32, kind="ExternalInput").ap()
    rw_d = nc.dram_tensor("rw", [NC, 128, E], F32, kind="ExternalInput").ap()
    w1_d = nc.dram_tensor("w1", [E, 2, NC, 128, 1024], F32, kind="ExternalInput").ap()
    w2_d = nc.dram_tensor("w2", [E, 2, NFC, 128, 512], F32, kind="ExternalInput").ap()
    y_d = nc.dram_tensor("y", [T, D], F32, kind="ExternalOutput").ap()
    acc_d = nc.dram_tensor("acc_scr", [T + 1, D], F32).ap()
    w1bf_d = nc.dram_tensor("w1bf_scr", [E, 128, 2 * NC * 1024], BF16).ap()
    w2bf_d = nc.dram_tensor("w2bf_scr", [E, 128, 2 * NFC * 512], BF16).ap()
    xg_d = nc.dram_tensor("xg_scr", [E * CAP + 1, 1028], BF16).ap()

    base = (nc.sbuf_base + 31) & ~31

    def at(name, shape, dtype, off):
        return nc.alloc_sbuf_tensor_at(name, shape, dtype, offset=off)

    REG_A = base
    REG_B = base + 65536
    REG_C = REG_B + 32768
    REG_D = REG_C + 32768
    REG_E = REG_D + 32768
    SM = REG_E + 32768

    qfm = at("qfm", [128, PAIRS, T], BF16, REG_A)
    kfm = at("kfm", [128, PAIRS, T], BF16, REG_A + 32768)
    out_acc = at("out_acc", [128, NT16, D], F32, REG_A)

    xg_sb = at("xg_sb", [128, CAPT, 1028], BF16, REG_A)
    x1g_fm = at("x1g_fm", [128, NC, CAP], BF16, REG_A + 12352)
    y_sb = at("y_sb", [128, CAPT, D], F32, REG_A + 24640)
    x1st = at("x1st", [128, 2, 1024], BF16, REG_B)
    eConst = at("eConst", [128, NT16, E], F32, REG_B + 4096)
    eCm8 = at("eCm8", [128, NT16, E], F32, REG_B + 4608)
    padA = at("padA", [128, 32, E], F32, REG_B + 5120)
    padB = at("padB", [128, 32, E], F32, REG_B + 6144)
    oh0 = at("oh0", [128, NT16, E], F32, REG_B + 7168)
    oh1 = at("oh1", [128, NT16, E], F32, REG_B + 7680)
    exb = at("exb", [128, NT16, E], F32, REG_B + 8192)
    trstg = at("trstg", [128, 2, 512], F32, REG_A + 49216)
    x0_fm = at("x0_fm", [128, NC, T], BF16, REG_B)
    p_slab = at("p_slab", [128, 2, KC, 512], BF16, REG_B)
    w1ring = at("w1ring", [128, 2, NC, 1024], BF16, REG_B)

    vaug = at("vaug", [128, KC, D], BF16, REG_C)
    rowbuf = at("rowbuf", [128, NT16, 1028], BF16, REG_C)

    wring = at("wring", [128, 2, NC, 1024], BF16, REG_D)
    attn_fm = at("attn_fm", [128, PAIRS, T], BF16, REG_D)
    h_fm = at("h_fm", [128, NFC, CAP], BF16, REG_D + 2048)

    x0stA = at("x0stA", [128, 2, D], F32, REG_E)
    q2bf = at("q2bf", [128, 2, 512], BF16, REG_E + 8192)
    invst = at("invst", [128, 2, 512], F32, REG_E + 10240)
    invraw = at("invraw", [128, 2, 512], F32, REG_E + 14336)
    wot = at("wot", [128, NC, D], BF16, REG_E)
    x0stC = at("x0stC", [128, 2, D], F32, REG_E + 18432)
    BT = REG_E + 26624
    lnmsA = at("lnmsA", [128, NT16], F32, BT)
    lnssA = at("lnssA", [128, NT16], F32, BT + 64)
    muA = at("muA", [128, NT16], F32, BT + 128)
    invA = at("invA", [128, NT16], F32, BT + 192)
    t1A = at("t1A", [128, NT16], F32, BT + 256)
    t2A = at("t2A", [128, NT16], F32, BT + 320)
    lnvA = at("lnvA", [128, NT16], F32, BT + 384)
    m1A = at("m1A", [128, NT16], F32, BT + 448)
    c1A = at("c1A", [128, NT16], F32, BT + 512)
    m2mA = at("m2mA", [128, NT16], F32, BT + 576)
    indA = at("indA", [128, NT16], F32, BT + 640)
    d12A = at("d12A", [128, NT16], F32, BT + 704)
    m2A = at("m2A", [128, NT16], F32, BT + 768)
    e2xA = at("e2xA", [128, NT16], F32, BT + 832)
    rA = at("rA", [128, NT16], F32, BT + 896)
    r2A = at("r2A", [128, NT16], F32, BT + 960)
    lg_all = at("lg_all", [128, NT16, E], F32, BT + 1024)
    gbA = at("gbA", [128, NT16, E], F32, BT + 1536)
    gbB = at("gbB", [128, NT16, E], F32, BT + 2048)
    gbC = at("gbC", [128, NT16, E], F32, BT + 2560)
    bo_row = at("bo_row", [1, D], BF16, BT + 3072)
    w2res = at("w2res", [128, 2, NFC, 512], BF16, REG_E)
    outst = at("outst", [128, 2, D], F32, REG_E)
    eln_in = at("eln_in", [128, 2, D], F32, REG_E + 8192)

    sm = SM
    smalls = {}

    def sb(name, shape, dtype):
        nonlocal sm
        t = at(name, shape, dtype, sm)
        n = 1
        for s in shape[1:]:
            n *= s
        sm += (n * (2 if dtype == BF16 else 4) + 31) & ~31
        smalls[name] = t
        return t

    rec_f = sb("rec_f", [128, 512], F32)
    rec_bf = sb("rec_bf", [128, 512], BF16)
    rb_sb = sb("rb_sb", [128, 512], F32)
    gates = sb("gates", [128, NT16, E], F32)
    ident_f32 = sb("ident_f32", [128, 128], F32)
    ident_bf = sb("ident_bf", [128, 128], BF16)
    iot_a = sb("iot_a", [128, 128], I32)
    iot_b = sb("iot_b", [128, 128], I32)
    indB = sb("indB", [128, 128], BF16)
    ones_bf = sb("ones_bf", [128, 128], BF16)
    sq_scr = sb("sq_scr", [128, D], BF16)
    qsc_sb = sb("qsc_sb", [16, DH], F32)
    ksc_sb = sb("ksc_sb", [16, DH], F32)
    qs_col = sb("qs_col", [128, NC], F32)
    ks_col = sb("ks_col", [128, NC], F32)
    mrow = sb("mrow", [1, 16], F32)
    mb_sb = sb("mb_sb", [128, 16], F32)
    negMp = sb("negMp", [128, PAIRS], F32)
    rwt = sb("rwt", [128, NC, E], BF16)
    lg_sb = sb("lg_sb", [128, E], F32)
    g_eq = sb("g_eq", [128, E], F32)
    g_lm = sb("g_lm", [128, E], F32)
    g_et = sb("g_et", [128, E], F32)
    g_ge2 = sb("g_ge2", [128, E], F32)
    ln_ms = sb("ln_ms", [128, 1], F32)
    ln_ss = sb("ln_ss", [128, 1], F32)
    ln_mu = sb("ln_mu", [128, 1], F32)
    ln_t1 = sb("ln_t1", [128, 1], F32)
    ln_t2 = sb("ln_t2", [128, 1], F32)
    ln_inv = sb("ln_inv", [128, 1], F32)
    g_m1 = sb("g_m1", [128, 1], F32)
    g_c1 = sb("g_c1", [128, 1], F32)
    g_m2m = sb("g_m2m", [128, 1], F32)
    g_ind = sb("g_ind", [128, 1], F32)
    g_d12 = sb("g_d12", [128, 1], F32)
    g_m2 = sb("g_m2", [128, 1], F32)
    g_nm1 = sb("g_nm1", [128, 1], F32)
    g_e2 = sb("g_e2", [128, 1], F32)
    g_e2x = sb("g_e2x", [128, 1], F32)
    g_r2 = sb("g_r2", [128, 1], F32)
    ln_t3 = sb("ln_t3", [128, 1], F32)
    eps_l2 = sb("eps_l2", [128, 1], F32)
    zline = sb("zline", [128, 1028], BF16)
    tid_i = sb("tid_i", [128, NT16], I32)
    a_i = sb("a_i", [128, NT16], I32)
    a_bf = sb("a_bf", [128, NT16], BF16)
    b_bf = sb("b_bf", [128, NT16], BF16)
    dec_a = sb("dec_a", [128, 2, CAPT], F32)
    dec_b = sb("dec_b", [128, 2, CAPT], F32)
    triU = sb("triU", [128, 128], F32)
    ones_f32x = sb("ones_f32x", [128, 128], F32)
    tid1 = sb("tid1", [128, NT16], F32)
    e0f = sb("e0f", [128, NT16], F32)
    e1f = sb("e1f", [128, NT16], F32)
    g1A = sb("g1A", [128, NT16], F32)
    s0A = sb("s0A", [128, NT16], F32)
    slotf = sb("slotf", [128, NT16], F32)
    slotu = sb("slotu", [128, 2, NT16], U32)
    tot_sb = sb("tot_sb", [128, E], F32)
    pp_sb = sb("pp_sb", [128, E], F32)
    g0b_sb = sb("g0b_sb", [128, E], F32)
    base_sb = sb("base_sb", [128, E], F32)
    dec_t = sb("dec_t", [128, 2, CAPT], F32)
    dec_g = sb("dec_g", [128, 2, CAPT], F32)
    dec_u = sb("dec_u", [128, 2, CAPT], U32)
    g_r = sb("g_r", [128, 1], F32)
    qmax = sb("qmax", [16, 1], F32)
    kmax = sb("kmax", [16, 1], F32)
    m16 = sb("m16", [16, 1], F32)
    assert sm <= nc.sbuf_top, (sm, nc.sbuf_top)

    prog = Prog()
    c = prog.cnt

    milestones = {}

    def build_program(ps, stop_after=stop_after):
        def PSb(bank, lo=0, hi=512):
            return ps[:, bank * 512 + lo: bank * 512 + hi]

        def dump_stop(src_handle, nelem, waits):
            flat = src_handle.rearrange("p ... -> p (...)") if False else None
            out_ap = bass.AP(y_d.tensor, 0, [[16384, 128], [1, nelem]])
            in_ap = bass.AP(src_handle, 0, [[nelem, 128], [1, nelem]])
            prog.emit("gpsimd", lambda g: g.dma_start(out=out_ap, in_=in_ap,
                      max_dma_last_dim=32768), waits=waits, inc=("dW", 16))
            prog.emit("sync", lambda s: s.nop(), waits=[("dW", c["dW"])])

        prog.emit("gpsimd", lambda g: g.dma_start(out=qsc_sb[:, :], in_=qsc_d[:, :]), inc=("dW", 16))
        dw_qsc = c["dW"]
        prog.emit("gpsimd", lambda g: g.dma_start(out=ksc_sb[:, :], in_=ksc_d[:, :]), inc=("dW", 16))
        dw_ksc = c["dW"]
        prog.emit("gpsimd", lambda g: g.dma_start(
            out=qs_col[:, :], in_=bass.AP(qsc_d.tensor, 0, [[1, 128], [128, NC]]),
            allow_slow_non_contiguous=True), inc=("dW", 16))
        dw_qcol = c["dW"]
        prog.emit("gpsimd", lambda g: g.dma_start(
            out=ks_col[:, :], in_=bass.AP(ksc_d.tensor, 0, [[1, 128], [128, NC]]),
            allow_slow_non_contiguous=True), inc=("dW", 16))
        dw_kcol = c["dW"]
        prog.emit("gpsimd", lambda g: g.dma_start(out=rwt[:, :, :], in_=rw_d.transpose([1, 0, 2])), inc=("dW", 16))
        dw_rwt = c["dW"]
        prog.emit("gpsimd", lambda g: g.dma_start(out=bo_row[:, :], in_=bo_d[:, :]), inc=("dW", 16))
        dw_bo = c["dW"]
        prog.emit("gpsimd", lambda g: g.dma_start(out=wring[:, 0, :, :], in_=wq_d.transpose([1, 0, 2])), inc=("dW", 16))
        dw_wq = c["dW"]
        prog.emit("gpsimd", lambda g: g.dma_start(out=wring[:, 1, :, :], in_=wk_d.transpose([1, 0, 2])), inc=("dW", 16))
        dw_wk = c["dW"]

        prog.emit("gpsimd", lambda g: g.iota(iot_a[:, :], [[-1, 128]], base=0, channel_multiplier=1),
                  inc=("gS", 1))
        g_i1 = c["gS"]
        prog.emit("vector", lambda v: v.tensor_scalar(out=ident_f32[:, :], in0=iot_a[:, :],
                                                      scalar1=0, scalar2=None, op0=OP.is_equal),
                  waits=[("gS", g_i1)], inc=("vS", 1))
        prog.emit("vector", lambda v: v.tensor_scalar(out=ident_bf[:, :], in0=iot_a[:, :],
                                                      scalar1=0, scalar2=None, op0=OP.is_equal),
                  inc=("vS", 1))
        prog.emit("vector", lambda v: v.tensor_scalar(out=triU[:, :], in0=iot_a[:, :],
                                                      scalar1=0, scalar2=None, op0=OP.is_lt),
                  inc=("vS", 1))
        v_id = c["vS"]
        prog.emit("gpsimd", lambda g: g.iota(iot_a[:, :], [[0, 128]], base=0, channel_multiplier=1),
                  waits=[("vS", v_id)], inc=("gS", 1))
        prog.emit("gpsimd", lambda g: g.iota(iot_b[:, :], [[1, 128]], base=0, channel_multiplier=0),
                  inc=("gS", 1))
        g_i2 = c["gS"]
        prog.emit("vector", lambda v: v.tensor_scalar(out=iot_a[:, :], in0=iot_a[:, :], scalar1=6,
                                                      scalar2=None, op0=OP.arith_shift_right),
                  waits=[("gS", g_i2)], inc=("vS", 1))
        prog.emit("vector", lambda v: v.tensor_scalar(out=iot_b[:, :], in0=iot_b[:, :], scalar1=6,
                                                      scalar2=None, op0=OP.arith_shift_right), inc=("vS", 1))
        prog.emit("vector", lambda v: v.tensor_tensor(out=indB[:, :], in0=iot_a[:, :], in1=iot_b[:, :],
                                                      op=OP.is_equal), selfw=True, inc=("vS", 1))
        prog.emit("vector", lambda v: v.memset(ones_bf[:, :], 1.0), inc=("vS", 1))
        prog.emit("vector", lambda v: v.memset(eps_l2[:, :], 1e-18), inc=("vS", 1))
        prog.emit("vector", lambda v: v.memset(ones_f32x[:, :], 1.0), inc=("vS", 1))
        prog.emit("vector", lambda v: v.memset(zline[:, :], 0.0), inc=("vS", 1))
        v_zline = c["vS"]
        prog.emit("gpsimd", lambda g: g.iota(tid1[:, :], [[128, NT16]], base=1, channel_multiplier=1,
                                             allow_small_or_imprecise_dtypes=True), inc=("gS", 1))
        prog.emit("gpsimd", lambda g: g.iota(tid_i[:, :], [[128, NT16]], base=1, channel_multiplier=1),
                  inc=("gS", 1))
        g_tid = c["gS"]
        prog.emit("vector", lambda v: v.tensor_scalar(out=a_i[:, :], in0=tid_i[:, :], scalar1=6,
                                                      scalar2=None, op0=OP.arith_shift_right),
                  waits=[("gS", g_tid)], inc=("vS", 1))
        prog.emit("vector", lambda v: v.tensor_copy(a_bf[:, :], a_i[:, :]), selfw=True, inc=("vS", 1))
        prog.emit("vector", lambda v: v.tensor_scalar(out=a_i[:, :], in0=a_i[:, :], scalar1=-64,
                                                      scalar2=None, op0=OP.mult), selfw=True, inc=("vS", 1))
        prog.emit("vector", lambda v: v.tensor_tensor(out=a_i[:, :], in0=tid_i[:, :], in1=a_i[:, :],
                                                      op=OP.add), selfw=True, inc=("vS", 1))
        prog.emit("vector", lambda v: v.tensor_copy(b_bf[:, :], a_i[:, :]), selfw=True, inc=("vS", 1))
        v_idsplit = c["vS"]
        prog.emit("gpsimd", lambda g: g.dma_start(
            out=acc_d[T:T + 1, :], in_=bass.AP(zline, 0, [[1026, 1], [1, D]])),
            waits=[("vS", v_zline)], inc=("dW", 16))
        dw_ztrash = c["dW"]
        prog.emit("vector", lambda v: v.reduce_max(qmax[:, :], qsc_sb[:, :], axis=AX.X,
                                                   apply_absolute_value=True),
                  waits=[("dW", dw_qsc)], inc=("vS", 1))
        prog.emit("vector", lambda v: v.reduce_max(kmax[:, :], ksc_sb[:, :], axis=AX.X,
                                                   apply_absolute_value=True),
                  waits=[("dW", dw_ksc)], inc=("vS", 1))
        prog.emit("vector", lambda v: v.tensor_tensor(out=m16[:, :], in0=qmax[:, :], in1=kmax[:, :],
                                                      op=OP.mult), selfw=True, inc=("vS", 1))
        v_m16 = c["vS"]
        prog.emit("gpsimd", lambda g: g.dma_start(
            out=bass.AP(mrow, 0, [[16, 1], [1, 16]]),
            in_=bass.AP(m16, 0, [[1, 16], [1, 1]]),
            allow_slow_non_contiguous=True),
            waits=[("vS", v_m16)], inc=("dW", 16))
        dw_mrow = c["dW"]
        prog.emit("vector", lambda v: v.tensor_copy(rec_bf[0:1, 0:16], mrow[0:1, :]),
                  waits=[("dW", dw_mrow)], inc=("vS", 1))
        v_mbf = c["vS"]
        prog.emit("tensor", lambda t: t.matmul(ps[:, 3616:3632][0:64, :16], ones_bf[0:1, 0:64],
                                               rec_bf[0:1, 0:16], start=True, stop=True),
                  waits=[("vS", v_mbf)], inc=("tS", 1))
        prog.emit("tensor", lambda t: t.matmul(ps[:, 3616:3632][64:128, :16], ones_bf[0:1, 0:64],
                                               rec_bf[0:1, 0:16], start=True, stop=True,
                                               tile_position=(0, 64)),
                  inc=("tS", 1))
        t_mb = c["tS"]
        prog.emit("vector", lambda v: v.tensor_copy(mb_sb[:, :], ps[:, 3616:3632]),
                  waits=[("tS", t_mb)], inc=("vS", 1))
        prog.emit("vector", lambda v: v.tensor_tensor(out=negMp[:, :], in0=mb_sb[:, 0:16:2],
                                                      in1=mb_sb[:, 1:16:2], op=OP.max), selfw=True, inc=("vS", 1))
        prog.emit("vector", lambda v: v.tensor_scalar(out=negMp[:, :], in0=negMp[:, :],
                                                      scalar1=-0.125, scalar2=None, op0=OP.mult),
                  selfw=True, inc=("vS", 1))
        v_setup = c["vS"]

        x0fm_done = [0] * NT16
        tr_evac = [0, 0]
        dx_tile = [0] * NT16

        def emit_x0_load(t16):
            par = t16 % 2
            waits = []
            if t16 >= 2:
                waits.append(("vS", x0fm_done[t16 - 2]))
            prog.emit("sync", lambda s, t16=t16, par=par: s.dma_start(
                out=x0stA[:, par, :], in_=x_d[t16 * 128:(t16 + 1) * 128, :]),
                waits=waits, inc=("dX", 16))
            dx_tile[t16] = c["dX"]

        emit_x0_load(0)
        emit_x0_load(1)
        for t16 in range(NT16):
            par = t16 % 2
            t_tr4 = [0, 0]
            for half in range(2):
                bank = 4 + half
                waits = [("dX", dx_tile[t16]), ("vS", v_id)]
                if t16 >= 1:
                    waits.append(("vS", tr_evac[half]))
                for j in range(4):
                    cc = half * 4 + j
                    prog.emit("tensor", lambda t, par=par, cc=cc, bank=bank, j=j: t.transpose(
                        PSb(bank, j * 128, (j + 1) * 128),
                        x0stA[:, par, cc * 128:(cc + 1) * 128], ident_f32[:, :]),
                        waits=waits if j == 0 else [], inc=("tS", 1) if j == 3 else None)
                t_tr4[half] = c["tS"]
            for half in range(2):
                bank = 4 + half
                prog.emit("vector", lambda v, t16=t16, half=half, bank=bank: v.tensor_copy(
                    x0_fm[:, half * 4:(half + 1) * 4, t16 * 128:(t16 + 1) * 128], PSb(bank)),
                    waits=[("tS", t_tr4[half])], inc=("vS", 1))
                tr_evac[half] = c["vS"]
            x0fm_done[t16] = c["vS"]
            if t16 + 2 < NT16:
                emit_x0_load(t16 + 2)
        v_x0fm = c["vS"]
        for blk in range(E * CAPT):
            if blk % 2 == 0:
                prog.emit("sync", lambda s, blk=blk: s.dma_start(
                    out=xg_d[blk * 128:(blk + 1) * 128, :], in_=zline[:, :]),
                    waits=[("vS", v_zline)] if blk == 0 else [], inc=("dX", 16))
            else:
                prog.emit("scalar", lambda s, blk=blk: s.dma_start(
                    out=xg_d[blk * 128:(blk + 1) * 128, :], in_=zline[:, :]),
                    waits=[("vS", v_zline)] if blk == 1 else [], inc=("dS", 16))
        dx_zero_done = c["dX"]
        ds_zero_done = c["dS"]
        if stop_after == "x0fm":
            return dump_stop(x0_fm, NC * T, [("vS", v_x0fm)])

        NU = 2 * NC * 2
        grp_done = [0] * NU
        sq_done = [[0, 0] for _ in range(NU)]
        ssmm_done = [[0, 0] for _ in range(NU)]
        exp_done = [[0, 0] for _ in range(NU)]
        stt_done = [[0, 0] for _ in range(NU)]

        def unit_info(u):
            proj = u // 16
            co = (u % 16) // 2
            half = u % 2
            wslot = 0 if proj == 0 else 1
            dst = qfm if proj == 0 else kfm
            scol = qs_col if proj == 0 else ks_col
            dw_w = dw_wq if proj == 0 else dw_wk
            dw_s = dw_qcol if proj == 0 else dw_kcol
            return proj, co, half, wslot, dst, scol, dw_w, dw_s

        def emit_ssmm(u):
            for loc in range(2):
                waits = [("sS", sq_done[u][loc])]
                if u == 0:
                    waits.append(("vS", v_setup))
                if u >= 1:
                    waits.append(("sS", exp_done[u - 1][loc]))
                prog.emit("tensor", lambda t, u=u, loc=loc: t.matmul(
                    PSb(6 + loc), indB[:, :], q2bf[:, loc, :], start=True, stop=True),
                    waits=waits, inc=("tS", 1))
                ssmm_done[u][loc] = c["tS"]

        def emit_grp(u):
            _, co, half, wslot, dst, scol, dw_w, dw_s = unit_info(u)
            b0 = (u % 3) * 2
            for dc in range(NC):
                waits = []
                if dc == 0:
                    waits = [("dW", dw_w), ("vS", v_x0fm)]
                    if u >= 3:
                        waits.append(("vS", stt_done[u - 3][1]))
                for loc in range(2):
                    t5 = half * 2 + loc
                    prog.emit("tensor", lambda t, co=co, dc=dc, t5=t5, wslot=wslot, b=b0 + loc: t.matmul(
                        PSb(b), wring[:, wslot, dc, co * 128:(co + 1) * 128],
                        x0_fm[:, dc, t5 * 512:(t5 + 1) * 512],
                        start=(dc == 0), stop=(dc == NC - 1)),
                        waits=waits if loc == 0 else [],
                        inc=("tS", 1) if (dc == NC - 1 and loc == 1) else None)
            grp_done[u] = c["tS"]

        def emit_sq(u):
            b0 = (u % 3) * 2
            for loc in range(2):
                waits = [("tS", grp_done[u])]
                if u >= 1:
                    waits.append(("tS", ssmm_done[u - 1][loc]))
                prog.emit("scalar", lambda s, loc=loc, b=b0 + loc: s.activation(
                    q2bf[:, loc, :], PSb(b), AF.Square),
                    waits=waits, inc=("sS", 1))
                sq_done[u][loc] = c["sS"]

        def emit_lnexp(u):
            for loc in range(2):
                waits = [("tS", ssmm_done[u][loc])]
                prog.emit("scalar", lambda s, loc=loc: s.activation(
                    invraw[:, loc, :], PSb(6 + loc), AF.Ln, bias=eps_l2[:, 0:1]),
                    waits=waits, inc=("sS", 1))
                w2_ = [("vS", stt_done[u - 1][loc])] if u >= 1 else []
                prog.emit("scalar", lambda s, loc=loc: s.activation(
                    invst[:, loc, :], invraw[:, loc, :], AF.Exp, scale=-0.5),
                    waits=w2_, selfw=True, inc=("sS", 1))
                exp_done[u][loc] = c["sS"]

        def emit_stt(u):
            _, co, half, wslot, dst, scol, dw_w, dw_s = unit_info(u)
            b0 = (u % 3) * 2
            for loc in range(2):
                t5 = half * 2 + loc
                prog.emit("vector", lambda v, co=co, t5=t5, loc=loc, b=b0 + loc, dst=dst, scol=scol: v.scalar_tensor_tensor(
                    out=dst[:, co, t5 * 512:(t5 + 1) * 512], in0=PSb(b),
                    scalar=scol[:, co:co + 1], in1=invst[:, loc, :],
                    op0=OP.mult, op1=OP.mult),
                    waits=[("dW", dw_s), ("sS", exp_done[u][loc])], inc=("vS", 1))
                stt_done[u][loc] = c["vS"]

        for u in range(NU + 2):
            if u >= 2:
                emit_ssmm(u - 2)
            if u < NU:
                emit_grp(u)
            if 1 <= u <= NU:
                emit_sq(u - 1)
            if u >= 2:
                emit_lnexp(u - 2)
                emit_stt(u - 2)
        v_q = stt_done[15][1]
        v_k = stt_done[NU - 1][1]
        if stop_after == "qfm":
            return dump_stop(qfm, PAIRS * T, [("vS", v_q)])
        prog.emit("gpsimd", lambda g: g.dma_start(out=wring[:, 0, :, :], in_=wv_d.transpose([1, 0, 2])),
                  waits=[("vS", v_q)], inc=("dW", 16))
        dw_wv = c["dW"]
        if stop_after == "kfm":
            return dump_stop(kfm, PAIRS * T, [("vS", v_k)])

        psv_evac = [0, 0]
        for t16 in range(NT16):
            pp = t16 % 2
            banks = (2 * pp, 2 * pp + 1)
            for dc in range(NC):
                waits = []
                if dc == 0:
                    waits = [("dW", dw_wv), ("vS", v_k)]
                    if t16 >= 2:
                        waits.append(("vS", psv_evac[pp]))
                for eh in range(2):
                    prog.emit("tensor", lambda t, t16=t16, dc=dc, eh=eh, banks=banks: t.matmul(
                        PSb(banks[eh]), x0_fm[:, dc, t16 * 128:(t16 + 1) * 128],
                        wring[:, 0, dc, eh * 512:(eh + 1) * 512],
                        start=(dc == 0), stop=(dc == NC - 1)),
                        waits=waits if eh == 0 else [],
                        inc=("tS", 1) if (dc == NC - 1 and eh == 1) else None)
            t_v = c["tS"]
            for eh in range(2):
                prog.emit("vector", lambda v, t16=t16, eh=eh, banks=banks: v.tensor_copy(
                    vaug[:, t16, eh * 512:(eh + 1) * 512], PSb(banks[eh])),
                    waits=[("tS", t_v)], inc=("vS", 1))
            psv_evac[pp] = c["vS"]
        v_vaug = c["vS"]
        if stop_after == "vaug":
            return dump_stop(vaug, KC * D, [("vS", v_vaug)])

        prog.emit("gpsimd", lambda g: g.dma_start(out=wot[:, :, :], in_=wo_d.transpose([1, 0, 2])),
                  waits=[("vS", v_vaug)], inc=("dW", 16))
        dw_wot = c["dW"]

        dw_cast_w1 = [[0, 0] for _ in range(E)]
        dw_cast_w2 = [0] * E
        for e in range(E):
            for fh in range(2):
                prog.emit("gpsimd", lambda g, e=e, fh=fh: g.dma_start(
                    out=bass.AP(w1bf_d.tensor, e * 128 * 2 * NC * 1024 + fh * NC * 1024,
                                [[2 * NC * 1024, 128], [1, NC * 1024]]),
                    in_=w1_d[e, fh].transpose([1, 0, 2])), inc=("dW", 16))
                dw_cast_w1[e][fh] = c["dW"]
            prog.emit("gpsimd", lambda g, e=e: g.dma_start(
                out=w2bf_d[e],
                in_=bass.AP(w2_d.tensor, e * 2 * NFC * 128 * 512,
                            [[512, 128], [NFC * 128 * 512, 2], [128 * 512, NFC], [1, 512]])),
                inc=("dW", 16))
            dw_cast_w2[e] = c["dW"]

        units = [(pr, qb) for pr in range(PAIRS) for qb in range(QB)]
        NU = len(units)
        prev_mult = [0, 0]
        prev_rec_par = [0, 0]
        av_last_final = 0
        ppb = [(0, 1), (2, 3)]
        qk_done = {}
        exp_done = {}

        def emit_qk(u, kc, extra):
            pr, qb = units[u]
            b0, b1 = ppb[kc % 2]
            prog.emit("tensor", lambda t, pr=pr, qb=qb, kc=kc, b0=b0: t.matmul(
                PSb(b0), kfm[0:64, pr, kc * 128:(kc + 1) * 128],
                qfm[0:64, pr, qb * 512:(qb + 1) * 512], start=True, stop=True),
                waits=extra)
            prog.emit("tensor", lambda t, pr=pr, qb=qb, kc=kc, b1=b1: t.matmul(
                PSb(b1), kfm[64:128, pr, kc * 128:(kc + 1) * 128],
                qfm[64:128, pr, qb * 512:(qb + 1) * 512], start=True, stop=True),
                inc=("tS", 1))
            qk_done[(u, kc)] = c["tS"]

        emit_qk(0, 0, [("vS", v_vaug)])
        emit_qk(0, 1, [])
        for it in range(NU):
            pr, qb = units[it]
            avb = 4 + (it % 2)
            DENB = 6 + (it % 2)
            for kc in range(KC):
                b0 = ppb[kc % 2][0]
                prog.emit("scalar", lambda s, pr=pr, kc=kc, b0=b0: s.activation(
                    p_slab[:, :, kc, :], ps[:, b0 * 512:(b0 + 2) * 512], AF.Exp,
                    bias=negMp[:, pr:pr + 1], scale=0.125),
                    waits=[("tS", qk_done[(it, kc)])], inc=("sS", 1))
                exp_done[(it, kc)] = c["sS"]
                waits = [("sS", exp_done[(it, kc)])]
                if kc == 0 and it >= 2:
                    waits.append(("vS", prev_mult[it % 2]))
                    waits.append(("vS", prev_rec_par[it % 2]))
                prog.emit("tensor", lambda t, pr=pr, kc=kc, avb=avb: t.matmul(
                    PSb(avb)[0:64, :], vaug[:, kc, (2 * pr) * 64:(2 * pr + 1) * 64],
                    p_slab[:, 0, kc, :], start=(kc == 0), stop=(kc == KC - 1)),
                    waits=waits)
                prog.emit("tensor", lambda t, pr=pr, kc=kc, avb=avb: t.matmul(
                    PSb(avb)[64:128, :], vaug[:, kc, (2 * pr + 1) * 64:(2 * pr + 2) * 64],
                    p_slab[:, 1, kc, :], start=(kc == 0), stop=(kc == KC - 1),
                    tile_position=(0, 64)))
                prog.emit("tensor", lambda t, kc=kc, DENB=DENB: t.matmul(
                    PSb(DENB)[0:64, :], ones_bf[:, 0:64], p_slab[:, 0, kc, :],
                    start=(kc == 0), stop=(kc == KC - 1), tile_position=(0, 0)))
                prog.emit("tensor", lambda t, kc=kc, DENB=DENB: t.matmul(
                    PSb(DENB)[64:128, :], ones_bf[:, 0:64], p_slab[:, 1, kc, :],
                    start=(kc == 0), stop=(kc == KC - 1), tile_position=(0, 64)),
                    inc=("tS", 1))
                if kc == KC - 1:
                    av_last = c["tS"]
                if kc + 2 < KC:
                    emit_qk(it, kc + 2, [])
                elif it + 1 < NU:
                    emit_qk(it + 1, kc + 2 - KC, [("sS", exp_done[(it, kc)])])

            prog.emit("vector", lambda v, DENB=DENB: v.reciprocal(rb_sb[:, :], PSb(DENB)),
                      waits=[("tS", av_last)], inc=("vS", 1))
            v_rec = c["vS"]
            prog.emit("vector", lambda v, pr=pr, qb=qb, avb=avb: v.tensor_tensor(
                out=attn_fm[:, pr, qb * 512:(qb + 1) * 512], in0=PSb(avb),
                in1=rb_sb[:, :], op=OP.mult), selfw=True, inc=("vS", 1))
            prev_mult[it % 2] = c["vS"]
            prev_rec_par[it % 2] = v_rec
            av_last_final = av_last
        v_attn = c["vS"]
        t_pslab_free = av_last_final
        if stop_after == "attn":
            return dump_stop(attn_fm, PAIRS * T, [("vS", v_attn)])

        dxc_tile = [0] * NT16
        pre_done = [0, 0]

        def emit_xc_load(t16):
            par = t16 % 2
            waits = [("tS", t_pslab_free)]
            if t16 >= 2:
                waits.append(("vS", pre_done[par]))
            prog.emit("sync", lambda s, t16=t16, par=par: s.dma_start(
                out=x0stC[:, par, :], in_=x_d[t16 * 128:(t16 + 1) * 128, :]),
                waits=waits, inc=("dX", 16))
            dxc_tile[t16] = c["dX"]

        emit_xc_load(0)
        emit_xc_load(1)
        s_stats = [0] * NT16
        for t16 in range(NT16):
            par = t16 % 2
            wob = (0, 1) if par == 0 else (2, 3)
            for dh2 in range(2):
                for cc in range(NC):
                    waits = []
                    if dh2 == 0 and cc == 0:
                        waits = [("dW", dw_wot), ("vS", v_attn)]
                        if t16 >= 2:
                            waits.append(("vS", pre_done[par]))
                    prog.emit("tensor", lambda t, t16=t16, dh2=dh2, cc=cc, wob=wob: t.matmul(
                        PSb(wob[dh2]), attn_fm[:, cc, t16 * 128:(t16 + 1) * 128],
                        wot[:, cc, dh2 * 512:(dh2 + 1) * 512],
                        start=(cc == 0), stop=False),
                        waits=waits)
                prog.emit("tensor", lambda t, dh2=dh2, wob=wob: t.matmul(
                    PSb(wob[dh2]), ones_bf[0:1, 0:128],
                    bo_row[0:1, dh2 * 512:(dh2 + 1) * 512],
                    start=False, stop=True),
                    waits=[("dW", dw_bo)] if dh2 == 0 else [],
                    inc=("tS", 1) if dh2 == 1 else None)
            t_wo = c["tS"]
            prog.emit("vector", lambda v, t16=t16, par=par, wob=wob: v.tensor_tensor(
                out=out_acc[:, t16, :], in0=ps[:, wob[0] * 512:(wob[0] + 2) * 512],
                in1=x0stC[:, par, :], op=OP.add),
                waits=[("tS", t_wo), ("dX", dxc_tile[t16])], inc=("vS", 1))
            v_pre = c["vS"]
            pre_done[par] = v_pre
            prog.emit("scalar", lambda s, t16=t16: s.activation(
                sq_scr[:, :], out_acc[:, t16, :], AF.Copy, accum_out=lnmsA[:, t16:t16 + 1]),
                waits=[("vS", v_pre)], inc=("sS", 1))
            prog.emit("scalar", lambda s, t16=t16: s.activation(
                sq_scr[:, :], out_acc[:, t16, :], AF.Square, accum_out=lnssA[:, t16:t16 + 1]),
                inc=("sS", 1))
            s_stats[t16] = c["sS"]
            if t16 + 2 < NT16:
                emit_xc_load(t16 + 2)
        t_wo15 = c["tS"]

        prog.emit("vector", lambda v: v.tensor_scalar(out=muA[:, :], in0=lnmsA[:, :],
                                                      scalar1=1.0 / D, scalar2=None, op0=OP.mult),
                  waits=[("sS", s_stats[NT16 - 1])], inc=("vS", 1))
        v_mu = c["vS"]
        prog.emit("scalar", lambda s: s.activation(t1A[:, :], muA[:, :], AF.Square),
                  waits=[("vS", v_mu)], selfw=True, inc=("sS", 1))
        s_musq = c["sS"]
        prog.emit("vector", lambda v: v.tensor_scalar(out=t2A[:, :], in0=lnssA[:, :],
                                                      scalar1=1.0 / D, scalar2=EPS_LN,
                                                      op0=OP.mult, op1=OP.add), inc=("vS", 1))
        prog.emit("vector", lambda v: v.tensor_tensor(out=t2A[:, :], in0=t2A[:, :],
                                                      in1=t1A[:, :], op=OP.subtract),
                  waits=[("sS", s_musq)], selfw=True, inc=("vS", 1))
        v_ln = c["vS"]
        prog.emit("scalar", lambda s: s.activation(lnvA[:, :], t2A[:, :], AF.Ln),
                  waits=[("vS", v_ln)], selfw=True, inc=("sS", 1))
        prog.emit("scalar", lambda s: s.activation(invA[:, :], lnvA[:, :], AF.Exp, scale=-0.5),
                  selfw=True, inc=("sS", 1))
        s_inv = c["sS"]

        x1_done = [0] * NT16
        tr_evac = [0, 0]
        t_lg_all = [0] * NT16
        s_row = [0] * NT16
        dx_accst = [0] * NT16
        for t16 in range(NT16):
            prog.emit("vector", lambda v, t16=t16: v.tensor_scalar(
                out=out_acc[:, t16, :], in0=out_acc[:, t16, :], scalar1=muA[:, t16:t16 + 1],
                scalar2=invA[:, t16:t16 + 1], op0=OP.subtract, op1=OP.mult),
                waits=[("sS", s_inv)], inc=("vS", 1))
            x1_done[t16] = c["vS"]
            prog.emit("scalar", lambda s, t16=t16: s.activation(
                rowbuf[:, t16, 0:1024], out_acc[:, t16, :], AF.Copy),
                waits=[("vS", x1_done[t16])], inc=("sS", 1))
            s_row[t16] = c["sS"]
            if t16 == 0:
                prog.emit("vector", lambda v: v.tensor_copy(
                    bass.AP(rowbuf, 1026, [[NT16 * 1028, 128], [1028, NT16], [1, 1]]),
                    a_bf[:, :]), waits=[("vS", v_idsplit)], inc=("vS", 1))
                prog.emit("vector", lambda v: v.tensor_copy(
                    bass.AP(rowbuf, 1027, [[NT16 * 1028, 128], [1028, NT16], [1, 1]]),
                    b_bf[:, :]), inc=("vS", 1))
                prog.emit("vector", lambda v: v.memset(
                    bass.AP(rowbuf, 1025, [[NT16 * 1028, 128], [1028, NT16], [1, 1]]), 0.0),
                    inc=("vS", 1))
            prog.emit("sync", lambda s, t16=t16: s.dma_start(
                out=acc_d[t16 * 128:(t16 + 1) * 128, :], in_=out_acc[:, t16, :]),
                waits=[("vS", x1_done[t16])], inc=("dX", 16))
            dx_accst[t16] = c["dX"]
            par = t16 % 2
            t_trh = [0, 0]
            for half in range(2):
                bank = 4 + half
                waits = [("vS", x1_done[t16])]
                if t16 >= 1:
                    waits.append(("vS", tr_evac[half]))
                for j in range(4):
                    cc = half * 4 + j
                    prog.emit("tensor", lambda t, t16=t16, cc=cc, bank=bank, j=j: t.transpose(
                        PSb(bank, j * 128, (j + 1) * 128),
                        out_acc[:, t16, cc * 128:(cc + 1) * 128], ident_f32[:, :]),
                        waits=waits if j == 0 else [], inc=("tS", 1) if j == 3 else None)
                t_trh[half] = c["tS"]
            for half in range(2):
                bank = 4 + half
                waits = [("tS", t_trh[half])]
                if t16 >= 2:
                    waits.append(("tS", t_lg_all[t16 - 2]))
                prog.emit("vector", lambda v, par=par, half=half, bank=bank: v.tensor_copy(
                    x1st[:, par, half * 512:(half + 1) * 512], PSb(bank)),
                    waits=waits, inc=("vS", 1))
                tr_evac[half] = c["vS"]
            v_stage = c["vS"]
            for cc in range(NC):
                prog.emit("tensor", lambda t, par=par, cc=cc: t.matmul(
                    PSb(6, 0, E), x1st[:, par, cc * 128:(cc + 1) * 128],
                    rwt[:, cc, :], start=(cc == 0), stop=(cc == NC - 1)),
                    waits=[("vS", v_stage), ("dW", dw_rwt)] if cc == 0 else [],
                    inc=("tS", 1) if cc == NC - 1 else None)
            t_lg_all[t16] = c["tS"]
            prog.emit("vector", lambda v, t16=t16: v.tensor_copy(lg_all[:, t16, :], PSb(6, 0, E)),
                      waits=[("tS", t_lg_all[t16])], inc=("vS", 1))
        v_lg = c["vS"]
        dx_acc_init = c["dX"]

        def bc16(th_):
            return bass.AP(th_, 0, [[NT16, 128], [1, NT16], [0, E]])

        prog.emit("vector", lambda v: v.reduce_max(m1A[:, :], lg_all[:, :, :], axis=AX.X),
                  selfw=True, inc=("vS", 1))
        prog.emit("vector", lambda v: v.tensor_tensor(out=gbA[:, :, :], in0=lg_all[:, :, :],
                                                      in1=bc16(m1A), op=OP.is_equal),
                  selfw=True, inc=("vS", 1))
        prog.emit("vector", lambda v: v.reduce_sum(c1A[:, :], gbA[:, :, :], axis=AX.X),
                  selfw=True, inc=("vS", 1))
        prog.emit("vector", lambda v: v.scalar_tensor_tensor(
            out=gbB[:, :, :], in0=gbA[:, :, :], scalar=-1e30, in1=lg_all[:, :, :],
            op0=OP.mult, op1=OP.add), selfw=True, inc=("vS", 1))
        prog.emit("vector", lambda v: v.reduce_max(m2mA[:, :], gbB[:, :, :], axis=AX.X),
                  selfw=True, inc=("vS", 1))
        prog.emit("vector", lambda v: v.tensor_scalar(out=indA[:, :], in0=c1A[:, :],
                                                      scalar1=2.0, scalar2=None, op0=OP.is_ge),
                  selfw=True, inc=("vS", 1))
        prog.emit("vector", lambda v: v.tensor_tensor(out=d12A[:, :], in0=m1A[:, :],
                                                      in1=m2mA[:, :], op=OP.subtract),
                  selfw=True, inc=("vS", 1))
        prog.emit("vector", lambda v: v.tensor_tensor(out=d12A[:, :], in0=indA[:, :],
                                                      in1=d12A[:, :], op=OP.mult),
                  selfw=True, inc=("vS", 1))
        prog.emit("vector", lambda v: v.tensor_tensor(out=m2A[:, :], in0=d12A[:, :],
                                                      in1=m2mA[:, :], op=OP.add),
                  selfw=True, inc=("vS", 1))
        prog.emit("vector", lambda v: v.tensor_tensor(out=gbA[:, :, :], in0=lg_all[:, :, :],
                                                      in1=bc16(m1A), op=OP.subtract),
                  selfw=True, inc=("vS", 1))
        prog.emit("vector", lambda v: v.tensor_tensor(out=e2xA[:, :], in0=m2A[:, :],
                                                      in1=m1A[:, :], op=OP.subtract),
                  selfw=True, inc=("vS", 1))
        v_g = c["vS"]
        prog.emit("scalar", lambda s: s.activation(gbC[:, :, :], gbA[:, :, :], AF.Exp),
                  waits=[("vS", v_g)], selfw=True, inc=("sS", 1))
        prog.emit("scalar", lambda s: s.activation(e2xA[:, :], e2xA[:, :], AF.Exp),
                  inc=("sS", 1))
        s_e2 = c["sS"]
        prog.emit("vector", lambda v: v.tensor_scalar(out=rA[:, :], in0=e2xA[:, :],
                                                      scalar1=1.0, scalar2=None, op0=OP.add),
                  waits=[("sS", s_e2)], inc=("vS", 1))
        prog.emit("vector", lambda v: v.reciprocal(r2A[:, :], rA[:, :]), selfw=True, inc=("vS", 1))
        prog.emit("vector", lambda v: v.tensor_tensor(out=gbB[:, :, :], in0=lg_all[:, :, :],
                                                      in1=bc16(m2A), op=OP.is_ge),
                  selfw=True, inc=("vS", 1))
        prog.emit("vector", lambda v: v.tensor_tensor(out=gbB[:, :, :], in0=gbC[:, :, :],
                                                      in1=gbB[:, :, :], op=OP.mult),
                  selfw=True, inc=("vS", 1))
        prog.emit("vector", lambda v: v.tensor_tensor(out=gates[:, :, :], in0=gbB[:, :, :],
                                                      in1=bc16(r2A), op=OP.mult),
                  selfw=True, inc=("vS", 1))
        v_x1_all = c["vS"]
        v_gates_all = c["vS"]
        if stop_after == "x1":
            return dump_stop(out_acc, NT16 * D, [("vS", v_x1_all)])
        if stop_after == "gates":
            return dump_stop(gates, NT16 * E, [("vS", v_x1_all)])

        def vsw(fn, waits=()):
            prog.emit("vector", fn, waits=waits, selfw=True, inc=("vS", 1))
            return c["vS"]

        prog.emit("gpsimd", lambda g: g.iota(eConst.rearrange("p a b -> p (a b)"),
                                             [[0, NT16], [1, E]], base=0, channel_multiplier=0,
                                             allow_small_or_imprecise_dtypes=True),
                  waits=[("tS", t_pslab_free)], inc=("gS", 1))
        prog.emit("gpsimd", lambda g: g.iota(eCm8.rearrange("p a b -> p (a b)"),
                                             [[0, NT16], [1, E]], base=-8, channel_multiplier=0,
                                             allow_small_or_imprecise_dtypes=True), inc=("gS", 1))
        g_eiota = c["gS"]
        vsw(lambda v: v.tensor_tensor(out=gbA[:, :, :], in0=lg_all[:, :, :],
                                      in1=bc16(m1A), op=OP.is_equal))
        vsw(lambda v: v.tensor_tensor(out=gbB[:, :, :], in0=gbA[:, :, :],
                                      in1=eCm8[:, :, :], op=OP.mult), waits=[("gS", g_eiota)])
        vsw(lambda v: v.tensor_reduce(out=e0f[:, :], in_=gbB[:, :, :], axis=AX.X, op=OP.min))
        vsw(lambda v: v.tensor_scalar(out=e0f[:, :], in0=e0f[:, :], scalar1=8.0,
                                      scalar2=None, op0=OP.add))
        vsw(lambda v: v.tensor_tensor(out=oh0[:, :, :], in0=eConst[:, :, :],
                                      in1=bc16(e0f), op=OP.is_equal))
        vsw(lambda v: v.tensor_tensor(out=gbB[:, :, :], in0=lg_all[:, :, :],
                                      in1=bc16(m2A), op=OP.is_equal))
        vsw(lambda v: v.tensor_scalar(out=gbC[:, :, :], in0=oh0[:, :, :],
                                      scalar1=-1.0, scalar2=1.0, op0=OP.mult, op1=OP.add))
        vsw(lambda v: v.tensor_tensor(out=gbB[:, :, :], in0=gbB[:, :, :],
                                      in1=gbC[:, :, :], op=OP.mult))
        vsw(lambda v: v.tensor_tensor(out=gbC[:, :, :], in0=gbB[:, :, :],
                                      in1=eCm8[:, :, :], op=OP.mult))
        vsw(lambda v: v.tensor_reduce(out=e1f[:, :], in_=gbC[:, :, :], axis=AX.X, op=OP.min))
        vsw(lambda v: v.tensor_scalar(out=e1f[:, :], in0=e1f[:, :], scalar1=8.0,
                                      scalar2=None, op0=OP.add))
        vsw(lambda v: v.tensor_tensor(out=oh1[:, :, :], in0=eConst[:, :, :],
                                      in1=bc16(e1f), op=OP.is_equal))
        vsw(lambda v: v.tensor_tensor(out=g1A[:, :], in0=e2xA[:, :], in1=r2A[:, :], op=OP.mult))
        v_ranks = c["vS"]

        def prefix_and_slots(oh, basev, e_f, g_f, slot_col, gate_waits):
            vsw(lambda v: v.memset(padA[:, :, :], 0.0))
            vsw(lambda v: v.tensor_copy(padA[:, 16:32, :], oh[:, :, :]))
            bufs = [padA, padB]
            for i, sh in enumerate((1, 2, 4, 8)):
                a, b = bufs[i % 2], bufs[(i + 1) % 2]
                vsw(lambda v, b=b: v.memset(b[:, 0:16, :], 0.0))
                vsw(lambda v, a=a, b=b, sh=sh: v.tensor_tensor(
                    out=b[:, 16:32, :], in0=a[:, 16:32, :],
                    in1=a[:, 16 - sh:32 - sh, :], op=OP.add))
            fin = bufs[0]
            vsw(lambda v, fin=fin: v.tensor_tensor(out=exb[:, :, :], in0=fin[:, 16:32, :],
                                                   in1=oh[:, :, :], op=OP.subtract))
            vsw(lambda v, fin=fin: v.tensor_copy(tot_sb[:, :], fin[:, 31, :]))
            v_tot = c["vS"]
            prog.emit("tensor", lambda t: t.matmul(PSb(0, 0, E), triU[:, :], tot_sb[:, :],
                                                   start=True, stop=True),
                      waits=[("vS", v_tot)], inc=("tS", 1))
            prog.emit("tensor", lambda t: t.matmul(PSb(1, 0, E), ones_f32x[:, :], tot_sb[:, :],
                                                   start=True, stop=True), inc=("tS", 1))
            t_pp = c["tS"]
            vsw(lambda v: v.tensor_copy(pp_sb[:, :], PSb(0, 0, E)), waits=[("tS", t_pp)])
            vsw(lambda v: v.tensor_copy(g0b_sb[:, :], PSb(1, 0, E)))
            if basev is not None:
                vsw(lambda v, basev=basev: v.tensor_tensor(out=pp_sb[:, :], in0=pp_sb[:, :],
                                                           in1=basev[:, :], op=OP.add))
            ppb = bass.AP(pp_sb, 0, [[E, 128], [0, NT16], [1, E]])
            vsw(lambda v, ppb=ppb: v.tensor_tensor(out=gbC[:, :, :], in0=exb[:, :, :],
                                                   in1=ppb, op=OP.add))
            vsw(lambda v, oh=oh: v.tensor_tensor(out=gbC[:, :, :], in0=gbC[:, :, :],
                                                 in1=oh[:, :, :], op=OP.mult))
            vsw(lambda v: v.reduce_sum(s0A[:, :], gbC[:, :, :], axis=AX.X))
            vsw(lambda v, e_f=e_f: v.scalar_tensor_tensor(
                out=slotf[:, :], in0=e_f[:, :], scalar=float(CAP), in1=s0A[:, :],
                op0=OP.mult, op1=OP.add))
            vsw(lambda v: v.tensor_scalar(out=lnvA[:, :], in0=s0A[:, :],
                                          scalar1=float(CAP), scalar2=None, op0=OP.is_lt))
            vsw(lambda v: v.tensor_scalar(out=slotf[:, :], in0=slotf[:, :],
                                          scalar1=float(E * CAP), scalar2=None, op0=OP.subtract))
            vsw(lambda v: v.tensor_tensor(out=slotf[:, :], in0=slotf[:, :],
                                          in1=lnvA[:, :], op=OP.mult))
            vsw(lambda v: v.tensor_scalar(out=slotf[:, :], in0=slotf[:, :],
                                          scalar1=float(E * CAP), scalar2=None, op0=OP.add))
            vsw(lambda v, slot_col=slot_col: v.tensor_copy(slotu[:, slot_col, :], slotf[:, :]))
            vsw(lambda v, g_f=g_f: v.tensor_copy(
                bass.AP(rowbuf, 1024, [[NT16 * 1028, 128], [1028, NT16], [1, 1]]),
                g_f[:, :]), waits=gate_waits)
            return c["vS"]

        v_slots0 = prefix_and_slots(oh0, None, e0f, r2A, 0, [])
        vsw(lambda v: v.tensor_copy(base_sb[:, :], g0b_sb[:, :]))

        disp0 = [0] * NT16
        for t16 in range(NT16):
            waits = [("vS", v_slots0), ("sS", s_row[t16]), ("dX", dx_zero_done),
                     ("dS", ds_zero_done)]
            prog.emit("gpsimd", lambda g, t16=t16: g.indirect_dma_start(
                out=xg_d, out_offset=bass.IndirectOffsetOnAxis(
                    ap=slotu[:, 0, t16:t16 + 1], axis=0),
                in_=rowbuf[:, t16, :], in_offset=None),
                waits=waits, inc=("dI", 16))
            disp0[t16] = c["dI"]

        v_slots1 = prefix_and_slots(oh1, base_sb, e1f, g1A, 1,
                                    [("dI", disp0[NT16 - 1])])
        disp1 = [0] * NT16
        for t16 in range(NT16):
            waits = [("vS", v_slots1)]
            prog.emit("gpsimd", lambda g, t16=t16: g.indirect_dma_start(
                out=xg_d, out_offset=bass.IndirectOffsetOnAxis(
                    ap=slotu[:, 1, t16:t16 + 1], axis=0),
                in_=rowbuf[:, t16, :], in_offset=None),
                waits=waits, inc=("dI", 16))
            disp1[t16] = c["dI"]
        di_dispatch = c["dI"]

        s_gelu_last = 0
        di_prev_scatter = di_dispatch
        di_scat_hist = []
        dw_w1 = [0, 0]
        dw_w1_cons = [0, 0]
        t_y_prev = 0
        t_tr_prev = 0
        v_dec_prev = 0
        for e in range(E):
            if e == 0:
                for fh in range(2):
                    prog.emit("scalar", lambda s, fh=fh: s.dma_start(
                        out=w1ring[:, fh, :, :],
                        in_=bass.AP(w1bf_d.tensor, fh * NC * 1024,
                                    [[2 * NC * 1024, 128], [1, NC * 1024]])),
                        waits=[("vS", v_slots1), ("dW", dw_cast_w1[0][fh])], inc=("dS", 16))
                    dw_w1[fh] = c["dS"]

            waits = [("dI", di_dispatch)]
            if e > 0:
                waits += [("tS", t_tr_prev), ("vS", v_dec_prev)]
            prog.emit("sync", lambda s, e=e: s.dma_start(
                out=xg_sb[:, :, :],
                in_=bass.AP(xg_d.tensor, e * CAP * 1028,
                            [[1028, 128], [128 * 1028, CAPT], [1, 1028]])),
                waits=waits, inc=("dX", 16))
            dx_xg = c["dX"]

            waits = ([("vS", v_slots1), ("dW", dw_cast_w2[0])] if e == 0
                     else [("tS", t_y_prev), ("dW", dw_cast_w2[e])])
            prog.emit("sync", lambda s, e=e: s.dma_start(
                out=w2res.rearrange("p a b c -> p (a b c)"),
                in_=w2bf_d[e]),
                waits=waits, inc=("dX", 16))
            dw_w2 = c["dX"]

            par = e % 2
            aAP = bass.AP(xg_sb, 1026, [[CAPT * 1028, 128], [1028, CAPT], [1, 1]])
            bAP = bass.AP(xg_sb, 1027, [[CAPT * 1028, 128], [1028, CAPT], [1, 1]])
            gAP = bass.AP(xg_sb, 1024, [[CAPT * 1028, 128], [1028, CAPT], [1, 1]])
            dwaits = [("dX", dx_xg)]
            if e >= 2:
                dwaits.append(("dI", di_scat_hist[e - 2]))
            v0 = vsw(lambda v, aAP=aAP, par=par: v.tensor_copy(dec_a[:, par, :], aAP),
                     waits=dwaits)
            vsw(lambda v, bAP=bAP, par=par: v.tensor_copy(dec_b[:, par, :], bAP))
            vsw(lambda v, gAP=gAP, par=par: v.tensor_copy(dec_g[:, par, :], gAP))
            vsw(lambda v, par=par: v.scalar_tensor_tensor(
                out=dec_t[:, par, :], in0=dec_a[:, par, :], scalar=64.0, in1=dec_b[:, par, :],
                op0=OP.mult, op1=OP.add))
            vsw(lambda v, par=par: v.tensor_scalar(out=dec_t[:, par, :], in0=dec_t[:, par, :],
                                                   scalar1=1.0, scalar2=None, op0=OP.subtract))
            vsw(lambda v, par=par: v.tensor_scalar(out=s0A[:, 0:CAPT], in0=dec_t[:, par, :],
                                                   scalar1=0.0, scalar2=float(T + 1), op0=OP.is_lt,
                                                   op1=OP.mult))
            vsw(lambda v, par=par: v.tensor_tensor(out=dec_t[:, par, :], in0=dec_t[:, par, :],
                                                   in1=s0A[:, 0:CAPT], op=OP.add))
            v_dec = vsw(lambda v, par=par: v.tensor_copy(dec_u[:, par, :], dec_t[:, par, :]))
            v_dec_prev = v_dec

            trev = {}
            for j in range(CAPT):
                for half in range(2):
                    unit = j * 2 + half
                    bank = unit % 4
                    waits = []
                    if unit == 0:
                        waits.append(("dX", dx_xg))
                        if e > 0:
                            waits.append(("sS", s_gelu_last))
                    if unit >= 4:
                        waits.append(("vS", trev[unit - 4]))
                    for q in range(4):
                        prog.emit("tensor", lambda t, j=j, half=half, bank=bank, q=q: t.matmul(
                            PSb(bank, q * 128, (q + 1) * 128),
                            xg_sb[:, j, half * 512 + q * 128:half * 512 + (q + 1) * 128],
                            ident_bf[:, :], start=True, stop=True),
                            waits=waits if q == 0 else [], inc=("tS", 1) if q == 3 else None)
                    t_trh = c["tS"]
                    prog.emit("vector", lambda v, j=j, half=half, bank=bank: v.tensor_copy(
                        x1g_fm[:, half * 4:(half + 1) * 4, j * 128:(j + 1) * 128], PSb(bank)),
                        waits=[("tS", t_trh)], inc=("vS", 1))
                    trev[unit] = c["vS"]
            v_tr_all = c["vS"]
            t_tr_prev = c["tS"]

            gelu_cnt = {}
            for fh in range(2):
                for fl in range(8):
                    fc = fh * 8 + fl
                    b0 = 2 * (fc % 2)
                    waits0 = [("dS", dw_w1[fh]), ("vS", v_tr_all)]
                    if fc >= 2:
                        waits0.append(("sS", gelu_cnt[fc - 2]))
                    for dc in range(NC):
                        for tt, (lo, hi) in enumerate(((0, 512), (512, CAP))):
                            prog.emit("tensor",
                                      lambda t, fh=fh, fl=fl, dc=dc, lo=lo, hi=hi, b0=b0, tt=tt: t.matmul(
                                          ps[:, (b0 + tt) * 512:(b0 + tt) * 512 + (hi - lo)],
                                          w1ring[:, fh, dc, fl * 128:(fl + 1) * 128],
                                          x1g_fm[:, dc, lo:hi],
                                          start=(dc == 0), stop=(dc == NC - 1)),
                                      waits=waits0 if (dc == 0 and tt == 0) else [],
                                      inc=("tS", 1) if (dc == NC - 1 and tt == 1) else None)
                    t_h = c["tS"]
                    prog.emit("scalar", lambda s, fc=fc, b0=b0: s.activation(
                        h_fm[:, fc, :], ps[:, b0 * 512:b0 * 512 + CAP], AF.Gelu),
                        waits=[("tS", t_h)], inc=("sS", 1))
                    gelu_cnt[fc] = c["sS"]
                dw_w1_cons[fh] = c["tS"]
                if e + 1 < E:
                    prog.emit("scalar", lambda s, e=e, fh=fh: s.dma_start(
                        out=w1ring[:, fh, :, :],
                        in_=bass.AP(w1bf_d.tensor,
                                    (e + 1) * 128 * 2 * NC * 1024 + fh * NC * 1024,
                                    [[2 * NC * 1024, 128], [1, NC * 1024]])),
                        waits=[("tS", dw_w1_cons[fh]), ("dW", dw_cast_w1[e + 1][fh])],
                        inc=("dS", 16))
                    dw_w1[fh] = c["dS"]
            s_gelu_last = c["sS"]

            yev = [0] * CAPT
            for j in range(CAPT):
                b0 = 4 + 2 * (j % 2)
                waits0 = [("dX", dw_w2), ("sS", s_gelu_last)]
                if j >= 2:
                    waits0.append(("sS", yev[j - 2]))
                for f in range(NFC):
                    for dh in range(2):
                        prog.emit("tensor", lambda t, j=j, f=f, dh=dh, b0=b0: t.matmul(
                            ps[:, (b0 + dh) * 512:(b0 + dh + 1) * 512],
                            h_fm[:, f, j * 128:(j + 1) * 128],
                            w2res[:, dh, f, :],
                            start=(f == 0), stop=(f == NFC - 1)),
                            waits=waits0 if (f == 0 and dh == 0) else [],
                            inc=("tS", 1) if (f == NFC - 1 and dh == 1) else None)
                t_y = c["tS"]
                prog.emit("scalar", lambda s, j=j, b0=b0, par=par: s.activation(
                    y_sb[:, j, :], ps[:, b0 * 512:(b0 + 2) * 512], AF.Copy,
                    scale=dec_g[:, par, j:j + 1]),
                    waits=[("tS", t_y), ("dI", di_prev_scatter)], inc=("sS", 1))
                yev[j] = c["sS"]
            t_y_prev = c["tS"]
            for j in range(CAPT):
                waits = [("sS", yev[j]), ("dI", di_prev_scatter), ("dW", dw_ztrash)]
                if e == 0 and j == 0:
                    waits.append(("dX", dx_acc_init))
                prog.emit("gpsimd", lambda g, j=j, par=par: g.indirect_dma_start(
                    out=acc_d, out_offset=bass.IndirectOffsetOnAxis(
                        ap=dec_u[:, par, j:j + 1], axis=0),
                    in_=y_sb[:, j, :], in_offset=None, compute_op=OP.add),
                    waits=waits, inc=("dI", 16))
            di_prev_scatter = c["dI"]
            di_scat_hist.append(di_prev_scatter)
        di_all_scatters = c["dI"]

        eload = [None] * NT16
        for t16 in range(NT16):
            eng, sem = ("sync", "dX") if t16 % 2 == 0 else ("scalar", "dS")
            prog.emit(eng, lambda s, t16=t16: s.dma_start(
                out=out_acc[:, t16, :], in_=acc_d[t16 * 128:(t16 + 1) * 128, :]),
                waits=[("dI", di_all_scatters)], inc=(sem, 16))
            eload[t16] = (sem, c[sem])
        s_estats = [0] * NT16
        for t16 in range(NT16):
            prog.emit("scalar", lambda s, t16=t16: s.activation(
                sq_scr[:, :], out_acc[:, t16, :], AF.Copy, accum_out=lnmsA[:, t16:t16 + 1]),
                waits=[eload[t16]], inc=("sS", 1))
            prog.emit("scalar", lambda s, t16=t16: s.activation(
                sq_scr[:, :], out_acc[:, t16, :], AF.Square, accum_out=lnssA[:, t16:t16 + 1]),
                inc=("sS", 1))
            s_estats[t16] = c["sS"]
        prog.emit("vector", lambda v: v.tensor_scalar(out=muA[:, :], in0=lnmsA[:, :],
                                                      scalar1=1.0 / D, scalar2=None, op0=OP.mult),
                  waits=[("sS", s_estats[NT16 - 1])], inc=("vS", 1))
        v_muE = c["vS"]
        prog.emit("scalar", lambda s: s.activation(t1A[:, :], muA[:, :], AF.Square),
                  waits=[("vS", v_muE)], selfw=True, inc=("sS", 1))
        s_musqE = c["sS"]
        prog.emit("vector", lambda v: v.tensor_scalar(out=t2A[:, :], in0=lnssA[:, :],
                                                      scalar1=1.0 / D, scalar2=EPS_LN,
                                                      op0=OP.mult, op1=OP.add), inc=("vS", 1))
        prog.emit("vector", lambda v: v.tensor_tensor(out=t2A[:, :], in0=t2A[:, :],
                                                      in1=t1A[:, :], op=OP.subtract),
                  waits=[("sS", s_musqE)], selfw=True, inc=("vS", 1))
        v_lnE = c["vS"]
        prog.emit("scalar", lambda s: s.activation(lnvA[:, :], t2A[:, :], AF.Ln),
                  waits=[("vS", v_lnE)], selfw=True, inc=("sS", 1))
        prog.emit("scalar", lambda s: s.activation(invA[:, :], lnvA[:, :], AF.Exp, scale=-0.5),
                  selfw=True, inc=("sS", 1))
        s_invE = c["sS"]
        for t16 in range(NT16):
            prog.emit("vector", lambda v, t16=t16: v.tensor_scalar(
                out=out_acc[:, t16, :], in0=out_acc[:, t16, :], scalar1=muA[:, t16:t16 + 1],
                scalar2=invA[:, t16:t16 + 1], op0=OP.subtract, op1=OP.mult),
                waits=[("sS", s_invE)], inc=("vS", 1))
            v_fin = c["vS"]
            eng, sem = ("gpsimd", "dW") if t16 % 2 == 0 else ("sync", "dX")
            prog.emit(eng, lambda s, t16=t16: s.dma_start(
                out=y_d[t16 * 128:(t16 + 1) * 128, :], in_=out_acc[:, t16, :]),
                waits=[("vS", v_fin)], inc=(sem, 16))

        prog.emit("sync", lambda s: s.nop(),
                  waits=[("dX", c["dX"]), ("dW", c["dW"]), ("dS", c["dS"])])

    with (
        nc.psum_tensor("PS", [128, 4096], F32) as ps,
        nc.semaphore("dW") as s_dW,
        nc.semaphore("dX") as s_dX,
        nc.semaphore("tS") as s_tS,
        nc.semaphore("vS") as s_vS,
        nc.semaphore("sS") as s_sS,
        nc.semaphore("gS") as s_gS,
        nc.semaphore("dI") as s_dI,
        nc.semaphore("dS") as s_dS,
        nc.Block() as block,
    ):
        sems = {"dW": s_dW, "dX": s_dX, "dI": s_dI, "dS": s_dS, "tS": s_tS, "vS": s_vS, "sS": s_sS, "gS": s_gS}
        build_program(ps)
        prog.run(block, sems)
    return nc


_NC_CACHE = None


def kernel(L0_attn_x, L0_attn_Wq, L0_attn_Wk, L0_attn_Wv, L0_attn_Wo, L0_attn_bo,
           L0_attn_q_scale, L0_attn_k_scale, L0_ffn_router_w, L0_ffn_w1, L0_ffn_w2):
    x = np.ascontiguousarray(np.asarray(L0_attn_x, dtype=np.float32))
    Wq = np.asarray(L0_attn_Wq, dtype=np.float32)
    Wk = np.asarray(L0_attn_Wk, dtype=np.float32)
    Wv = np.asarray(L0_attn_Wv, dtype=np.float32)
    Wo = np.asarray(L0_attn_Wo, dtype=np.float32)
    bo = np.ascontiguousarray(np.asarray(L0_attn_bo, dtype=np.float32).reshape(1, D))
    qsc = np.ascontiguousarray(np.asarray(L0_attn_q_scale, dtype=np.float32).reshape(H, DH))
    ksc = np.ascontiguousarray(np.asarray(L0_attn_k_scale, dtype=np.float32).reshape(H, DH))
    rw = np.asarray(L0_ffn_router_w, dtype=np.float32)
    w1 = np.asarray(L0_ffn_w1, dtype=np.float32)
    w2 = np.asarray(L0_ffn_w2, dtype=np.float32)

    wq_t = np.ascontiguousarray(Wq.T.reshape(NC, 128, D))
    wk_t = np.ascontiguousarray(Wk.T.reshape(NC, 128, D))
    wv_t = np.ascontiguousarray(Wv.T.reshape(NC, 128, D))
    wo_t = np.ascontiguousarray(Wo.T.reshape(NC, 128, D))
    rw_t = np.ascontiguousarray(rw.T.reshape(NC, 128, E))
    w1_t = np.ascontiguousarray(
        w1.transpose(0, 2, 1).reshape(E, NC, 128, 2, 1024).transpose(0, 3, 1, 2, 4))
    w2_t = np.ascontiguousarray(
        w2.transpose(0, 2, 1).reshape(E, NFC, 128, 2, 512).transpose(0, 3, 1, 2, 4))

    global _NC_CACHE
    if _NC_CACHE is None:
        _NC_CACHE = build_nc()
    nc = _NC_CACHE

    shared = {"wq": wq_t, "wk": wk_t, "wv": wv_t, "wo": wo_t, "bo": bo,
              "qsc": qsc, "ksc": ksc, "rw": rw_t, "w1": w1_t, "w2": w2_t}
    in_maps = []
    for core in range(8):
        m = dict(shared)
        m["x"] = np.ascontiguousarray(x[core])
        in_maps.append(m)
    res = run_bass_kernel_spmd(nc, in_maps, core_ids=list(range(8)))
    return np.stack([res.results[core]["y"] for core in range(8)], axis=0)
